# version 34
# baseline (speedup 1.0000x reference)
import numpy as np
import ml_dtypes
from contextlib import ExitStack

import concourse.bass as bass
import concourse.mybir as mybir
from concourse.bass_utils import run_bass_kernel_spmd

F32 = mybir.dt.float32
BF16 = mybir.dt.bfloat16
AF = mybir.ActivationFunctionType
ALU = mybir.AluOpType
AX = mybir.AxisListType.X

C = 1024
NCH = 8
W = 256
NSUB = W // 128
NWARM = 2
E60 = float(np.float32(np.exp(np.float64(60.0))))


PARANOID = False
SKIP_OWN = True


class Sched:

    def __init__(self):
        self.items = []
        self.cnt = {}

    AUTO = {"vector": "dve", "scalar": "act", "pool": "pool"}

    def op(self, eng, fn, waits=(), inc=None, k=1, hard=False):
        ws = []
        if PARANOID:
            ws.extend((s, v) for s, v in self.cnt.items() if v > 0)
        if inc is None:
            inc = self.AUTO.get(eng)
        incs = []
        ev = None
        if inc is not None:
            self.cnt[inc] = self.cnt.get(inc, 0) + k
            ev = (inc, self.cnt[inc])
            incs.append((inc, k))
        for w in waits:
            if w is None:
                continue
            if isinstance(w, list):
                ws.extend(w)
            else:
                ws.append(w)
        self.items.append((eng, fn, tuple(ws), incs, k == 16 or hard))
        return ev

    def replay(self, engname, engobj, sems):
        lastwait = {}
        own = self.AUTO.get(engname)
        owncnt = 0
        skip_own = engname == "vector" and SKIP_OWN
        for eng, fn, waits, incs, is_dma in self.items:
            if eng != engname:
                continue
            allw = list(waits)
            if own is not None and owncnt > 0 and not (skip_own and not is_dma):
                allw.append((own, owncnt))
            for (s, v) in allw:
                if s == own and skip_own and not is_dma:
                    continue
                if lastwait.get(s, 0) < v:
                    engobj.wait_ge(sems[s], v)
                    lastwait[s] = v
            inst = fn(engobj)
            for (s, k) in incs:
                inst.then_inc(sems[s], k)
                if s == own:
                    owncnt += k


def flat(*evs):
    r = []
    for e in evs:
        if e is None:
            continue
        if isinstance(e, list):
            r.extend(e)
        else:
            r.append(e)
    return r


def ev_max(*evs):
    bysem = {}
    for e in flat(*evs):
        s, v = e
        bysem[s] = max(bysem.get(s, 0), v)
    return [(s, v) for s, v in bysem.items()] or None


def build_program(nmain, debug=False):
    ntiles = NWARM + nmain
    TOK = W * nmain
    NROW = W * ntiles
    scratch_kind = "ExternalOutput" if debug else "Internal"

    nc = bass.Bass("TRN2", target_bir_lowering=False)

    xs = nc.dram_tensor("xs", [NROW, C], F32, kind="ExternalInput").ap()
    wkT = nc.dram_tensor("wkT", [C, C], F32, kind="ExternalInput").ap()
    wvT = nc.dram_tensor("wvT", [C, C], F32, kind="ExternalInput").ap()
    wrT = nc.dram_tensor("wrT", [C, C], F32, kind="ExternalInput").ap()
    woT = nc.dram_tensor("woT", [C, C], F32, kind="ExternalInput").ap()
    expT = nc.dram_tensor("expT", [8 * C, C], F32, kind="ExternalInput").ap()
    tdp = nc.dram_tensor("tdp", [128, 8], F32, kind="ExternalInput").ap()
    tfp = nc.dram_tensor("tfp", [128, 8], F32, kind="ExternalInput").ap()
    tmixp = nc.dram_tensor("tmixp", [128, 24], F32, kind="ExternalInput").ap()
    wroutT = nc.dram_tensor("wroutT", [C, 8], F32, kind="ExternalInput").ap()
    brout = nc.dram_tensor("brout", [128, 8], F32, kind="ExternalInput").ap()
    bexp = nc.dram_tensor("bexp", [8, C], F32, kind="ExternalInput").ap()
    eye_in = nc.dram_tensor("eye_in", [128, 128], F32, kind="ExternalInput").ap()
    wmask_in = nc.dram_tensor("wmask_in", [128, 1], F32, kind="ExternalInput").ap()
    out = nc.dram_tensor("out", [TOK, C], F32, kind="ExternalOutput").ap()

    scr_s = nc.dram_tensor("scr_s", [C, TOK], BF16, kind=scratch_kind).ap()
    scr_xn = nc.dram_tensor("scr_xn", [nmain * 128, NCH * W], BF16, kind=scratch_kind).ap()
    scr_b = nc.dram_tensor("scr_b", [TOK, C], BF16, kind="Internal").ap()
    if debug:
        scr_g = nc.dram_tensor("scr_g", [TOK, C], F32, kind="ExternalOutput").ap()
        scr_mask = nc.dram_tensor("scr_mask", [128, nmain * NSUB * 8], F32,
                                  kind="ExternalOutput").ap()

    S = Sched()
    ctx = ExitStack()
    with ctx:
        def sbuf(name, shape, dt=F32):
            return ctx.enter_context(nc.sbuf_tensor(name, shape, dt))

        wbig = sbuf("wbig", [128, 32768], BF16)
        xb = sbuf("xb", [128, 2048])
        xnfm2 = [sbuf("xnfm0", [128, NCH * (W + 1)]), sbuf("xnfm1", [128, NCH * (W + 1)])]
        dxn2 = [sbuf("dxn0", [128, 2048]), sbuf("dxn1", [128, 2048])]
        tmrep = sbuf("tmrep", [128, 3 * 2048], BF16)
        mixbf = [sbuf("mixbf0", [128, 2048], BF16), sbuf("mixbf1", [128, 2048], BF16)]
        abuf = sbuf("abuf", [128, 2048], BF16)
        xnbf_s = sbuf("xnbf0", [128, 2048], BF16)
        xnbf2 = [xnbf_s, mixbf[1]]
        kcb = sbuf("kcb", [128, 2048])
        vvb = sbuf("vvb", [128, 2048])
        kvb = sbuf("kvb", [128, 2048])
        den = sbuf("den", [128, 2048])
        t1b = sbuf("t1b", [128, 2048])
        ub = sbuf("ub", [128, 2048])
        sa = sbuf("sa", [128, NCH * (W + 1)])
        sbb = sbuf("sbb", [128, NCH * (W + 1)])
        sbf = sbuf("sbf", [128, 2048], BF16)
        masks = sbuf("masks", [128, nmain * NSUB * 8])
        stat = sbuf("stat", [128, 20])
        dd = sbuf("dd", [128, 8])
        etf = sbuf("etf", [128, 8])
        tmix = sbuf("tmix", [128, 24])
        td_sb = sbuf("td_sb", [128, 8])
        tf_sb = sbuf("tf_sb", [128, 8])
        wrt_sb = sbuf("wrt_sb", [128, 64])
        sbrep = sbuf("sbrep", [128, 8])
        rtev = sbuf("rtev", [8, 256])
        eye = sbuf("eye", [128, 128])
        wmk = sbuf("wmk", [128, 1])
        maskTb = [sbuf("maskT0", [8, 128], BF16), sbuf("maskT1", [8, 128], BF16)]
        sbexp_bf = sbuf("sbexp_bf", [8, C], BF16)
        lgt = sbuf("lgt", [128, 8])
        lm1 = sbuf("lm1", [128, 1])
        lm2 = sbuf("lm2", [128, 1])
        leq = sbuf("leq", [128, 8])
        consts = sbuf("consts", [128, 8])

        QA = ctx.enter_context(nc.psum_tensor("QA", [128, 1024], F32))
        QB = ctx.enter_context(nc.psum_tensor("QB", [128, 1024], F32))
        QT = ctx.enter_context(nc.psum_tensor("QT", [128, 512], F32))
        QR = ctx.enter_context(nc.psum_tensor("QR", [128, 512], F32))
        QS = ctx.enter_context(nc.psum_tensor("QS", [128, 1024], F32))
        QQ = {"QA": QA, "QB": QB, "QT": QT, "QR": QR, "QS": QS}

        CONSTS = [-1.0, -0.5, 1e-5, 1e-8, 1.0, 0.0, 0.5, 1.0 / 1024.0]
        for i, cval in enumerate(CONSTS):
            nc.const_aps.register(F32, cval, consts[:, i:i + 1])

        semnames = ["ld_sm", "ld_x", "ld_xn0", "ld_xn1", "ld_s0", "ld_s1", "ld_xr0",
                    "ld_xr1", "ld_b0", "ld_b1", "sl0", "sl1", "sl2", "sl3", "st_s",
                    "st_xn0", "st_xn1", "st_o0", "st_o1", "st_g", "sb0", "sb1", "sb2",
                    "sb3", "pe", "dve", "act", "pool"]
        sems = {n: ctx.enter_context(nc.semaphore(n)) for n in semnames}

        def v3(buf):
            return buf[:, :].rearrange("p (c w) -> p c w", w=W + 1)

        def p3(buf):
            return buf[:, :].rearrange("p (c w) -> p c w", w=W)

        def bcast8(buf, off, rowlen):
            return bass.AP(buf, off, [[rowlen, 128], [1, 8], [0, W]])

        for i, cval in enumerate(CONSTS):
            S.op("vector", (lambda i=i, cval=cval: (lambda v: v.memset(consts[:, i:i + 1], cval)))())
        ev_bnd0 = None
        for buf in (xnfm2[0], xnfm2[1], sa, sbb):
            ev_bnd0 = S.op("vector", (lambda buf=buf: (lambda v: v.memset(v3(buf)[:, :, 0:1], 0.0)))())
        ev_consts = S.op("vector", lambda v: v.memset(lgt[:, :], 0.0), inc="dve")

        evs_small = []
        evs_small.append(S.op("sync", lambda e: e.dma_start(td_sb[:, :], tdp), inc="ld_sm", k=16))
        evs_small.append(S.op("sync", lambda e: e.dma_start(tf_sb[:, :], tfp), inc="ld_sm", k=16))
        evs_small.append(S.op("sync", lambda e: e.dma_start(tmix[:, :], tmixp), inc="ld_sm", k=16))
        for ci in range(8):
            evs_small.append(S.op("sync", (lambda ci=ci: (lambda e: e.dma_start(
                wrt_sb[:, ci * 8:(ci + 1) * 8], wroutT[ci * 128:(ci + 1) * 128, :])))(),
                inc="ld_sm", k=16))
        sbexp = sbf.bitcast(F32)
        evs_small.append(S.op("sync", lambda e: e.dma_start(sbexp[0:8, 0:C], bexp), inc="ld_sm", k=16))
        evs_small.append(S.op("sync", lambda e: e.dma_start(sbrep[:, :], brout), inc="ld_sm", k=16))
        evs_small.append(S.op("sync", lambda e: e.dma_start(eye[:, :], eye_in), inc="ld_sm", k=16))
        evs_small.append(S.op("sync", lambda e: e.dma_start(wmk[:, :], wmask_in), inc="ld_sm", k=16))
        ev_smalls = evs_small[-1]

        ev_bexpc = S.op("vector", lambda v: v.tensor_copy(sbexp_bf[:, :], sbexp[0:8, 0:C]),
                        waits=[ev_smalls], inc="dve")

        e1 = S.op("scalar", lambda s: s.activation(leq[:, :], td_sb[:, :], AF.Exp),
                  waits=[ev_smalls, ev_consts], inc="act")
        e2 = S.op("scalar", lambda s: s.activation(dd[:, :], leq[:, :], AF.Exp, scale=-1.0),
                  waits=[e1], inc="act")
        ev_etf = S.op("scalar", lambda s: s.activation(etf[:, :], tf_sb[:, :], AF.Exp),
                      waits=[e2], inc="act")
        def emit_reps():
            for m in range(3):
                S.op("vector", (lambda m=m: (lambda v: v.tensor_copy(
                    tmrep[:, m * 2048:(m + 1) * 2048]
                    .rearrange("p (c w) -> p c w", w=W),
                    bcast8(tmix, m * 8, 24))))(),
                    waits=[ev_smalls], inc="dve")

        ev_brep = ev_smalls

        wsrc = [wkT, wvT, wrT, woT]
        slots = [(ub, 0), (ub, 1024), (t1b, 0), (t1b, 1024)]
        ev_cast = {}
        ev_slot_free = [None] * 4
        ev_prologue_cast_end = None

        def emit_wcast(m):
            for ci in range(8):
                si = (m * 8 + ci) % 4
                buf, off = slots[si]
                evl = S.op("sync", (lambda m=m, ci=ci, buf=buf, off=off: (lambda e: e.dma_start(
                    buf[:, off:off + 1024], wsrc[m][ci * 128:(ci + 1) * 128, :])))(),
                    waits=[ev_slot_free[si]], inc=f"sl{si}", k=16)
                evc = S.op("scalar", (lambda m=m, ci=ci, buf=buf, off=off: (lambda s: s.activation(
                    wbig[:, m * 8192 + ci * 1024: m * 8192 + (ci + 1) * 1024],
                    buf[:, off:off + 1024], AF.Copy)))(), waits=[evl], inc="act")
                ev_cast[(m, ci)] = evc
                ev_slot_free[si] = evc

        last_evac = {}
        last_pe_mix = [None, None]
        ev_xb_free = None
        ev_s_stored = None
        ev_xn_stored = None
        ev_xn_stored2 = [None, None]
        ev_masks = {}
        ev_stats_chain = None
        ev_statrd_prev = [None, None]
        ev_xnfm_rd = [None, None]
        ev_dxn_rd = [None, None]
        ev_xnfm_all = [None, None]
        ev_grp_evac2 = [None, None]
        ev_bnd_ev = [None, None]
        ev_dxn_ev = [None, None]
        ev_t1_prev = None
        ev_den1_prev = None
        ev_s12_prev = None
        ev_kv_prev = None
        proj_evs = {}
        proj_mm = {}
        attn_evs = {}
        kv_evs = {}
        scanB_evs = {}
        ev_abuf_rd = None
        ev_attn_last = None
        ev_wo_last = None

        def dbc(dj):
            return bass.AP(dd, dj, [[8, 128], [0, W]])

        rt_pend = {}
        rtr_read = {0: ev_brep, 1: ev_brep}
        rtt_read = {0: ev_brep, 1: ev_brep, 2: ev_brep, 3: ev_brep}
        ev_rtev_rd = None

        def emit_masks(ti):
            nonlocal ev_rtev_rd
            mi = ti - NWARM
            tpar = ti % 2
            qoff = tpar * 256
            evr = rt_pend.pop(ti)
            ev_ev = S.op("scalar", (lambda qoff=qoff: (lambda s: s.activation(
                rtev[0:8, :], QR[0:8, qoff:qoff + 256], AF.Copy)))(),
                waits=flat(evr, ev_rtev_rd), inc="act")
            rtr_read[tpar] = ev_ev
            for sub in range(NSUB):
                s_ = tpar * 2 + sub
                ev_tp = S.op("tensor", (lambda sub=sub, s_=s_: (lambda t: t.transpose(
                    QS[:, 512 + s_ * 8: 512 + s_ * 8 + 8],
                    rtev[0:8, sub * 128:(sub + 1) * 128], eye[0:8, 0:8])))(),
                    waits=flat(ev_ev, rtt_read[s_]), inc="pe")
                ev_rtev_rd = ev_tp
                ev_lg = S.op("vector", (lambda s_=s_: (lambda v: v.tensor_tensor(
                    lgt[:, :], QS[:, 512 + s_ * 8: 512 + s_ * 8 + 8],
                    sbrep[:, 0:8], ALU.add)))(),
                    waits=flat(ev_tp, ev_brep), inc="dve", hard=True)
                rtt_read[s_] = ev_lg
                evm1 = S.op("vector", lambda v: v.reduce_max(lm1[:, :], lgt[:, :], axis=AX),
                            waits=[ev_lg], inc="dve", hard=True)
                evm1b = S.op("vector", lambda v: v.tensor_scalar(
                    leq[:, :], lgt[:, :], lm1[:, 0:1], None, ALU.is_equal),
                    waits=[evm1], inc="dve", hard=True)
                evl2 = S.op("vector", lambda v: v.scalar_tensor_tensor(
                    leq[:, :], leq[:, :], -1e30, lgt[:, :], ALU.mult, ALU.add),
                    waits=[evm1b], inc="dve", hard=True)
                evm2 = S.op("vector", lambda v: v.reduce_max(lm2[:, :], leq[:, :], axis=AX),
                            waits=[evl2], inc="dve", hard=True)
                ev_masks[(mi, sub)] = S.op("vector", (lambda mi=mi, sub=sub:
                    (lambda v: v.tensor_scalar(
                        masks[:, (mi * NSUB + sub) * 8:(mi * NSUB + sub + 1) * 8],
                        lgt[:, :], lm2[:, 0:1], None, ALU.is_ge)))(),
                    waits=[evm2], inc="dve", hard=True)

        A_state = {}

        def emit_A_stats(ti):
            nonlocal ev_stats_chain
            par = ti % 2
            so = 10 * par
            r0 = ti * W
            evx = []
            for sub in range(NSUB):
                evx.append(S.op("sync", (lambda r0=r0, sub=sub: (lambda e: e.dma_start(
                    xb[:, sub * 1024:(sub + 1) * 1024],
                    xs[r0 + sub * 128: r0 + (sub + 1) * 128, :])))(),
                    waits=flat(ev_xb_free), inc="ld_x", k=16))
            ev_x = evx[-1]
            ev_kv_war = kv_evs.get(ti - 3)
            evq = ev_stats_chain
            for sub in range(NSUB):
                evq = S.op("scalar", (lambda sub=sub, so=so: (lambda s: s.activation(
                    vvb[:, 0:1024], xb[:, sub * 1024:(sub + 1) * 1024],
                    AF.Square, accum_out=stat[:, so + sub:so + sub + 1])))(),
                    waits=flat(ev_x, ev_statrd_prev[par], evq, ev_kv_war), inc="act")
            for sub in range(NSUB):
                evq = S.op("scalar", (lambda sub=sub, so=so: (lambda s: s.activation(
                    vvb[:, 0:1024], xb[:, sub * 1024:(sub + 1) * 1024],
                    AF.Copy, accum_out=stat[:, so + 2 + sub:so + sub + 3])))(),
                    waits=[evq], inc="act")
            ev_stats_acc = evq
            ev_stats_chain = evq
            S.op("scalar", (lambda so=so: (lambda s: s.activation(
                stat[:, so + 4:so + 6], stat[:, so + 2:so + 4], AF.Copy,
                scale=1.0 / 1024.0)))(), waits=[ev_stats_acc], inc="act")
            S.op("scalar", (lambda so=so: (lambda s: s.activation(
                stat[:, so + 8:so + 10], stat[:, so + 4:so + 6], AF.Square)))(), inc="act")
            S.op("scalar", (lambda so=so: (lambda s: s.activation(
                stat[:, so + 6:so + 8], stat[:, so + 8:so + 10], AF.Copy,
                scale=-1.0, bias=1e-5)))(), inc="act")
            for sub in range(NSUB):
                S.op("scalar", (lambda so=so, sub=sub: (lambda s: s.activation(
                    stat[:, so + 8 + sub:so + 9 + sub], stat[:, so + sub:so + sub + 1],
                    AF.Ln, scale=1.0 / 1024.0, bias=stat[:, so + 6 + sub:so + 7 + sub])))(),
                    inc="act")
            ev_rstd = S.op("scalar", (lambda so=so: (lambda s: s.activation(
                stat[:, so + 6:so + 8], stat[:, so + 8:so + 10], AF.Exp, scale=-0.5)))(),
                inc="act")
            A_state[ti] = (ev_x, ev_rstd, ev_stats_acc)

        def emit_A_rest(ti):
            nonlocal ev_xb_free, ev_xn_stored
            par = ti % 2
            warm = ti < NWARM
            mi = ti - NWARM
            xf = xnfm2[par]
            dx = dxn2[par]
            so = 10 * par
            ev_x, ev_rstd, ev_stats_acc = A_state.pop(ti)
            ev_xn_sub = []
            for sub in range(NSUB):
                ev_xn_sub.append(S.op("vector", (lambda sub=sub, so=so: (lambda v: v.tensor_scalar(
                    xb[:, sub * 1024:(sub + 1) * 1024], xb[:, sub * 1024:(sub + 1) * 1024],
                    stat[:, so + 4 + sub:so + 5 + sub], stat[:, so + 6 + sub:so + 7 + sub],
                    ALU.subtract, ALU.mult)))(), waits=flat(ev_rstd), inc="dve"))
            ev_xn = ev_xn_sub[-1]
            ev_statrd_prev[par] = ev_xn
            if ti > 0:
                ev_bnd = S.op("vector", (lambda par=par: (lambda v: v.tensor_copy(
                    v3(xnfm2[par])[:, :, 0:1], v3(xnfm2[1 - par])[:, :, W:W + 1])))(),
                    waits=flat(ev_xnfm_all[1 - par], ev_xnfm_rd[par]), inc="dve")
            else:
                ev_bnd = None
            ev_bnd_ev[par] = ev_bnd
            ev_grp_evac = []
            evt_last = None
            for cpair in range(4):
                qname = "QT" if cpair % 2 == 0 else "QS"
                q = QQ[qname]
                evt = None
                for k in range(2):
                    cik = cpair * 2 + k
                    for sub in range(NSUB):
                        evt = S.op("tensor", (lambda cik=cik, sub=sub, q=q, k=k:
                            (lambda t: t.transpose(
                                q[:, k * 256 + sub * 128: k * 256 + (sub + 1) * 128],
                                xb[:, sub * 1024 + cik * 128: sub * 1024 + (cik + 1) * 128],
                                eye[:, :])))(),
                            waits=flat(ev_xn, last_evac.get(qname),
                                       ev_smalls if ti == 0 else None), inc="pe")
                eve = S.op("scalar", (lambda cpair=cpair, q=q, xf=xf: (lambda s: s.activation(
                    v3(xf)[:, cpair * 2:cpair * 2 + 2, 1:W + 1],
                    q[:, 0:512].rearrange("p (c w) -> p c w", w=W), AF.Copy)))(),
                    waits=flat(evt, ev_xnfm_rd[par]), inc="act")
                ev_grp_evac.append(eve)
                last_evac[qname] = eve
                evt_last = evt
            ev_evac_all = ev_grp_evac[-1]
            ev_xnfm_all[par] = ev_evac_all
            ev_grp_evac2[par] = ev_grp_evac
            ev_xb_free = ev_max(evt_last, ev_stats_acc)
            if not warm:
                ev_cast_xn = S.op("scalar", (lambda xf=xf: (lambda s: s.activation(
                    p3(xnbf_s), v3(xf)[:, :, 1:W + 1], AF.Copy)))(),
                    waits=flat(ev_evac_all, ev_xn_stored), inc="act")
                ev_xn_stored = S.op("scalar", (lambda mi=mi: (lambda e: e.dma_start(
                    scr_xn[mi * 128:(mi + 1) * 128, :], xnbf_s[:, :])))(),
                    waits=[ev_cast_xn], inc="st_xn0", k=16)
            ev_dxn = S.op("pool", (lambda xf=xf, dx=dx: (lambda v: v.tensor_tensor(
                p3(dx), v3(xf)[:, :, 1:W + 1], v3(xf)[:, :, 0:W], ALU.subtract)))(),
                waits=flat(ev_evac_all, ev_bnd, ev_dxn_rd[par]), inc="pool")
            ev_dxn_ev[par] = ev_dxn
            ev_rt_last = None
            if not warm:
                tpar = ti % 2
                qoff = tpar * 256
                evr = None
                for ci in range(8):
                    evr = S.op("tensor", (lambda ci=ci, xf=xf, qoff=qoff:
                        (lambda t: t.matmul(
                            QR[0:8, qoff:qoff + 256],
                            wrt_sb[:, ci * 8:(ci + 1) * 8],
                            xf[:, ci * (W + 1) + 1: ci * (W + 1) + 1 + W],
                            start=(ci == 0), stop=(ci == 7))))(),
                        waits=flat(ev_grp_evac[ci // 2], rtr_read[tpar]),
                        inc="pe" if ci == 7 else None)
                rt_pend[ti] = evr
                ev_rt_last = evr
            ev_xnfm_rd[par] = ev_rt_last

        def emit_B1(ti):
            nonlocal ev_kv_prev
            par = ti % 2
            warm = ti < NWARM
            xf = xnfm2[par]
            dx = dxn2[par]
            mixes = [("k", 0, 0), ("v", 1, 1)] + ([] if warm else [("r", 2, 0)])
            proj_ev = {}
            pmm = {}
            ev_mix_last = None
            ev_mix1_last = None
            for (mn, mslot, mpar) in mixes:
                S.op("vector", (lambda mslot=mslot, mpar=mpar, dx=dx:
                    (lambda v: v.tensor_tensor(
                        p3(mixbf[mpar]), p3(dx),
                        tmrep[:, mslot * 2048:(mslot + 1) * 2048]
                        .rearrange("p (c w) -> p c w", w=W),
                        ALU.mult)))(),
                    waits=flat(ev_dxn_ev[par], last_pe_mix[mpar]), inc="dve")
                ev_mix = S.op("vector", (lambda mpar=mpar, xf=xf:
                    (lambda v: v.tensor_tensor(
                        p3(mixbf[mpar]), p3(mixbf[mpar]), v3(xf)[:, :, 0:W],
                        ALU.add)))(),
                    waits=flat(ev_bnd_ev[par], ev_xnfm_all[par]), inc="dve")
                evmix = [ev_mix] * 8
                ev_mix_last = ev_mix
                ev_mix1_last = ev_mix
                dstb = {"k": kcb, "v": vvb, "r": ub}[mn]
                for h2 in range(2):
                    qname = ("QA", "QB")[h2]
                    q = QQ[qname]
                    ev_mm = None
                    for dj in range(h2 * 4, h2 * 4 + 4):
                        off = (dj % 4) * W
                        for ci in range(8):
                            ev_mm = S.op("tensor", (lambda mslot=mslot, ci=ci, dj=dj,
                                                    q=q, mpar=mpar, off=off:
                                (lambda t: t.matmul(
                                    q[:, off:off + W],
                                    wbig[:, mslot * 8192 + ci * 1024 + dj * 128:
                                         mslot * 8192 + ci * 1024 + (dj + 1) * 128],
                                    mixbf[mpar][:, ci * W:(ci + 1) * W],
                                    start=(ci == 0), stop=(ci == 7))))(),
                                waits=flat(evmix[ci], last_evac.get(qname),
                                           ev_cast[(mslot, ci)]
                                           if (ti == 0 or (mslot == 2 and ti == NWARM))
                                           else None),
                                inc="pe" if ci == 7 else None)
                    if mn == "k":
                        war = [ev_den1_prev, ev_s12_prev]
                        fn = (lambda h2=h2, q=q, dstb=dstb: (lambda s: s.activation(
                            dstb[:, h2 * 1024:(h2 + 1) * 1024], q[:, :], AF.Exp)))()
                    elif mn == "v":
                        war = [ev_kv_prev]
                        fn = (lambda h2=h2, q=q, dstb=dstb: (lambda s: s.activation(
                            dstb[:, h2 * 1024:(h2 + 1) * 1024], q[:, :], AF.Copy)))()
                    else:
                        war = [ev_attn_last,
                               ev_prologue_cast_end if ti == NWARM else None]
                        fn = (lambda h2=h2, q=q, dstb=dstb: (lambda s: s.activation(
                            dstb[:, h2 * 1024:(h2 + 1) * 1024], q[:, :], AF.Exp,
                            scale=-1.0)))()
                    eve = S.op("scalar", fn, waits=flat(ev_mm, *war), inc="act")
                    last_evac[qname] = eve
                    proj_ev[(mn, h2)] = eve
                    pmm[mn] = ev_mm
                last_pe_mix[mpar] = ev_mm
            ev_xnfm_rd[par] = ev_max(ev_xnfm_rd[par], ev_mix_last)
            ev_dxn_rd[par] = ev_mix1_last
            proj_evs[ti] = proj_ev
            proj_mm[ti] = pmm
            ev_kcl = S.op("vector", lambda v: v.tensor_scalar(
                kcb[:, :], kcb[:, :], E60, None, ALU.min),
                waits=flat(proj_ev[("k", 0)], proj_ev[("k", 1)]), inc="dve")
            if warm:
                ev_kcl = S.op("vector", lambda v: v.tensor_scalar(
                    kcb[:, :], kcb[:, :], wmk[:, 0:1], None, ALU.mult),
                    waits=flat(ev_smalls, ev_kcl), inc="dve")
            ev_kv = S.op("pool", lambda v: v.tensor_tensor(
                kvb[:, :], kcb[:, :], vvb[:, :], ALU.mult),
                waits=flat(proj_ev[("v", 0)], proj_ev[("v", 1)], ev_kcl,
                           ev_attn_last), inc="pool")
            kv_evs[ti] = ev_kv
            ev_kv_prev = ev_kv
            if ti > 0:
                evb = []
                for buf in (sa, sbb):
                    evb.append(S.op("vector", (lambda buf=buf: (lambda v: v.tensor_copy(
                        v3(buf)[:, :, 0:1], v3(buf)[:, :, W:W + 1])))(), inc="dve"))
                ev_sbnd = evb[-1]
            else:
                ev_sbnd = None
            ev_scanB = {}
            for dj in range(8):
                ev_scanB[dj] = S.op("vector", (lambda dj=dj: (lambda v: v.tensor_tensor_scan(
                    sbb[:, dj * (W + 1) + 1: dj * (W + 1) + 1 + W],
                    dbc(dj), kcb[:, dj * W:(dj + 1) * W],
                    sbb[:, dj * (W + 1): dj * (W + 1) + 1], ALU.mult, ALU.add)))(),
                    waits=flat(ev_kcl, ev_sbnd, e2, ev_bnd0 if ti == 0 else None),
                    inc="dve", hard=(dj == 0))
            for dj in range(8):
                S.op("vector", (lambda dj=dj: (lambda v: v.tensor_tensor_scan(
                    sa[:, dj * (W + 1) + 1: dj * (W + 1) + 1 + W],
                    dbc(dj), kvb[:, dj * W:(dj + 1) * W],
                    sa[:, dj * (W + 1): dj * (W + 1) + 1], ALU.mult, ALU.add)))(),
                    waits=flat(ev_kv), inc="dve")
            scanB_evs[ti] = ev_scanB

        def emit_B2a(ti):
            nonlocal ev_t1_prev, ev_attn_last, ev_den1_prev, ev_s12_prev
            proj_ev = proj_evs[ti]
            ev_scanB = scanB_evs[ti]
            for dj in range(8):
                S.op("vector", (lambda dj=dj: (lambda v: v.tensor_scalar(
                    kvb[:, dj * W:(dj + 1) * W], kvb[:, dj * W:(dj + 1) * W],
                    etf[:, dj:dj + 1], None, ALU.mult)))(),
                    waits=flat(ev_etf), inc="dve")
            ev_n2 = S.op("vector", lambda v: v.tensor_tensor(
                p3(kvb), p3(kvb), v3(sa)[:, :, 0:W], ALU.add), inc="dve")
            for dj in range(8):
                S.op("vector", (lambda dj=dj: (lambda v: v.tensor_scalar(
                    den[:, dj * W:(dj + 1) * W], kcb[:, dj * W:(dj + 1) * W],
                    etf[:, dj:dj + 1], None, ALU.mult)))(),
                    waits=flat(ev_t1_prev if dj == 0 else None), inc="dve")
            ev_d2 = S.op("vector", lambda v: v.tensor_tensor(
                p3(den), p3(den), v3(sbb)[:, :, 0:W], ALU.add), inc="dve")
            ev_den1_prev = ev_d2
            ev_t1 = S.op("scalar", lambda s: s.activation(
                t1b[:, :], den[:, :], AF.Ln, bias=1e-8),
                waits=flat(ev_d2, ev_prologue_cast_end if ti == NWARM else None),
                inc="act")
            ev_t1_prev = ev_t1
            ev_t2 = S.op("scalar", lambda s: s.activation(
                kcb[:, :], ub[:, :], AF.Ln, bias=1.0),
                waits=flat(proj_ev[("r", 0)], proj_ev[("r", 1)], ev_d2,
                           ev_scanB[7], kv_evs[ti]), inc="act")
            ev_s12 = S.op("pool", lambda v: v.tensor_tensor(
                t1b[:, :], t1b[:, :], kcb[:, :], ALU.add),
                waits=flat(ev_t1, ev_t2), inc="pool")
            ev_s12_prev = ev_s12
            ev_rcp = S.op("scalar", lambda s: s.activation(
                ub[:, :], t1b[:, :], AF.Exp, scale=-1.0), waits=[ev_s12], inc="act")
            ev_attn = S.op("pool", lambda v: v.tensor_tensor(
                abuf[:, :], kvb[:, :], ub[:, :], ALU.mult),
                waits=flat(ev_rcp, ev_n2, ev_abuf_rd), inc="pool")
            attn_evs[ti] = ev_attn
            ev_attn_last = ev_attn

        def emit_B2b(ti):
            nonlocal ev_s_stored, ev_abuf_rd, ev_wo_last
            mi = ti - NWARM
            ev_attn = attn_evs[ti]
            ev_mm = None
            evst = None
            for h2 in range(2):
                qname = ("QA", "QB")[h2]
                q = QQ[qname]
                for cj in range(h2 * 4, h2 * 4 + 4):
                    off = (cj % 4) * W
                    for dj in range(8):
                        ev_mm = S.op("tensor", (lambda cj=cj, dj=dj, q=q, off=off:
                            (lambda t: t.matmul(
                                q[:, off:off + W],
                                wbig[:, 3 * 8192 + dj * 1024 + cj * 128:
                                     3 * 8192 + dj * 1024 + (cj + 1) * 128],
                                abuf[:, dj * W:(dj + 1) * W],
                                start=(dj == 0), stop=(dj == 7))))(),
                            waits=flat(ev_attn, last_evac.get(qname),
                                       ev_cast[(3, dj)] if ti == NWARM else None),
                            inc="pe" if dj == 7 else None)
                eve = S.op("scalar", (lambda h2=h2, q=q: (lambda s: s.activation(
                    sbf[:, h2 * 1024:(h2 + 1) * 1024], q[:, :], AF.Copy)))(),
                    waits=flat(ev_mm, ev_s_stored,
                               ev_bexpc if ti == NWARM else None), inc="act")
                last_evac[qname] = eve
                evst = S.op("scalar", (lambda mi=mi, h2=h2: (lambda e: e.dma_start(
                    scr_s[h2 * 512:(h2 + 1) * 512, mi * W:(mi + 1) * W]
                    .rearrange("(c p) w -> p c w", p=128),
                    sbf[:, h2 * 1024:(h2 + 1) * 1024]
                    .rearrange("p (c w) -> p c w", w=W))))(),
                    waits=[eve], inc="st_s", k=16)
            ev_abuf_rd = ev_mm
            ev_wo_last = ev_mm
            ev_s_stored = evst

        emit_A_stats(0)
        emit_A_rest(0)
        emit_wcast(0)
        emit_reps()
        emit_A_stats(1)
        emit_wcast(1)
        emit_A_rest(1)
        prev_b2 = None
        for ti in range(ntiles):
            if ti + 2 < ntiles:
                emit_A_stats(ti + 2)
            emit_B1(ti)
            if ti == 0:
                emit_wcast(2)
            if ti == 1:
                emit_wcast(3)
                ev_prologue_cast_end = max(
                    (ev_cast[(m, ci)] for m in range(4) for ci in range(8)),
                    key=lambda e: e[1])
            if (ti + 1) in rt_pend:
                emit_masks(ti + 1)
            if prev_b2 is not None:
                emit_B2b(prev_b2)
                prev_b2 = None
            if ti + 2 < ntiles:
                emit_A_rest(ti + 2)
            if ti >= NWARM:
                emit_B2a(ti)
                prev_b2 = ti
        if (ntiles - 1) in rt_pend:
            emit_masks(ntiles - 1)
        if prev_b2 is not None:
            emit_B2b(prev_b2)

        barrier = ev_max(ev_s_stored, ev_xn_stored)
        ev_sload_free = {}
        ev_xnld_free = [None, None]
        ev_sg_prev = {}
        ev_out_prev = {}
        ev_slot_init = ev_xb_free
        ev_mm = None
        bias_bank_read = [last_evac.get("QS"),
                          ev_max(last_evac.get("QS"), rtt_read[0], rtt_read[1],
                                 rtt_read[2], rtt_read[3])]
        tp_bank_read = [last_evac.get("QT"), last_evac.get("QT")]
        maskT_war = [None, None]
        bias_sb_free = [None, None, None, None]
        babuf_free = {}
        ev_silu_prev = {}
        EXPQ = [("QA", 0), ("QA", 512), ("QB", 0), ("QB", 512)]
        last_exp_bank_read = {("QA", 0): last_evac.get("QA"),
                              ("QA", 512): last_evac.get("QA"),
                              ("QB", 0): last_evac.get("QB"),
                              ("QB", 512): last_evac.get("QB")}
        ev_exp_free = {}

        def emit_bias_prologue(half):
            nonlocal bias_sb_free
            for j in range(nmain):
                for sub in range(NSUB):
                    msl = (j * NSUB + sub) * 8
                    bi2 = (j * 2 + sub) % 2
                    slot = (j * 2 + sub) % 4
                    ev_mt = S.op("tensor", (lambda msl=msl, bi2=bi2: (lambda t: t.transpose(
                        QT[0:8, bi2 * 128:(bi2 + 1) * 128], masks[:, msl:msl + 8],
                        eye[:, :])))(),
                        waits=flat(tp_bank_read[bi2], ev_masks[(j, sub)]), inc="pe")
                    ev_mte = S.op("scalar", (lambda bi2=bi2: (lambda s: s.activation(
                        maskTb[bi2][:, :], QT[0:8, bi2 * 128:(bi2 + 1) * 128],
                        AF.Copy)))(),
                        waits=flat(ev_mt, maskT_war[bi2]), inc="act")
                    tp_bank_read[bi2] = ev_mte
                    ev_bm = S.op("tensor", (lambda half=half, bi2=bi2: (lambda t: t.matmul(
                        QS[:, bi2 * 512:(bi2 + 1) * 512], maskTb[bi2][0:8, :],
                        sbexp_bf[0:8, half * 512:(half + 1) * 512],
                        start=True, stop=True)))(),
                        waits=flat(ev_mte, bias_bank_read[bi2], ev_bexpc), inc="pe")
                    maskT_war[bi2] = ev_bm
                    ev_be = S.op("scalar", (lambda slot=slot, bi2=bi2: (lambda s: s.activation(
                        sbf[:, slot * 512:(slot + 1) * 512],
                        QS[:, bi2 * 512:(bi2 + 1) * 512], AF.Copy)))(),
                        waits=flat(ev_bm, bias_sb_free[slot],
                                   ev_s_stored if (half == 0 and j == 0) else None),
                        inc="act")
                    bias_bank_read[bi2] = ev_be
                    bias_sb_free[slot] = S.op("scalar", (lambda j=j, sub=sub, half=half,
                                                         slot=slot: (lambda q: q.dma_start(
                        scr_b[j * W + sub * 128: j * W + (sub + 1) * 128,
                              half * 512:(half + 1) * 512],
                        sbf[:, slot * 512:(slot + 1) * 512])))(),
                        waits=[ev_be], inc=f"sb{slot}", k=16)

        estage = [b.bitcast(BF16) for b in
                  (sa, sbb, xnfm2[0], xnfm2[1], dxn2[0], dxn2[1])]
        ev_stage = {}

        for half in range(2):
            ev_ecast = {}
            ev_slotf = [ev_slot_init] * 4
            elist = range(8) if half == 0 else (6, 7)
            for e in elist:
                for ci in range(8):
                    si = (e * 8 + ci) % 4
                    evl = S.op("sync", (lambda e_=e, ci=ci, si=si, half=half: (lambda q: q.dma_start(
                        xb[:, si * 512:(si + 1) * 512],
                        expT[e_ * C + ci * 128: e_ * C + (ci + 1) * 128,
                             half * 512:(half + 1) * 512])))(),
                        waits=flat(ev_slotf[si]), inc=f"sl{si}", k=16)
                    if half == 0:
                        reg = ((e * 8 + ci) * 512) // 8192
                        regev = (proj_mm[ntiles - 1]["k"], proj_mm[ntiles - 1]["v"],
                                 proj_mm[ntiles - 1]["r"], ev_wo_last)[reg]
                    else:
                        regev = ev_exp_free.get(e)
                    evc = S.op("vector", (lambda e_=e, ci=ci, si=si: (lambda v: v.tensor_copy(
                        wbig[:, (e_ * 8 + ci) * 512:(e_ * 8 + ci + 1) * 512],
                        xb[:, si * 512:(si + 1) * 512])))(),
                        waits=flat(evl, regev), inc="dve")
                    ev_ecast[(e, ci)] = evc
                    ev_slotf[si] = evc
            if half == 0:
                ev_ecast_e = [max((ev_ecast[(e, ci)] for ci in range(8)),
                               key=lambda x: x[1]) for e in range(8)]
                emit_bias_prologue(half)
                for e in range(6):
                    for ci in range(8):
                        si = (e * 8 + ci) % 4
                        evl = S.op("sync", (lambda e_=e, ci=ci, si=si: (lambda q: q.dma_start(
                            xb[:, si * 512:(si + 1) * 512],
                            expT[e_ * C + ci * 128: e_ * C + (ci + 1) * 128,
                                 512:1024])))(),
                            waits=flat(ev_slotf[si]), inc=f"sl{si}", k=16)
                        evc = S.op("scalar", (lambda e_=e, ci=ci, si=si: (lambda s: s.activation(
                            estage[e_][:, ci * 512:(ci + 1) * 512],
                            xb[:, si * 512:(si + 1) * 512], AF.Copy)))(),
                            waits=[evl], inc="act")
                        ev_stage[(e, ci)] = evc
                        ev_slotf[si] = evc
                ev_slot_init = ev_slotf[3]
            else:
                ev_ecast_e = [max((ev_stage[(e, ci)] for ci in range(8)),
                               key=lambda x: x[1]) for e in range(6)]
                ev_ecast_e += [max((ev_ecast[(e, ci)] for ci in range(8)),
                                key=lambda x: x[1]) for e in (6, 7)]
                emit_bias_prologue(half)
            ev_bias_all = ev_max(*[bias_sb_free[s] for s in range(4)])
            babuf_free = {0: ev_abuf_rd, 1: ev_abuf_rd} if half == 0 else babuf_free

            for j in range(nmain):
                jp = j % 2
                jo = jp * 1024
                ev_xnld = S.op("sync", (lambda j=j, jp=jp: (lambda q: q.dma_start(
                    xnbf2[jp][:, :], scr_xn[j * 128:(j + 1) * 128, :])))(),
                    waits=flat(barrier if (half == 0 and j <= 1) else None,
                               ev_xnld_free[jp]), inc=f"ld_xn{jp}", k=16)
                ev_sld = []
                for sub in range(NSUB):
                    ev_sld.append(S.op("sync", (lambda j=j, sub=sub, half=half, jo=jo:
                        (lambda q: q.dma_start(
                            mixbf[0][:, jo + sub * 512: jo + (sub + 1) * 512],
                            scr_s[half * 512:(half + 1) * 512,
                                  j * W + sub * 128: j * W + (sub + 1) * 128],
                            transpose=True)))(),
                        waits=flat(barrier if (half == 0 and j == 0) else None,
                                   ev_sload_free.get(jp)),
                        inc=f"ld_s{jp}", k=16))
                ev_xld = []
                for sub in range(NSUB):
                    ev_xld.append(S.op("sync", (lambda j=j, sub=sub, half=half, jo=jo:
                        (lambda q: q.dma_start(
                            kvb[:, jo + sub * 512: jo + (sub + 1) * 512],
                            xs[NWARM * W + j * W + sub * 128:
                               NWARM * W + j * W + (sub + 1) * 128,
                               half * 512:(half + 1) * 512])))(),
                        waits=flat(barrier if (half == 0 and j == 0) else None,
                                   ev_sload_free.get(jp)),
                        inc=f"ld_xr{jp}", k=16))
                ev_bld = []
                for sub in range(NSUB):
                    ev_bld.append(S.op("sync", (lambda j=j, sub=sub, half=half, jo=jo:
                        (lambda q: q.dma_start(
                            abuf[:, jo + sub * 512: jo + (sub + 1) * 512],
                            scr_b[j * W + sub * 128: j * W + (sub + 1) * 128,
                                  half * 512:(half + 1) * 512])))(),
                        waits=flat(ev_bias_all if j <= 1 else None,
                                   babuf_free.get(jp)),
                        inc=f"ld_b{jp}", k=16))

                for sub in range(NSUB):
                    msl = (j * NSUB + sub) * 8
                    ev_acc = None
                    eorder = ([6, 7, 0, 1, 2, 3, 4, 5]
                              if (half == 0 and j == nmain - 1) else list(range(8)))
                    for idx_e, e in enumerate(eorder):
                        qname, qoff = EXPQ[e % 4]
                        q = QQ[qname]
                        if half == 1 and e < 6:
                            wsrc_e = (lambda e_=e: (lambda ci:
                                estage[e_][:, ci * 512:(ci + 1) * 512]))()
                        else:
                            wsrc_e = (lambda e_=e: (lambda ci:
                                wbig[:, (e_ * 8 + ci) * 512:(e_ * 8 + ci + 1) * 512]))()
                        ev_mm = None
                        for ci in range(8):
                            ev_mm = S.op("tensor", (lambda ci=ci, sub=sub, q=q,
                                                    qoff=qoff, jp=jp, wsrc_e=wsrc_e:
                                (lambda t: t.matmul(
                                    q[:, qoff:qoff + 512],
                                    xnbf2[jp][:, ci * W + sub * 128: ci * W + (sub + 1) * 128],
                                    wsrc_e(ci),
                                    start=(ci == 0), stop=(ci == 7))))(),
                                waits=flat(ev_xnld, ev_ecast_e[e] if j == 0 else None,
                                           last_exp_bank_read.get((qname, qoff))),
                                inc="pe" if ci == 7 else None)
                        if half == 0 and j == nmain - 1 and sub == NSUB - 1:
                            ev_exp_free[e] = ev_mm
                        src_in1 = ((lambda jo=jo, sub=sub: abuf[:, jo + sub * 512:
                                                               jo + (sub + 1) * 512])()
                                   if idx_e == 0 else
                                   (lambda jo=jo, sub=sub: kcb[:, jo + sub * 512:
                                                               jo + (sub + 1) * 512])())
                        ev_acc = S.op("vector", (lambda e_=e, sub=sub, q=q, qoff=qoff,
                                                 msl=msl, jo=jo, src_in1=src_in1:
                            (lambda v: v.scalar_tensor_tensor(
                                kcb[:, jo + sub * 512: jo + (sub + 1) * 512],
                                q[:, qoff:qoff + 512],
                                masks[:, msl + e_: msl + e_ + 1], src_in1,
                                ALU.mult, ALU.add)))(),
                            waits=flat(ev_mm, ev_bld[-1] if idx_e == 0 else None,
                                       ev_silu_prev.get((jp, sub)) if idx_e == 0 else None,
                                       ev_acc), inc="dve")
                        last_exp_bank_read[(qname, qoff)] = ev_acc
                        if idx_e == 7:
                            babuf_free[jp] = ev_acc
                    if debug:
                        S.op("scalar", (lambda j=j, sub=sub, half=half, jo=jo:
                            (lambda q: q.dma_start(
                                scr_g[j * W + sub * 128: j * W + (sub + 1) * 128,
                                      half * 512:(half + 1) * 512],
                                kcb[:, jo + sub * 512: jo + (sub + 1) * 512])))(),
                            waits=[ev_acc], inc="st_g", k=16)
                    ev_silu = S.op("scalar", (lambda sub=sub, jo=jo: (lambda s: s.activation(
                        t1b[:, jo + sub * 512: jo + (sub + 1) * 512],
                        kcb[:, jo + sub * 512: jo + (sub + 1) * 512],
                        AF.Silu)))(), waits=flat(ev_acc, ev_sg_prev.get((jp, sub))), inc="act")
                    ev_silu_prev[(jp, sub)] = ev_silu
                    ev_sg = S.op("pool", (lambda sub=sub, jo=jo: (lambda g: g.tensor_tensor(
                        ub[:, jo + sub * 512: jo + (sub + 1) * 512],
                        t1b[:, jo + sub * 512: jo + (sub + 1) * 512],
                        mixbf[0][:, jo + sub * 512: jo + (sub + 1) * 512], ALU.mult)))(),
                        waits=flat(ev_silu, ev_sld[-1]), inc="pool")
                    ev_sg_prev[(jp, sub)] = ev_sg
                    ev_o = S.op("pool", (lambda sub=sub, jo=jo: (lambda g: g.tensor_tensor(
                        den[:, jo + sub * 512: jo + (sub + 1) * 512],
                        ub[:, jo + sub * 512: jo + (sub + 1) * 512],
                        kvb[:, jo + sub * 512: jo + (sub + 1) * 512], ALU.add)))(),
                        waits=flat(ev_xld[-1], ev_out_prev.get(jp)), inc="pool")
                    ev_st = S.op("scalar", (lambda j=j, sub=sub, half=half, jo=jo:
                        (lambda q: q.dma_start(
                            out[j * W + sub * 128: j * W + (sub + 1) * 128,
                                half * 512:(half + 1) * 512],
                            den[:, jo + sub * 512: jo + (sub + 1) * 512])))(),
                        waits=[ev_o], inc=f"st_o{jp}", k=16)
                    ev_out_prev[jp] = ev_st
                ev_sload_free[jp] = ev_o
                ev_xnld_free[jp] = ev_mm

        if debug:
            S.op("sync", lambda q: q.dma_start(scr_mask, masks[:, :]),
                 waits=flat(*[ev_masks[k2] for k2 in ev_masks]), inc="st_g", k=16)
        S.op("sync", lambda q: q.nop(),
             waits=[(n, S.cnt[n]) for n in
                    ("st_s", "st_xn0", "st_xn1", "st_o0", "st_o1", "st_g")
                    if S.cnt.get(n)])

        with nc.Block() as block:
            @block.sync
            def _(e):
                S.replay("sync", e, sems)

            @block.vector
            def _(e):
                S.replay("vector", e, sems)

            @block.scalar
            def _(e):
                S.replay("scalar", e, sems)

            @block.tensor
            def _(e):
                S.replay("tensor", e, sems)

            @block.gpsimd
            def _(e):
                S.replay("pool", e, sems)

    return nc


_CACHE = {}


def _get_program(nmain):
    if nmain not in _CACHE:
        _CACHE[nmain] = build_program(nmain)
    return _CACHE[nmain]


def kernel(L0_attn_xq, L0_attn_xk, L0_attn_xv, L0_attn_time_decay, L0_attn_time_first,
           L0_attn_time_mix_k, L0_attn_time_mix_v, L0_attn_time_mix_r,
           L0_attn_Wk, L0_attn_Wv, L0_attn_Wr, L0_attn_Wo,
           L0_ffn_w_router, L0_ffn_b_router, L0_ffn_w_experts, L0_ffn_b_experts):
    x = np.ascontiguousarray(np.asarray(L0_attn_xq, dtype=np.float32))
    B, T, Cc = x.shape
    assert (B, T, Cc) == (4, 8192, 1024)
    nmain = 16
    TOK = W * nmain
    NROW = W * (NWARM + nmain)

    def chunked(a):
        a = np.asarray(a, np.float32).reshape(-1)
        return np.ascontiguousarray(a.reshape(8, 128).T)

    common = {
        "wkT": np.ascontiguousarray(np.asarray(L0_attn_Wk, np.float32).T),
        "wvT": np.ascontiguousarray(np.asarray(L0_attn_Wv, np.float32).T),
        "wrT": np.ascontiguousarray(np.asarray(L0_attn_Wr, np.float32).T),
        "woT": np.ascontiguousarray(np.asarray(L0_attn_Wo, np.float32).T),
        "expT": np.ascontiguousarray(
            np.asarray(L0_ffn_w_experts, np.float32).transpose(0, 2, 1).reshape(8 * C, C)),
        "tdp": chunked(L0_attn_time_decay),
        "tfp": chunked(L0_attn_time_first),
        "tmixp": np.ascontiguousarray(np.concatenate(
            [chunked(L0_attn_time_mix_k), chunked(L0_attn_time_mix_v),
             chunked(L0_attn_time_mix_r)], axis=1)),
        "wroutT": np.ascontiguousarray(np.asarray(L0_ffn_w_router, np.float32).T),
        "brout": np.ascontiguousarray(np.broadcast_to(
            np.asarray(L0_ffn_b_router, np.float32).reshape(1, 8), (128, 8))),
        "bexp": np.ascontiguousarray(np.asarray(L0_ffn_b_experts, np.float32)),
        "eye_in": np.eye(128, dtype=np.float32),
    }

    in_maps = []
    for core in range(8):
        b, h = core // 2, core % 2
        t0 = h * TOK
        xsv = np.zeros((NROW, C), np.float32)
        lo = max(0, t0 - NWARM * W)
        xsv[NWARM * W - (t0 - lo):] = x[b, lo:t0 + TOK]
        m = dict(common)
        m["xs"] = xsv
        m["wmask_in"] = np.full((128, 1), 1.0 if h == 1 else 0.0, np.float32)
        in_maps.append(m)

    nc = _get_program(nmain)
    res = run_bass_kernel_spmd(nc, in_maps, core_ids=list(range(8)))
    outf = np.empty((B, T, C), np.float32)
    for core in range(8):
        b, h = core // 2, core % 2
        outf[b, h * TOK:(h + 1) * TOK] = np.asarray(res.results[core]["out"])
    return outf


# version 35
# speedup vs baseline: 1.0296x; 1.0296x over previous
import numpy as np
import ml_dtypes
from contextlib import ExitStack

import concourse.bass as bass
import concourse.mybir as mybir
from concourse.bass_utils import run_bass_kernel_spmd

F32 = mybir.dt.float32
BF16 = mybir.dt.bfloat16
AF = mybir.ActivationFunctionType
ALU = mybir.AluOpType
AX = mybir.AxisListType.X

C = 1024
NCH = 8
W = 256
NSUB = W // 128
NWARM = 2
E60 = float(np.float32(np.exp(np.float64(60.0))))


PARANOID = False
SKIP_OWN = True


class Sched:

    def __init__(self):
        self.items = []
        self.cnt = {}

    AUTO = {"vector": "dve", "scalar": "act", "pool": "pool"}

    def op(self, eng, fn, waits=(), inc=None, k=1, hard=False):
        ws = []
        if PARANOID:
            ws.extend((s, v) for s, v in self.cnt.items() if v > 0)
        if inc is None:
            inc = self.AUTO.get(eng)
        incs = []
        ev = None
        if inc is not None:
            self.cnt[inc] = self.cnt.get(inc, 0) + k
            ev = (inc, self.cnt[inc])
            incs.append((inc, k))
        for w in waits:
            if w is None:
                continue
            if isinstance(w, list):
                ws.extend(w)
            else:
                ws.append(w)
        self.items.append((eng, fn, tuple(ws), incs, k == 16 or hard))
        return ev

    def replay(self, engname, engobj, sems):
        lastwait = {}
        own = self.AUTO.get(engname)
        owncnt = 0
        skip_own = engname == "vector" and SKIP_OWN
        for eng, fn, waits, incs, is_dma in self.items:
            if eng != engname:
                continue
            allw = list(waits)
            if own is not None and owncnt > 0 and not (skip_own and not is_dma):
                allw.append((own, owncnt))
            for (s, v) in allw:
                if s == own and skip_own and not is_dma:
                    continue
                if lastwait.get(s, 0) < v:
                    engobj.wait_ge(sems[s], v)
                    lastwait[s] = v
            inst = fn(engobj)
            for (s, k) in incs:
                inst.then_inc(sems[s], k)
                if s == own:
                    owncnt += k


def flat(*evs):
    r = []
    for e in evs:
        if e is None:
            continue
        if isinstance(e, list):
            r.extend(e)
        else:
            r.append(e)
    return r


def ev_max(*evs):
    bysem = {}
    for e in flat(*evs):
        s, v = e
        bysem[s] = max(bysem.get(s, 0), v)
    return [(s, v) for s, v in bysem.items()] or None


def build_program(nmain, debug=False):
    ntiles = NWARM + nmain
    TOK = W * nmain
    NROW = W * ntiles
    scratch_kind = "ExternalOutput" if debug else "Internal"

    nc = bass.Bass("TRN2", target_bir_lowering=False)

    xs = nc.dram_tensor("xs", [NROW, C], F32, kind="ExternalInput").ap()
    wkT = nc.dram_tensor("wkT", [C, C], F32, kind="ExternalInput").ap()
    wvT = nc.dram_tensor("wvT", [C, C], F32, kind="ExternalInput").ap()
    wrT = nc.dram_tensor("wrT", [C, C], F32, kind="ExternalInput").ap()
    woT = nc.dram_tensor("woT", [C, C], F32, kind="ExternalInput").ap()
    expT = nc.dram_tensor("expT", [8 * C, C], F32, kind="ExternalInput").ap()
    tdp = nc.dram_tensor("tdp", [128, 8], F32, kind="ExternalInput").ap()
    tfp = nc.dram_tensor("tfp", [128, 8], F32, kind="ExternalInput").ap()
    tmixp = nc.dram_tensor("tmixp", [128, 24], F32, kind="ExternalInput").ap()
    wroutT = nc.dram_tensor("wroutT", [C, 8], F32, kind="ExternalInput").ap()
    brout = nc.dram_tensor("brout", [128, 8], F32, kind="ExternalInput").ap()
    bexp = nc.dram_tensor("bexp", [8, C], F32, kind="ExternalInput").ap()
    eye_in = nc.dram_tensor("eye_in", [128, 128], F32, kind="ExternalInput").ap()
    wmask_in = nc.dram_tensor("wmask_in", [128, 1], F32, kind="ExternalInput").ap()
    out = nc.dram_tensor("out", [TOK, C], F32, kind="ExternalOutput").ap()

    scr_s = nc.dram_tensor("scr_s", [C, TOK], BF16, kind=scratch_kind).ap()
    scr_xn = nc.dram_tensor("scr_xn", [nmain * 128, NCH * W], BF16, kind=scratch_kind).ap()
    scr_b = nc.dram_tensor("scr_b", [TOK, C], BF16, kind="Internal").ap()
    if debug:
        scr_g = nc.dram_tensor("scr_g", [TOK, C], F32, kind="ExternalOutput").ap()
        scr_mask = nc.dram_tensor("scr_mask", [128, nmain * NSUB * 8], F32,
                                  kind="ExternalOutput").ap()

    S = Sched()
    ctx = ExitStack()
    with ctx:
        def sbuf(name, shape, dt=F32):
            return ctx.enter_context(nc.sbuf_tensor(name, shape, dt))

        wbig = sbuf("wbig", [128, 32768], BF16)
        xb = sbuf("xb", [128, 2048])
        xnfm2 = [sbuf("xnfm0", [128, NCH * (W + 1)]), sbuf("xnfm1", [128, NCH * (W + 1)])]
        dxn2 = [sbuf("dxn0", [128, 2048], BF16), sbuf("dxn1", [128, 2048], BF16)]
        tmrep = sbuf("tmrep", [128, 3 * 2048], BF16)
        mixbf = [sbuf("mixbf0", [128, 2048], BF16), sbuf("mixbf1", [128, 2048], BF16)]
        abuf = sbuf("abuf", [128, 2048], BF16)
        xnbf_s = sbuf("xnbf0", [128, 2048], BF16)
        xnbf2 = [xnbf_s, mixbf[1]]
        kcb = sbuf("kcb", [128, 2048])
        vvb = sbuf("vvb", [128, 2048])
        kvb = sbuf("kvb", [128, 2048])
        den = sbuf("den", [128, 2048])
        t1b = sbuf("t1b", [128, 2048])
        ub = sbuf("ub", [128, 2048])
        sa = sbuf("sa", [128, NCH * (W + 1)])
        sbb = sbuf("sbb", [128, NCH * (W + 1)])
        sbf = sbuf("sbf", [128, 2048], BF16)
        masks = sbuf("masks", [128, nmain * NSUB * 8])
        stat = sbuf("stat", [128, 20])
        dd = sbuf("dd", [128, 8])
        etf = sbuf("etf", [128, 8])
        tmix = sbuf("tmix", [128, 24])
        td_sb = sbuf("td_sb", [128, 8])
        tf_sb = sbuf("tf_sb", [128, 8])
        wrt_sb = sbuf("wrt_sb", [128, 64])
        sbrep = sbuf("sbrep", [128, 8])
        rtev = sbuf("rtev", [8, 256])
        eye = sbuf("eye", [128, 128])
        wmk = sbuf("wmk", [128, 1])
        maskTb = [sbuf("maskT0", [8, 128], BF16), sbuf("maskT1", [8, 128], BF16)]
        sbexp_bf = sbuf("sbexp_bf", [8, C], BF16)
        lgt = sbuf("lgt", [128, 8])
        lm1 = sbuf("lm1", [128, 1])
        lm2 = sbuf("lm2", [128, 1])
        leq = sbuf("leq", [128, 8])
        consts = sbuf("consts", [128, 8])

        QA = ctx.enter_context(nc.psum_tensor("QA", [128, 1024], F32))
        QB = ctx.enter_context(nc.psum_tensor("QB", [128, 1024], F32))
        QT = ctx.enter_context(nc.psum_tensor("QT", [128, 512], F32))
        QR = ctx.enter_context(nc.psum_tensor("QR", [128, 512], F32))
        QS = ctx.enter_context(nc.psum_tensor("QS", [128, 1024], F32))
        QQ = {"QA": QA, "QB": QB, "QT": QT, "QR": QR, "QS": QS}

        CONSTS = [-1.0, -0.5, 1e-5, 1e-8, 1.0, 0.0, 0.5, 1.0 / 1024.0]
        for i, cval in enumerate(CONSTS):
            nc.const_aps.register(F32, cval, consts[:, i:i + 1])

        semnames = ["ld_sm", "ld_x", "ld_xn0", "ld_xn1", "ld_s0", "ld_s1", "ld_xr0",
                    "ld_xr1", "ld_b0", "ld_b1", "sl0", "sl1", "sl2", "sl3", "st_s",
                    "st_xn0", "st_xn1", "st_o0", "st_o1", "st_g", "sb0", "sb1", "sb2",
                    "sb3", "pe", "dve", "act", "pool"]
        sems = {n: ctx.enter_context(nc.semaphore(n)) for n in semnames}

        def v3(buf):
            return buf[:, :].rearrange("p (c w) -> p c w", w=W + 1)

        def p3(buf):
            return buf[:, :].rearrange("p (c w) -> p c w", w=W)

        def bcast8(buf, off, rowlen):
            return bass.AP(buf, off, [[rowlen, 128], [1, 8], [0, W]])

        for i, cval in enumerate(CONSTS):
            S.op("vector", (lambda i=i, cval=cval: (lambda v: v.memset(consts[:, i:i + 1], cval)))())
        ev_bnd0 = None
        for buf in (xnfm2[0], xnfm2[1], sa, sbb):
            ev_bnd0 = S.op("vector", (lambda buf=buf: (lambda v: v.memset(v3(buf)[:, :, 0:1], 0.0)))())
        ev_consts = S.op("vector", lambda v: v.memset(lgt[:, :], 0.0), inc="dve")

        evs_small = []
        evs_small.append(S.op("sync", lambda e: e.dma_start(td_sb[:, :], tdp), inc="ld_sm", k=16))
        evs_small.append(S.op("sync", lambda e: e.dma_start(tf_sb[:, :], tfp), inc="ld_sm", k=16))
        evs_small.append(S.op("sync", lambda e: e.dma_start(tmix[:, :], tmixp), inc="ld_sm", k=16))
        for ci in range(8):
            evs_small.append(S.op("sync", (lambda ci=ci: (lambda e: e.dma_start(
                wrt_sb[:, ci * 8:(ci + 1) * 8], wroutT[ci * 128:(ci + 1) * 128, :])))(),
                inc="ld_sm", k=16))
        sbexp = sbf.bitcast(F32)
        evs_small.append(S.op("sync", lambda e: e.dma_start(sbexp[0:8, 0:C], bexp), inc="ld_sm", k=16))
        evs_small.append(S.op("sync", lambda e: e.dma_start(sbrep[:, :], brout), inc="ld_sm", k=16))
        evs_small.append(S.op("sync", lambda e: e.dma_start(eye[:, :], eye_in), inc="ld_sm", k=16))
        evs_small.append(S.op("sync", lambda e: e.dma_start(wmk[:, :], wmask_in), inc="ld_sm", k=16))
        ev_smalls = evs_small[-1]

        ev_bexpc = S.op("vector", lambda v: v.tensor_copy(sbexp_bf[:, :], sbexp[0:8, 0:C]),
                        waits=[ev_smalls], inc="dve")

        e1 = S.op("scalar", lambda s: s.activation(leq[:, :], td_sb[:, :], AF.Exp),
                  waits=[ev_smalls, ev_consts], inc="act")
        e2 = S.op("scalar", lambda s: s.activation(dd[:, :], leq[:, :], AF.Exp, scale=-1.0),
                  waits=[e1], inc="act")
        ev_etf = S.op("scalar", lambda s: s.activation(etf[:, :], tf_sb[:, :], AF.Exp),
                      waits=[e2], inc="act")
        def emit_reps():
            for m in range(3):
                S.op("vector", (lambda m=m: (lambda v: v.tensor_copy(
                    tmrep[:, m * 2048:(m + 1) * 2048]
                    .rearrange("p (c w) -> p c w", w=W),
                    bcast8(tmix, m * 8, 24))))(),
                    waits=[ev_smalls], inc="dve")

        ev_brep = ev_smalls

        wsrc = [wkT, wvT, wrT, woT]
        slots = [(ub, 0), (ub, 1024), (t1b, 0), (t1b, 1024)]
        ev_cast = {}
        ev_slot_free = [None] * 4
        ev_prologue_cast_end = None

        def emit_wcast(m):
            for ci in range(8):
                si = (m * 8 + ci) % 4
                buf, off = slots[si]
                evl = S.op("sync", (lambda m=m, ci=ci, buf=buf, off=off: (lambda e: e.dma_start(
                    buf[:, off:off + 1024], wsrc[m][ci * 128:(ci + 1) * 128, :])))(),
                    waits=[ev_slot_free[si]], inc=f"sl{si}", k=16)
                evc = S.op("scalar", (lambda m=m, ci=ci, buf=buf, off=off: (lambda s: s.activation(
                    wbig[:, m * 8192 + ci * 1024: m * 8192 + (ci + 1) * 1024],
                    buf[:, off:off + 1024], AF.Copy)))(), waits=[evl], inc="act")
                ev_cast[(m, ci)] = evc
                ev_slot_free[si] = evc

        last_evac = {}
        last_pe_mix = [None, None]
        ev_xb_free = None
        ev_s_stored = None
        ev_xn_stored = None
        ev_xn_stored2 = [None, None]
        ev_masks = {}
        ev_stats_chain = None
        ev_statrd_prev = [None, None]
        ev_xnfm_rd = [None, None]
        ev_dxn_rd = [None, None]
        ev_xnfm_all = [None, None]
        ev_grp_evac2 = [None, None]
        ev_bnd_ev = [None, None]
        ev_dxn_ev = [None, None]
        ev_t1_prev = None
        ev_den1_prev = None
        ev_s12_prev = None
        ev_kv_prev = None
        proj_evs = {}
        proj_mm = {}
        attn_evs = {}
        kv_evs = {}
        scanB_evs = {}
        ev_abuf_rd = None
        ev_attn_last = None
        ev_wo_last = None

        def dbc(dj):
            return bass.AP(dd, dj, [[8, 128], [0, W]])

        rt_pend = {}
        rtr_read = {0: ev_brep, 1: ev_brep}
        rtt_read = {0: ev_brep, 1: ev_brep, 2: ev_brep, 3: ev_brep}
        ev_rtev_rd = None

        def emit_masks(ti):
            nonlocal ev_rtev_rd
            mi = ti - NWARM
            tpar = ti % 2
            qoff = tpar * 256
            evr = rt_pend.pop(ti)
            ev_ev = S.op("scalar", (lambda qoff=qoff: (lambda s: s.activation(
                rtev[0:8, :], QR[0:8, qoff:qoff + 256], AF.Copy)))(),
                waits=flat(evr, ev_rtev_rd), inc="act")
            rtr_read[tpar] = ev_ev
            for sub in range(NSUB):
                s_ = tpar * 2 + sub
                ev_tp = S.op("tensor", (lambda sub=sub, s_=s_: (lambda t: t.transpose(
                    QS[:, 512 + s_ * 8: 512 + s_ * 8 + 8],
                    rtev[0:8, sub * 128:(sub + 1) * 128], eye[0:8, 0:8])))(),
                    waits=flat(ev_ev, rtt_read[s_]), inc="pe")
                ev_rtev_rd = ev_tp
                ev_lg = S.op("vector", (lambda s_=s_: (lambda v: v.tensor_tensor(
                    lgt[:, :], QS[:, 512 + s_ * 8: 512 + s_ * 8 + 8],
                    sbrep[:, 0:8], ALU.add)))(),
                    waits=flat(ev_tp, ev_brep), inc="dve", hard=True)
                rtt_read[s_] = ev_lg
                evm1 = S.op("vector", lambda v: v.reduce_max(lm1[:, :], lgt[:, :], axis=AX),
                            waits=[ev_lg], inc="dve", hard=True)
                evm1b = S.op("vector", lambda v: v.tensor_scalar(
                    leq[:, :], lgt[:, :], lm1[:, 0:1], None, ALU.is_equal),
                    waits=[evm1], inc="dve", hard=True)
                evl2 = S.op("vector", lambda v: v.scalar_tensor_tensor(
                    leq[:, :], leq[:, :], -1e30, lgt[:, :], ALU.mult, ALU.add),
                    waits=[evm1b], inc="dve", hard=True)
                evm2 = S.op("vector", lambda v: v.reduce_max(lm2[:, :], leq[:, :], axis=AX),
                            waits=[evl2], inc="dve", hard=True)
                ev_masks[(mi, sub)] = S.op("vector", (lambda mi=mi, sub=sub:
                    (lambda v: v.tensor_scalar(
                        masks[:, (mi * NSUB + sub) * 8:(mi * NSUB + sub + 1) * 8],
                        lgt[:, :], lm2[:, 0:1], None, ALU.is_ge)))(),
                    waits=[evm2], inc="dve", hard=True)

        A_state = {}

        def emit_A_stats(ti):
            nonlocal ev_stats_chain
            par = ti % 2
            so = 10 * par
            r0 = ti * W
            evx = []
            for sub in range(NSUB):
                evx.append(S.op("sync", (lambda r0=r0, sub=sub: (lambda e: e.dma_start(
                    xb[:, sub * 1024:(sub + 1) * 1024],
                    xs[r0 + sub * 128: r0 + (sub + 1) * 128, :])))(),
                    waits=flat(ev_xb_free), inc="ld_x", k=16))
            ev_x = evx[-1]
            ev_kv_war = kv_evs.get(ti - 3)
            evq = ev_stats_chain
            for sub in range(NSUB):
                evq = S.op("scalar", (lambda sub=sub, so=so: (lambda s: s.activation(
                    vvb[:, 0:1024], xb[:, sub * 1024:(sub + 1) * 1024],
                    AF.Square, accum_out=stat[:, so + sub:so + sub + 1])))(),
                    waits=flat(ev_x, ev_statrd_prev[par], evq, ev_kv_war), inc="act")
            for sub in range(NSUB):
                evq = S.op("scalar", (lambda sub=sub, so=so: (lambda s: s.activation(
                    vvb[:, 0:1024], xb[:, sub * 1024:(sub + 1) * 1024],
                    AF.Copy, accum_out=stat[:, so + 2 + sub:so + sub + 3])))(),
                    waits=[evq], inc="act")
            ev_stats_acc = evq
            ev_stats_chain = evq
            S.op("scalar", (lambda so=so: (lambda s: s.activation(
                stat[:, so + 4:so + 6], stat[:, so + 2:so + 4], AF.Copy,
                scale=1.0 / 1024.0)))(), waits=[ev_stats_acc], inc="act")
            S.op("scalar", (lambda so=so: (lambda s: s.activation(
                stat[:, so + 8:so + 10], stat[:, so + 4:so + 6], AF.Square)))(), inc="act")
            S.op("scalar", (lambda so=so: (lambda s: s.activation(
                stat[:, so + 6:so + 8], stat[:, so + 8:so + 10], AF.Copy,
                scale=-1.0, bias=1e-5)))(), inc="act")
            for sub in range(NSUB):
                S.op("scalar", (lambda so=so, sub=sub: (lambda s: s.activation(
                    stat[:, so + 8 + sub:so + 9 + sub], stat[:, so + sub:so + sub + 1],
                    AF.Ln, scale=1.0 / 1024.0, bias=stat[:, so + 6 + sub:so + 7 + sub])))(),
                    inc="act")
            ev_rstd = S.op("scalar", (lambda so=so: (lambda s: s.activation(
                stat[:, so + 6:so + 8], stat[:, so + 8:so + 10], AF.Exp, scale=-0.5)))(),
                inc="act")
            A_state[ti] = (ev_x, ev_rstd, ev_stats_acc)

        def emit_A_rest(ti):
            nonlocal ev_xb_free, ev_xn_stored
            par = ti % 2
            warm = ti < NWARM
            mi = ti - NWARM
            xf = xnfm2[par]
            dx = dxn2[par]
            so = 10 * par
            ev_x, ev_rstd, ev_stats_acc = A_state.pop(ti)
            ev_xn_sub = []
            for sub in range(NSUB):
                ev_xn_sub.append(S.op("vector", (lambda sub=sub, so=so: (lambda v: v.tensor_scalar(
                    xb[:, sub * 1024:(sub + 1) * 1024], xb[:, sub * 1024:(sub + 1) * 1024],
                    stat[:, so + 4 + sub:so + 5 + sub], stat[:, so + 6 + sub:so + 7 + sub],
                    ALU.subtract, ALU.mult)))(), waits=flat(ev_rstd), inc="dve"))
            ev_xn = ev_xn_sub[-1]
            ev_statrd_prev[par] = ev_xn
            if ti > 0:
                ev_bnd = S.op("vector", (lambda par=par: (lambda v: v.tensor_copy(
                    v3(xnfm2[par])[:, :, 0:1], v3(xnfm2[1 - par])[:, :, W:W + 1])))(),
                    waits=flat(ev_xnfm_all[1 - par], ev_xnfm_rd[par]), inc="dve")
            else:
                ev_bnd = None
            ev_bnd_ev[par] = ev_bnd
            ev_grp_evac = []
            evt_last = None
            for cpair in range(4):
                qname = "QT" if cpair % 2 == 0 else "QS"
                q = QQ[qname]
                evt = None
                for k in range(2):
                    cik = cpair * 2 + k
                    for sub in range(NSUB):
                        evt = S.op("tensor", (lambda cik=cik, sub=sub, q=q, k=k:
                            (lambda t: t.transpose(
                                q[:, k * 256 + sub * 128: k * 256 + (sub + 1) * 128],
                                xb[:, sub * 1024 + cik * 128: sub * 1024 + (cik + 1) * 128],
                                eye[:, :])))(),
                            waits=flat(ev_xn, last_evac.get(qname),
                                       ev_smalls if ti == 0 else None), inc="pe")
                eve = S.op("scalar", (lambda cpair=cpair, q=q, xf=xf: (lambda s: s.activation(
                    v3(xf)[:, cpair * 2:cpair * 2 + 2, 1:W + 1],
                    q[:, 0:512].rearrange("p (c w) -> p c w", w=W), AF.Copy)))(),
                    waits=flat(evt, ev_xnfm_rd[par]), inc="act")
                ev_grp_evac.append(eve)
                last_evac[qname] = eve
                evt_last = evt
            ev_evac_all = ev_grp_evac[-1]
            ev_xnfm_all[par] = ev_evac_all
            ev_grp_evac2[par] = ev_grp_evac
            ev_xb_free = ev_max(evt_last, ev_stats_acc)
            if not warm:
                ev_cast_xn = S.op("scalar", (lambda xf=xf: (lambda s: s.activation(
                    p3(xnbf_s), v3(xf)[:, :, 1:W + 1], AF.Copy)))(),
                    waits=flat(ev_evac_all, ev_xn_stored), inc="act")
                ev_xn_stored = S.op("scalar", (lambda mi=mi: (lambda e: e.dma_start(
                    scr_xn[mi * 128:(mi + 1) * 128, :], xnbf_s[:, :])))(),
                    waits=[ev_cast_xn], inc="st_xn0", k=16)
            ev_dxn = S.op("pool", (lambda xf=xf, dx=dx: (lambda v: v.tensor_tensor(
                p3(dx), v3(xf)[:, :, 1:W + 1], v3(xf)[:, :, 0:W], ALU.subtract)))(),
                waits=flat(ev_evac_all, ev_bnd, ev_dxn_rd[par]), inc="pool")
            ev_dxn_ev[par] = ev_dxn
            ev_rt_last = None
            if not warm:
                tpar = ti % 2
                qoff = tpar * 256
                evr = None
                for ci in range(8):
                    evr = S.op("tensor", (lambda ci=ci, xf=xf, qoff=qoff:
                        (lambda t: t.matmul(
                            QR[0:8, qoff:qoff + 256],
                            wrt_sb[:, ci * 8:(ci + 1) * 8],
                            xf[:, ci * (W + 1) + 1: ci * (W + 1) + 1 + W],
                            start=(ci == 0), stop=(ci == 7))))(),
                        waits=flat(ev_grp_evac[ci // 2], rtr_read[tpar]),
                        inc="pe" if ci == 7 else None)
                rt_pend[ti] = evr
                ev_rt_last = evr
            ev_xnfm_rd[par] = ev_rt_last

        def emit_B1(ti):
            nonlocal ev_kv_prev
            par = ti % 2
            warm = ti < NWARM
            xf = xnfm2[par]
            dx = dxn2[par]
            mixes = [("k", 0, 0), ("v", 1, 1)] + ([] if warm else [("r", 2, 0)])
            proj_ev = {}
            pmm = {}
            ev_mix_last = None
            ev_mix1_last = None
            for (mn, mslot, mpar) in mixes:
                S.op("vector", (lambda mslot=mslot, mpar=mpar, dx=dx:
                    (lambda v: v.tensor_tensor(
                        p3(mixbf[mpar]), p3(dx),
                        tmrep[:, mslot * 2048:(mslot + 1) * 2048]
                        .rearrange("p (c w) -> p c w", w=W),
                        ALU.mult)))(),
                    waits=flat(ev_dxn_ev[par], last_pe_mix[mpar]), inc="dve")
                ev_mix = S.op("vector", (lambda mpar=mpar, xf=xf:
                    (lambda v: v.tensor_tensor(
                        p3(mixbf[mpar]), p3(mixbf[mpar]), v3(xf)[:, :, 0:W],
                        ALU.add)))(),
                    waits=flat(ev_bnd_ev[par], ev_xnfm_all[par]), inc="dve")
                evmix = [ev_mix] * 8
                ev_mix_last = ev_mix
                ev_mix1_last = ev_mix
                dstb = {"k": kcb, "v": vvb, "r": ub}[mn]
                for h2 in range(2):
                    qname = ("QA", "QB")[h2]
                    q = QQ[qname]
                    ev_mm = None
                    for dj in range(h2 * 4, h2 * 4 + 4):
                        off = (dj % 4) * W
                        for ci in range(8):
                            ev_mm = S.op("tensor", (lambda mslot=mslot, ci=ci, dj=dj,
                                                    q=q, mpar=mpar, off=off:
                                (lambda t: t.matmul(
                                    q[:, off:off + W],
                                    wbig[:, mslot * 8192 + ci * 1024 + dj * 128:
                                         mslot * 8192 + ci * 1024 + (dj + 1) * 128],
                                    mixbf[mpar][:, ci * W:(ci + 1) * W],
                                    start=(ci == 0), stop=(ci == 7))))(),
                                waits=flat(evmix[ci], last_evac.get(qname),
                                           ev_cast[(mslot, ci)]
                                           if (ti == 0 or (mslot == 2 and ti == NWARM))
                                           else None),
                                inc="pe" if ci == 7 else None)
                    if mn == "k":
                        war = [ev_den1_prev, ev_s12_prev]
                        fn = (lambda h2=h2, q=q, dstb=dstb: (lambda s: s.activation(
                            dstb[:, h2 * 1024:(h2 + 1) * 1024], q[:, :], AF.Exp)))()
                    elif mn == "v":
                        war = [ev_kv_prev]
                        fn = (lambda h2=h2, q=q, dstb=dstb: (lambda s: s.activation(
                            dstb[:, h2 * 1024:(h2 + 1) * 1024], q[:, :], AF.Copy)))()
                    else:
                        war = [ev_attn_last,
                               ev_prologue_cast_end if ti == NWARM else None]
                        fn = (lambda h2=h2, q=q, dstb=dstb: (lambda s: s.activation(
                            dstb[:, h2 * 1024:(h2 + 1) * 1024], q[:, :], AF.Exp,
                            scale=-1.0)))()
                    eve = S.op("scalar", fn, waits=flat(ev_mm, *war), inc="act")
                    last_evac[qname] = eve
                    proj_ev[(mn, h2)] = eve
                    pmm[mn] = ev_mm
                last_pe_mix[mpar] = ev_mm
            ev_xnfm_rd[par] = ev_max(ev_xnfm_rd[par], ev_mix_last)
            ev_dxn_rd[par] = ev_mix1_last
            proj_evs[ti] = proj_ev
            proj_mm[ti] = pmm
            ev_kcl = S.op("vector", lambda v: v.tensor_scalar(
                kcb[:, :], kcb[:, :], E60, None, ALU.min),
                waits=flat(proj_ev[("k", 0)], proj_ev[("k", 1)]), inc="dve")
            if warm:
                ev_kcl = S.op("vector", lambda v: v.tensor_scalar(
                    kcb[:, :], kcb[:, :], wmk[:, 0:1], None, ALU.mult),
                    waits=flat(ev_smalls, ev_kcl), inc="dve")
            ev_kv = S.op("vector", lambda v: v.tensor_tensor(
                kvb[:, :], kcb[:, :], vvb[:, :], ALU.mult),
                waits=flat(proj_ev[("v", 0)], proj_ev[("v", 1)], ev_kcl,
                           ev_attn_last), inc="dve")
            kv_evs[ti] = ev_kv
            ev_kv_prev = ev_kv
            if ti > 0:
                evb = []
                for buf in (sa, sbb):
                    evb.append(S.op("vector", (lambda buf=buf: (lambda v: v.tensor_copy(
                        v3(buf)[:, :, 0:1], v3(buf)[:, :, W:W + 1])))(), inc="dve"))
                ev_sbnd = evb[-1]
            else:
                ev_sbnd = None
            ev_scanB = {}
            for dj in range(8):
                ev_scanB[dj] = S.op("vector", (lambda dj=dj: (lambda v: v.tensor_tensor_scan(
                    sbb[:, dj * (W + 1) + 1: dj * (W + 1) + 1 + W],
                    dbc(dj), kcb[:, dj * W:(dj + 1) * W],
                    sbb[:, dj * (W + 1): dj * (W + 1) + 1], ALU.mult, ALU.add)))(),
                    waits=flat(ev_kcl, ev_sbnd, e2, ev_bnd0 if ti == 0 else None),
                    inc="dve", hard=(dj == 0))
            for dj in range(8):
                S.op("vector", (lambda dj=dj: (lambda v: v.tensor_tensor_scan(
                    sa[:, dj * (W + 1) + 1: dj * (W + 1) + 1 + W],
                    dbc(dj), kvb[:, dj * W:(dj + 1) * W],
                    sa[:, dj * (W + 1): dj * (W + 1) + 1], ALU.mult, ALU.add)))(),
                    waits=flat(ev_kv), inc="dve")
            scanB_evs[ti] = ev_scanB

        def emit_B2a(ti):
            nonlocal ev_t1_prev, ev_attn_last, ev_den1_prev, ev_s12_prev
            proj_ev = proj_evs[ti]
            ev_scanB = scanB_evs[ti]
            for dj in range(8):
                S.op("vector", (lambda dj=dj: (lambda v: v.tensor_scalar(
                    kvb[:, dj * W:(dj + 1) * W], kvb[:, dj * W:(dj + 1) * W],
                    etf[:, dj:dj + 1], None, ALU.mult)))(),
                    waits=flat(ev_etf), inc="dve")
            ev_n2 = S.op("vector", lambda v: v.tensor_tensor(
                p3(kvb), p3(kvb), v3(sa)[:, :, 0:W], ALU.add), inc="dve")
            for dj in range(8):
                S.op("vector", (lambda dj=dj: (lambda v: v.tensor_scalar(
                    den[:, dj * W:(dj + 1) * W], kcb[:, dj * W:(dj + 1) * W],
                    etf[:, dj:dj + 1], None, ALU.mult)))(),
                    waits=flat(ev_t1_prev if dj == 0 else None), inc="dve")
            ev_d2 = S.op("vector", lambda v: v.tensor_tensor(
                p3(den), p3(den), v3(sbb)[:, :, 0:W], ALU.add), inc="dve")
            ev_den1_prev = ev_d2
            ev_t1 = S.op("scalar", lambda s: s.activation(
                t1b[:, :], den[:, :], AF.Ln, bias=1e-8),
                waits=flat(ev_d2, ev_prologue_cast_end if ti == NWARM else None),
                inc="act")
            ev_t1_prev = ev_t1
            ev_t2 = S.op("scalar", lambda s: s.activation(
                kcb[:, :], ub[:, :], AF.Ln, bias=1.0),
                waits=flat(proj_ev[("r", 0)], proj_ev[("r", 1)], ev_d2,
                           ev_scanB[7], kv_evs[ti]), inc="act")
            ev_s12 = S.op("vector", lambda v: v.tensor_tensor(
                t1b[:, :], t1b[:, :], kcb[:, :], ALU.add),
                waits=flat(ev_t1, ev_t2), inc="dve")
            ev_s12_prev = ev_s12
            ev_rcp = S.op("scalar", lambda s: s.activation(
                ub[:, :], t1b[:, :], AF.Exp, scale=-1.0), waits=[ev_s12], inc="act")
            ev_attn = S.op("vector", lambda v: v.tensor_tensor(
                abuf[:, :], kvb[:, :], ub[:, :], ALU.mult),
                waits=flat(ev_rcp, ev_n2, ev_abuf_rd), inc="dve")
            attn_evs[ti] = ev_attn
            ev_attn_last = ev_attn

        def emit_B2b(ti):
            nonlocal ev_s_stored, ev_abuf_rd, ev_wo_last
            mi = ti - NWARM
            ev_attn = attn_evs[ti]
            ev_mm = None
            evst = None
            for h2 in range(2):
                qname = ("QA", "QB")[h2]
                q = QQ[qname]
                for cj in range(h2 * 4, h2 * 4 + 4):
                    off = (cj % 4) * W
                    for dj in range(8):
                        ev_mm = S.op("tensor", (lambda cj=cj, dj=dj, q=q, off=off:
                            (lambda t: t.matmul(
                                q[:, off:off + W],
                                wbig[:, 3 * 8192 + dj * 1024 + cj * 128:
                                     3 * 8192 + dj * 1024 + (cj + 1) * 128],
                                abuf[:, dj * W:(dj + 1) * W],
                                start=(dj == 0), stop=(dj == 7))))(),
                            waits=flat(ev_attn, last_evac.get(qname),
                                       ev_cast[(3, dj)] if ti == NWARM else None),
                            inc="pe" if dj == 7 else None)
                eve = S.op("scalar", (lambda h2=h2, q=q: (lambda s: s.activation(
                    sbf[:, h2 * 1024:(h2 + 1) * 1024], q[:, :], AF.Copy)))(),
                    waits=flat(ev_mm, ev_s_stored,
                               ev_bexpc if ti == NWARM else None), inc="act")
                last_evac[qname] = eve
                evst = S.op("scalar", (lambda mi=mi, h2=h2: (lambda e: e.dma_start(
                    scr_s[h2 * 512:(h2 + 1) * 512, mi * W:(mi + 1) * W]
                    .rearrange("(c p) w -> p c w", p=128),
                    sbf[:, h2 * 1024:(h2 + 1) * 1024]
                    .rearrange("p (c w) -> p c w", w=W))))(),
                    waits=[eve], inc="st_s", k=16)
            ev_abuf_rd = ev_mm
            ev_wo_last = ev_mm
            ev_s_stored = evst

        emit_A_stats(0)
        emit_A_rest(0)
        emit_wcast(0)
        emit_reps()
        emit_A_stats(1)
        emit_wcast(1)
        emit_A_rest(1)
        prev_b2 = None
        for ti in range(ntiles):
            if ti + 2 < ntiles:
                emit_A_stats(ti + 2)
            emit_B1(ti)
            if ti == 0:
                emit_wcast(2)
            if ti == 1:
                emit_wcast(3)
                ev_prologue_cast_end = max(
                    (ev_cast[(m, ci)] for m in range(4) for ci in range(8)),
                    key=lambda e: e[1])
            if (ti + 1) in rt_pend:
                emit_masks(ti + 1)
            if prev_b2 is not None:
                emit_B2b(prev_b2)
                prev_b2 = None
            if ti + 2 < ntiles:
                emit_A_rest(ti + 2)
            if ti >= NWARM:
                emit_B2a(ti)
                prev_b2 = ti
        if (ntiles - 1) in rt_pend:
            emit_masks(ntiles - 1)
        if prev_b2 is not None:
            emit_B2b(prev_b2)

        barrier = ev_max(ev_s_stored, ev_xn_stored)
        ev_sload_free = {}
        ev_xnld_free = [None, None]
        ev_sg_prev = {}
        ev_out_prev = {}
        ev_slot_init = ev_xb_free
        ev_mm = None
        bias_bank_read = [last_evac.get("QS"),
                          ev_max(last_evac.get("QS"), rtt_read[0], rtt_read[1],
                                 rtt_read[2], rtt_read[3])]
        tp_bank_read = [last_evac.get("QT"), last_evac.get("QT")]
        maskT_war = [None, None]
        bias_sb_free = [None, None, None, None]
        babuf_free = {}
        ev_silu_prev = {}
        EXPQ = [("QA", 0), ("QA", 512), ("QB", 0), ("QB", 512)]
        last_exp_bank_read = {("QA", 0): last_evac.get("QA"),
                              ("QA", 512): last_evac.get("QA"),
                              ("QB", 0): last_evac.get("QB"),
                              ("QB", 512): last_evac.get("QB")}
        ev_exp_free = {}

        def emit_bias_prologue(half):
            nonlocal bias_sb_free
            for j in range(nmain):
                for sub in range(NSUB):
                    msl = (j * NSUB + sub) * 8
                    bi2 = (j * 2 + sub) % 2
                    slot = (j * 2 + sub) % 4
                    ev_mt = S.op("tensor", (lambda msl=msl, bi2=bi2: (lambda t: t.transpose(
                        QT[0:8, bi2 * 128:(bi2 + 1) * 128], masks[:, msl:msl + 8],
                        eye[:, :])))(),
                        waits=flat(tp_bank_read[bi2], ev_masks[(j, sub)]), inc="pe")
                    ev_mte = S.op("scalar", (lambda bi2=bi2: (lambda s: s.activation(
                        maskTb[bi2][:, :], QT[0:8, bi2 * 128:(bi2 + 1) * 128],
                        AF.Copy)))(),
                        waits=flat(ev_mt, maskT_war[bi2]), inc="act")
                    tp_bank_read[bi2] = ev_mte
                    ev_bm = S.op("tensor", (lambda half=half, bi2=bi2: (lambda t: t.matmul(
                        QS[:, bi2 * 512:(bi2 + 1) * 512], maskTb[bi2][0:8, :],
                        sbexp_bf[0:8, half * 512:(half + 1) * 512],
                        start=True, stop=True)))(),
                        waits=flat(ev_mte, bias_bank_read[bi2], ev_bexpc), inc="pe")
                    maskT_war[bi2] = ev_bm
                    ev_be = S.op("scalar", (lambda slot=slot, bi2=bi2: (lambda s: s.activation(
                        sbf[:, slot * 512:(slot + 1) * 512],
                        QS[:, bi2 * 512:(bi2 + 1) * 512], AF.Copy)))(),
                        waits=flat(ev_bm, bias_sb_free[slot],
                                   ev_s_stored if (half == 0 and j == 0) else None),
                        inc="act")
                    bias_bank_read[bi2] = ev_be
                    bias_sb_free[slot] = S.op("scalar", (lambda j=j, sub=sub, half=half,
                                                         slot=slot: (lambda q: q.dma_start(
                        scr_b[j * W + sub * 128: j * W + (sub + 1) * 128,
                              half * 512:(half + 1) * 512],
                        sbf[:, slot * 512:(slot + 1) * 512])))(),
                        waits=[ev_be], inc=f"sb{slot}", k=16)

        _stage_bufs = [b.bitcast(BF16) for b in (sa, sbb, xnfm2[0], xnfm2[1])]
        _dxn_bf = [dxn2[0], dxn2[1]]
        NSTAGE = 5

        def stage_ap(e, ci):
            if e < 4:
                return _stage_bufs[e][:, ci * 512:(ci + 1) * 512]
            return _dxn_bf[ci // 4][:, (ci % 4) * 512:(ci % 4 + 1) * 512]
        ev_stage = {}

        for half in range(2):
            ev_ecast = {}
            ev_slotf = [ev_slot_init] * 4
            elist = range(8) if half == 0 else range(NSTAGE, 8)
            for e in elist:
                for ci in range(8):
                    si = (e * 8 + ci) % 4
                    evl = S.op("sync", (lambda e_=e, ci=ci, si=si, half=half: (lambda q: q.dma_start(
                        xb[:, si * 512:(si + 1) * 512],
                        expT[e_ * C + ci * 128: e_ * C + (ci + 1) * 128,
                             half * 512:(half + 1) * 512])))(),
                        waits=flat(ev_slotf[si]), inc=f"sl{si}", k=16)
                    if half == 0:
                        reg = ((e * 8 + ci) * 512) // 8192
                        regev = (proj_mm[ntiles - 1]["k"], proj_mm[ntiles - 1]["v"],
                                 proj_mm[ntiles - 1]["r"], ev_wo_last)[reg]
                    else:
                        regev = ev_exp_free.get(e)
                    evc = S.op("vector", (lambda e_=e, ci=ci, si=si: (lambda v: v.tensor_copy(
                        wbig[:, (e_ * 8 + ci) * 512:(e_ * 8 + ci + 1) * 512],
                        xb[:, si * 512:(si + 1) * 512])))(),
                        waits=flat(evl, regev), inc="dve")
                    ev_ecast[(e, ci)] = evc
                    ev_slotf[si] = evc
            if half == 0:
                ev_ecast_e = [max((ev_ecast[(e, ci)] for ci in range(8)),
                               key=lambda x: x[1]) for e in range(8)]
                emit_bias_prologue(half)
                for e in range(NSTAGE):
                    for ci in range(8):
                        si = (e * 8 + ci) % 4
                        evl = S.op("sync", (lambda e_=e, ci=ci, si=si: (lambda q: q.dma_start(
                            xb[:, si * 512:(si + 1) * 512],
                            expT[e_ * C + ci * 128: e_ * C + (ci + 1) * 128,
                                 512:1024])))(),
                            waits=flat(ev_slotf[si]), inc=f"sl{si}", k=16)
                        evc = S.op("scalar", (lambda e_=e, ci=ci, si=si: (lambda s: s.activation(
                            stage_ap(e_, ci),
                            xb[:, si * 512:(si + 1) * 512], AF.Copy)))(),
                            waits=[evl], inc="act")
                        ev_stage[(e, ci)] = evc
                        ev_slotf[si] = evc
                ev_slot_init = ev_slotf[3]
            else:
                ev_ecast_e = [max((ev_stage[(e, ci)] for ci in range(8)),
                               key=lambda x: x[1]) for e in range(NSTAGE)]
                ev_ecast_e += [max((ev_ecast[(e, ci)] for ci in range(8)),
                                key=lambda x: x[1]) for e in range(NSTAGE, 8)]
                emit_bias_prologue(half)
            ev_bias_all = ev_max(*[bias_sb_free[s] for s in range(4)])
            babuf_free = {0: ev_abuf_rd, 1: ev_abuf_rd} if half == 0 else babuf_free

            for j in range(nmain):
                jp = j % 2
                jo = jp * 1024
                ev_xnld = S.op("pool", (lambda j=j, jp=jp: (lambda q: q.dma_start(
                    xnbf2[jp][:, :], scr_xn[j * 128:(j + 1) * 128, :])))(),
                    waits=flat(barrier if (half == 0 and j <= 1) else None,
                               ev_xnld_free[jp]), inc=f"ld_xn{jp}", k=16)
                ev_sld = []
                for sub in range(NSUB):
                    ev_sld.append(S.op("scalar", (lambda j=j, sub=sub, half=half, jo=jo:
                        (lambda q: q.dma_start(
                            mixbf[0][:, jo + sub * 512: jo + (sub + 1) * 512],
                            scr_s[half * 512:(half + 1) * 512,
                                  j * W + sub * 128: j * W + (sub + 1) * 128],
                            transpose=True)))(),
                        waits=flat(barrier if (half == 0 and j == 0) else None,
                                   ev_sload_free.get(jp)),
                        inc=f"ld_s{jp}", k=16))
                ev_xld = []
                for sub in range(NSUB):
                    ev_xld.append(S.op("pool", (lambda j=j, sub=sub, half=half, jo=jo:
                        (lambda q: q.dma_start(
                            kvb[:, jo + sub * 512: jo + (sub + 1) * 512],
                            xs[NWARM * W + j * W + sub * 128:
                               NWARM * W + j * W + (sub + 1) * 128,
                               half * 512:(half + 1) * 512])))(),
                        waits=flat(barrier if (half == 0 and j == 0) else None,
                                   ev_sload_free.get(jp)),
                        inc=f"ld_xr{jp}", k=16))
                ev_bld = []
                for sub in range(NSUB):
                    ev_bld.append(S.op("pool", (lambda j=j, sub=sub, half=half, jo=jo:
                        (lambda q: q.dma_start(
                            abuf[:, jo + sub * 512: jo + (sub + 1) * 512],
                            scr_b[j * W + sub * 128: j * W + (sub + 1) * 128,
                                  half * 512:(half + 1) * 512])))(),
                        waits=flat(ev_bias_all if j <= 1 else None,
                                   babuf_free.get(jp)),
                        inc=f"ld_b{jp}", k=16))

                for sub in range(NSUB):
                    msl = (j * NSUB + sub) * 8
                    ev_acc = None
                    eorder = ([5, 6, 7, 0, 1, 2, 3, 4]
                              if (half == 0 and j == nmain - 1) else list(range(8)))
                    for idx_e, e in enumerate(eorder):
                        qname, qoff = EXPQ[e % 4]
                        q = QQ[qname]
                        if half == 1 and e < NSTAGE:
                            wsrc_e = (lambda e_=e: (lambda ci: stage_ap(e_, ci)))()
                        else:
                            wsrc_e = (lambda e_=e: (lambda ci:
                                wbig[:, (e_ * 8 + ci) * 512:(e_ * 8 + ci + 1) * 512]))()
                        ev_mm = None
                        for ci in range(8):
                            ev_mm = S.op("tensor", (lambda ci=ci, sub=sub, q=q,
                                                    qoff=qoff, jp=jp, wsrc_e=wsrc_e:
                                (lambda t: t.matmul(
                                    q[:, qoff:qoff + 512],
                                    xnbf2[jp][:, ci * W + sub * 128: ci * W + (sub + 1) * 128],
                                    wsrc_e(ci),
                                    start=(ci == 0), stop=(ci == 7))))(),
                                waits=flat(ev_xnld, ev_ecast_e[e] if j == 0 else None,
                                           last_exp_bank_read.get((qname, qoff))),
                                inc="pe" if ci == 7 else None)
                        if half == 0 and j == nmain - 1 and sub == NSUB - 1:
                            ev_exp_free[e] = ev_mm
                        src_in1 = ((lambda jo=jo, sub=sub: abuf[:, jo + sub * 512:
                                                               jo + (sub + 1) * 512])()
                                   if idx_e == 0 else
                                   (lambda jo=jo, sub=sub: kcb[:, jo + sub * 512:
                                                               jo + (sub + 1) * 512])())
                        ev_acc = S.op("vector", (lambda e_=e, sub=sub, q=q, qoff=qoff,
                                                 msl=msl, jo=jo, src_in1=src_in1:
                            (lambda v: v.scalar_tensor_tensor(
                                kcb[:, jo + sub * 512: jo + (sub + 1) * 512],
                                q[:, qoff:qoff + 512],
                                masks[:, msl + e_: msl + e_ + 1], src_in1,
                                ALU.mult, ALU.add)))(),
                            waits=flat(ev_mm, ev_bld[-1] if idx_e == 0 else None,
                                       ev_silu_prev.get((jp, sub)) if idx_e == 0 else None,
                                       ev_acc), inc="dve")
                        last_exp_bank_read[(qname, qoff)] = ev_acc
                        if idx_e == 7:
                            babuf_free[jp] = ev_acc
                    if debug:
                        S.op("scalar", (lambda j=j, sub=sub, half=half, jo=jo:
                            (lambda q: q.dma_start(
                                scr_g[j * W + sub * 128: j * W + (sub + 1) * 128,
                                      half * 512:(half + 1) * 512],
                                kcb[:, jo + sub * 512: jo + (sub + 1) * 512])))(),
                            waits=[ev_acc], inc="st_g", k=16)
                    ev_silu = S.op("scalar", (lambda sub=sub, jo=jo: (lambda s: s.activation(
                        t1b[:, jo + sub * 512: jo + (sub + 1) * 512],
                        kcb[:, jo + sub * 512: jo + (sub + 1) * 512],
                        AF.Silu)))(), waits=flat(ev_acc, ev_sg_prev.get((jp, sub))), inc="act")
                    ev_silu_prev[(jp, sub)] = ev_silu
                    ev_sg = S.op("pool", (lambda sub=sub, jo=jo: (lambda g: g.tensor_tensor(
                        ub[:, jo + sub * 512: jo + (sub + 1) * 512],
                        t1b[:, jo + sub * 512: jo + (sub + 1) * 512],
                        mixbf[0][:, jo + sub * 512: jo + (sub + 1) * 512], ALU.mult)))(),
                        waits=flat(ev_silu, ev_sld[-1]), inc="pool")
                    ev_sg_prev[(jp, sub)] = ev_sg
                    ev_o = S.op("pool", (lambda sub=sub, jo=jo: (lambda g: g.tensor_tensor(
                        den[:, jo + sub * 512: jo + (sub + 1) * 512],
                        ub[:, jo + sub * 512: jo + (sub + 1) * 512],
                        kvb[:, jo + sub * 512: jo + (sub + 1) * 512], ALU.add)))(),
                        waits=flat(ev_xld[-1], ev_out_prev.get(jp)), inc="pool")
                    ev_st = S.op("scalar", (lambda j=j, sub=sub, half=half, jo=jo:
                        (lambda q: q.dma_start(
                            out[j * W + sub * 128: j * W + (sub + 1) * 128,
                                half * 512:(half + 1) * 512],
                            den[:, jo + sub * 512: jo + (sub + 1) * 512])))(),
                        waits=[ev_o], inc=f"st_o{jp}", k=16)
                    ev_out_prev[jp] = ev_st
                ev_sload_free[jp] = ev_o
                ev_xnld_free[jp] = ev_mm

        if debug:
            S.op("sync", lambda q: q.dma_start(scr_mask, masks[:, :]),
                 waits=flat(*[ev_masks[k2] for k2 in ev_masks]), inc="st_g", k=16)
        S.op("sync", lambda q: q.nop(),
             waits=[(n, S.cnt[n]) for n in
                    ("st_s", "st_xn0", "st_xn1", "st_o0", "st_o1", "st_g")
                    if S.cnt.get(n)])

        with nc.Block() as block:
            @block.sync
            def _(e):
                S.replay("sync", e, sems)

            @block.vector
            def _(e):
                S.replay("vector", e, sems)

            @block.scalar
            def _(e):
                S.replay("scalar", e, sems)

            @block.tensor
            def _(e):
                S.replay("tensor", e, sems)

            @block.gpsimd
            def _(e):
                S.replay("pool", e, sems)

    return nc


_CACHE = {}


def _get_program(nmain):
    if nmain not in _CACHE:
        _CACHE[nmain] = build_program(nmain)
    return _CACHE[nmain]


def kernel(L0_attn_xq, L0_attn_xk, L0_attn_xv, L0_attn_time_decay, L0_attn_time_first,
           L0_attn_time_mix_k, L0_attn_time_mix_v, L0_attn_time_mix_r,
           L0_attn_Wk, L0_attn_Wv, L0_attn_Wr, L0_attn_Wo,
           L0_ffn_w_router, L0_ffn_b_router, L0_ffn_w_experts, L0_ffn_b_experts):
    x = np.ascontiguousarray(np.asarray(L0_attn_xq, dtype=np.float32))
    B, T, Cc = x.shape
    assert (B, T, Cc) == (4, 8192, 1024)
    nmain = 16
    TOK = W * nmain
    NROW = W * (NWARM + nmain)

    def chunked(a):
        a = np.asarray(a, np.float32).reshape(-1)
        return np.ascontiguousarray(a.reshape(8, 128).T)

    common = {
        "wkT": np.ascontiguousarray(np.asarray(L0_attn_Wk, np.float32).T),
        "wvT": np.ascontiguousarray(np.asarray(L0_attn_Wv, np.float32).T),
        "wrT": np.ascontiguousarray(np.asarray(L0_attn_Wr, np.float32).T),
        "woT": np.ascontiguousarray(np.asarray(L0_attn_Wo, np.float32).T),
        "expT": np.ascontiguousarray(
            np.asarray(L0_ffn_w_experts, np.float32).transpose(0, 2, 1).reshape(8 * C, C)),
        "tdp": chunked(L0_attn_time_decay),
        "tfp": chunked(L0_attn_time_first),
        "tmixp": np.ascontiguousarray(np.concatenate(
            [chunked(L0_attn_time_mix_k), chunked(L0_attn_time_mix_v),
             chunked(L0_attn_time_mix_r)], axis=1)),
        "wroutT": np.ascontiguousarray(np.asarray(L0_ffn_w_router, np.float32).T),
        "brout": np.ascontiguousarray(np.broadcast_to(
            np.asarray(L0_ffn_b_router, np.float32).reshape(1, 8), (128, 8))),
        "bexp": np.ascontiguousarray(np.asarray(L0_ffn_b_experts, np.float32)),
        "eye_in": np.eye(128, dtype=np.float32),
    }

    in_maps = []
    for core in range(8):
        b, h = core // 2, core % 2
        t0 = h * TOK
        xsv = np.zeros((NROW, C), np.float32)
        lo = max(0, t0 - NWARM * W)
        xsv[NWARM * W - (t0 - lo):] = x[b, lo:t0 + TOK]
        m = dict(common)
        m["xs"] = xsv
        m["wmask_in"] = np.full((128, 1), 1.0 if h == 1 else 0.0, np.float32)
        in_maps.append(m)

    nc = _get_program(nmain)
    res = run_bass_kernel_spmd(nc, in_maps, core_ids=list(range(8)))
    outf = np.empty((B, T, C), np.float32)
    for core in range(8):
        b, h = core // 2, core % 2
        outf[b, h * TOK:(h + 1) * TOK] = np.asarray(res.results[core]["out"])
    return outf


# version 37
# speedup vs baseline: 1.1668x; 1.1332x over previous
import numpy as np
import ml_dtypes
from contextlib import ExitStack

import concourse.bass as bass
import concourse.mybir as mybir
from concourse.bass_utils import run_bass_kernel_spmd

F32 = mybir.dt.float32
BF16 = mybir.dt.bfloat16
AF = mybir.ActivationFunctionType
ALU = mybir.AluOpType
AX = mybir.AxisListType.X

C = 1024
NCH = 8
W = 256
NSUB = W // 128
NWARM = 2
E60 = float(np.float32(np.exp(np.float64(60.0))))


PARANOID = False
SKIP_OWN = True


class Sched:

    def __init__(self):
        self.items = []
        self.cnt = {}

    AUTO = {"vector": "dve", "scalar": "act", "pool": "pool"}

    def op(self, eng, fn, waits=(), inc=None, k=1, hard=False):
        ws = []
        if PARANOID:
            ws.extend((s, v) for s, v in self.cnt.items() if v > 0)
        if inc is None:
            inc = self.AUTO.get(eng)
        incs = []
        ev = None
        if inc is not None:
            self.cnt[inc] = self.cnt.get(inc, 0) + k
            ev = (inc, self.cnt[inc])
            incs.append((inc, k))
        for w in waits:
            if w is None:
                continue
            if isinstance(w, list):
                ws.extend(w)
            else:
                ws.append(w)
        self.items.append((eng, fn, tuple(ws), incs, k == 16 or hard))
        return ev

    def replay(self, engname, engobj, sems):
        lastwait = {}
        own = self.AUTO.get(engname)
        owncnt = 0
        skip_own = engname == "vector" and SKIP_OWN
        for eng, fn, waits, incs, is_dma in self.items:
            if eng != engname:
                continue
            allw = list(waits)
            if own is not None and owncnt > 0 and not (skip_own and not is_dma):
                allw.append((own, owncnt))
            for (s, v) in allw:
                if s == own and skip_own and not is_dma:
                    continue
                if lastwait.get(s, 0) < v:
                    engobj.wait_ge(sems[s], v)
                    lastwait[s] = v
            inst = fn(engobj)
            for (s, k) in incs:
                inst.then_inc(sems[s], k)
                if s == own:
                    owncnt += k


def flat(*evs):
    r = []
    for e in evs:
        if e is None:
            continue
        if isinstance(e, list):
            r.extend(e)
        else:
            r.append(e)
    return r


def ev_max(*evs):
    bysem = {}
    for e in flat(*evs):
        s, v = e
        bysem[s] = max(bysem.get(s, 0), v)
    return [(s, v) for s, v in bysem.items()] or None


def build_program(nmain, debug=False):
    ntiles = NWARM + nmain
    TOK = W * nmain
    NROW = W * ntiles
    scratch_kind = "ExternalOutput" if debug else "Internal"

    nc = bass.Bass("TRN2", target_bir_lowering=False)

    xs = nc.dram_tensor("xs", [NROW, C], F32, kind="ExternalInput").ap()
    wkT = nc.dram_tensor("wkT", [C, C], F32, kind="ExternalInput").ap()
    wvT = nc.dram_tensor("wvT", [C, C], F32, kind="ExternalInput").ap()
    wrT = nc.dram_tensor("wrT", [C, C], F32, kind="ExternalInput").ap()
    woT = nc.dram_tensor("woT", [C, C], F32, kind="ExternalInput").ap()
    expT = nc.dram_tensor("expT", [8 * C, C], F32, kind="ExternalInput").ap()
    tdp = nc.dram_tensor("tdp", [128, 8], F32, kind="ExternalInput").ap()
    tfp = nc.dram_tensor("tfp", [128, 8], F32, kind="ExternalInput").ap()
    tmixp = nc.dram_tensor("tmixp", [128, 24], F32, kind="ExternalInput").ap()
    wroutT = nc.dram_tensor("wroutT", [C, 8], F32, kind="ExternalInput").ap()
    brout = nc.dram_tensor("brout", [128, 8], F32, kind="ExternalInput").ap()
    bexp = nc.dram_tensor("bexp", [8, C], F32, kind="ExternalInput").ap()
    eye_in = nc.dram_tensor("eye_in", [128, 128], F32, kind="ExternalInput").ap()
    wmask_in = nc.dram_tensor("wmask_in", [128, 1], F32, kind="ExternalInput").ap()
    out = nc.dram_tensor("out", [TOK, C], F32, kind="ExternalOutput").ap()

    scr_s = nc.dram_tensor("scr_s", [C, TOK], BF16, kind=scratch_kind).ap()
    scr_xn = nc.dram_tensor("scr_xn", [nmain * 128, NCH * W], BF16, kind=scratch_kind).ap()
    scr_b = nc.dram_tensor("scr_b", [TOK, C], BF16, kind="Internal").ap()
    if debug:
        scr_g = nc.dram_tensor("scr_g", [TOK, C], F32, kind="ExternalOutput").ap()
        scr_mask = nc.dram_tensor("scr_mask", [128, nmain * NSUB * 8], F32,
                                  kind="ExternalOutput").ap()

    S = Sched()
    ctx = ExitStack()
    with ctx:
        def sbuf(name, shape, dt=F32):
            return ctx.enter_context(nc.sbuf_tensor(name, shape, dt))

        wbig = sbuf("wbig", [128, 32768], BF16)
        xb = sbuf("xb", [128, 2048])
        xnfm2 = [sbuf("xnfm0", [128, NCH * (W + 1)]), sbuf("xnfm1", [128, NCH * (W + 1)])]
        dxn2 = [sbuf("dxn0", [128, 2048], BF16), sbuf("dxn1", [128, 2048], BF16)]
        tmrep = sbuf("tmrep", [128, 3 * 2048], BF16)
        mixbf = [sbuf("mixbf0", [128, 2048], BF16), sbuf("mixbf1", [128, 2048], BF16)]
        abuf = sbuf("abuf", [128, 2048], BF16)
        xnbf_s = sbuf("xnbf0", [128, 2048], BF16)
        xnbf2 = [xnbf_s, mixbf[1]]
        kcb = sbuf("kcb", [128, 2048])
        vvb = sbuf("vvb", [128, 2048])
        kvb = sbuf("kvb", [128, 2048])
        den = sbuf("den", [128, 2048])
        t1b = sbuf("t1b", [128, 2048])
        ub = sbuf("ub", [128, 2048])
        sa = sbuf("sa", [128, NCH * (W + 1)])
        sbb = sbuf("sbb", [128, NCH * (W + 1)])
        sbf = sbuf("sbf", [128, 2048], BF16)
        masks = sbuf("masks", [128, nmain * NSUB * 8])
        stat = sbuf("stat", [128, 20])
        dd = sbuf("dd", [128, 8])
        etf = sbuf("etf", [128, 8])
        tmix = sbuf("tmix", [128, 24])
        td_sb = sbuf("td_sb", [128, 8])
        tf_sb = sbuf("tf_sb", [128, 8])
        wrt_sb = sbuf("wrt_sb", [128, 64])
        sbrep = sbuf("sbrep", [128, 8])
        rtev = sbuf("rtev", [8, 256])
        eye = sbuf("eye", [128, 128])
        wmk = sbuf("wmk", [128, 1])
        maskTb = [sbuf("maskT0", [8, 128], BF16), sbuf("maskT1", [8, 128], BF16)]
        sbexp_bf = sbuf("sbexp_bf", [8, C], BF16)
        lgt = sbuf("lgt", [128, 8])
        lm1 = sbuf("lm1", [128, 1])
        lm2 = sbuf("lm2", [128, 1])
        leq = sbuf("leq", [128, 8])
        consts = sbuf("consts", [128, 8])

        QA = ctx.enter_context(nc.psum_tensor("QA", [128, 1024], F32))
        QB = ctx.enter_context(nc.psum_tensor("QB", [128, 1024], F32))
        QT = ctx.enter_context(nc.psum_tensor("QT", [128, 512], F32))
        QR = ctx.enter_context(nc.psum_tensor("QR", [128, 512], F32))
        QS = ctx.enter_context(nc.psum_tensor("QS", [128, 1024], F32))
        QQ = {"QA": QA, "QB": QB, "QT": QT, "QR": QR, "QS": QS}

        CONSTS = [-1.0, -0.5, 1e-5, 1e-8, 1.0, 0.0, 0.5, 1.0 / 1024.0]
        for i, cval in enumerate(CONSTS):
            nc.const_aps.register(F32, cval, consts[:, i:i + 1])

        semnames = ["ld_sm", "ld_x", "ld_xn0", "ld_xn1", "ld_s0", "ld_s1", "ld_xr0",
                    "ld_xr1", "ld_b0", "ld_b1", "sl0", "sl1", "sl2", "sl3", "st_s",
                    "st_xn0", "st_xn1", "st_o0", "st_o1", "st_g", "sb0", "sb1", "sb2",
                    "sb3", "pe", "dve", "act", "pool"]
        sems = {n: ctx.enter_context(nc.semaphore(n)) for n in semnames}

        def v3(buf):
            return buf[:, :].rearrange("p (c w) -> p c w", w=W + 1)

        def p3(buf):
            return buf[:, :].rearrange("p (c w) -> p c w", w=W)

        def bcast8(buf, off, rowlen):
            return bass.AP(buf, off, [[rowlen, 128], [1, 8], [0, W]])

        for i, cval in enumerate(CONSTS):
            S.op("vector", (lambda i=i, cval=cval: (lambda v: v.memset(consts[:, i:i + 1], cval)))())
        ev_bnd0 = None
        for buf in (xnfm2[0], xnfm2[1], sa, sbb):
            ev_bnd0 = S.op("vector", (lambda buf=buf: (lambda v: v.memset(v3(buf)[:, :, 0:1], 0.0)))())
        ev_consts = S.op("vector", lambda v: v.memset(lgt[:, :], 0.0), inc="dve")

        evs_small = []
        evs_small.append(S.op("sync", lambda e: e.dma_start(td_sb[:, :], tdp), inc="ld_sm", k=16))
        evs_small.append(S.op("sync", lambda e: e.dma_start(tf_sb[:, :], tfp), inc="ld_sm", k=16))
        evs_small.append(S.op("sync", lambda e: e.dma_start(tmix[:, :], tmixp), inc="ld_sm", k=16))
        for ci in range(8):
            evs_small.append(S.op("sync", (lambda ci=ci: (lambda e: e.dma_start(
                wrt_sb[:, ci * 8:(ci + 1) * 8], wroutT[ci * 128:(ci + 1) * 128, :])))(),
                inc="ld_sm", k=16))
        sbexp = sbf.bitcast(F32)
        evs_small.append(S.op("sync", lambda e: e.dma_start(sbexp[0:8, 0:C], bexp), inc="ld_sm", k=16))
        evs_small.append(S.op("sync", lambda e: e.dma_start(sbrep[:, :], brout), inc="ld_sm", k=16))
        evs_small.append(S.op("sync", lambda e: e.dma_start(eye[:, :], eye_in), inc="ld_sm", k=16))
        evs_small.append(S.op("sync", lambda e: e.dma_start(wmk[:, :], wmask_in), inc="ld_sm", k=16))
        ev_smalls = evs_small[-1]

        ev_bexpc = S.op("vector", lambda v: v.tensor_copy(sbexp_bf[:, :], sbexp[0:8, 0:C]),
                        waits=[ev_smalls], inc="dve")

        e1 = S.op("scalar", lambda s: s.activation(leq[:, :], td_sb[:, :], AF.Exp),
                  waits=[ev_smalls, ev_consts], inc="act")
        e2 = S.op("scalar", lambda s: s.activation(dd[:, :], leq[:, :], AF.Exp, scale=-1.0),
                  waits=[e1], inc="act")
        ev_etf = S.op("scalar", lambda s: s.activation(etf[:, :], tf_sb[:, :], AF.Exp),
                      waits=[e2], inc="act")
        def emit_reps():
            for m in range(3):
                S.op("vector", (lambda m=m: (lambda v: v.tensor_copy(
                    tmrep[:, m * 2048:(m + 1) * 2048]
                    .rearrange("p (c w) -> p c w", w=W),
                    bcast8(tmix, m * 8, 24))))(),
                    waits=[ev_smalls], inc="dve")

        ev_brep = ev_smalls

        wsrc = [wkT, wvT, wrT, woT]
        slots = [(ub, 0), (ub, 1024), (t1b, 0), (t1b, 1024)]
        ev_cast = {}
        ev_slot_free = [None] * 4
        ev_prologue_cast_end = None

        def emit_wcast(m):
            for ci in range(8):
                si = (m * 8 + ci) % 4
                buf, off = slots[si]
                evl = S.op("sync", (lambda m=m, ci=ci, buf=buf, off=off: (lambda e: e.dma_start(
                    buf[:, off:off + 1024], wsrc[m][ci * 128:(ci + 1) * 128, :])))(),
                    waits=[ev_slot_free[si]], inc=f"sl{si}", k=16)
                evc = S.op("scalar", (lambda m=m, ci=ci, buf=buf, off=off: (lambda s: s.activation(
                    wbig[:, m * 8192 + ci * 1024: m * 8192 + (ci + 1) * 1024],
                    buf[:, off:off + 1024], AF.Copy)))(), waits=[evl], inc="act")
                ev_cast[(m, ci)] = evc
                ev_slot_free[si] = evc

        last_evac = {}
        last_pe_mix = [None, None]
        ev_xb_free = None
        ev_s_stored = None
        ev_xn_stored = None
        ev_xn_stored2 = [None, None]
        ev_masks = {}
        ev_stats_chain = None
        ev_statrd_prev = [None, None]
        ev_xnfm_rd = [None, None]
        ev_dxn_rd = [None, None]
        ev_xnfm_all = [None, None]
        ev_grp_evac2 = [None, None]
        ev_bnd_ev = [None, None]
        ev_dxn_ev = [None, None]
        ev_den1_prev = None
        ev_kv_prev = None
        proj_evs = {}
        proj_mm = {}
        attn_evs = {}
        kv_evs = {}
        scanB_evs = {}
        ev_abuf_rd = None
        ev_attn_last = None
        ev_wo_last = None

        def dbc(dj):
            return bass.AP(dd, dj, [[8, 128], [0, W]])

        rt_pend = {}
        rtr_read = {0: ev_brep, 1: ev_brep}
        rtt_read = {0: ev_brep, 1: ev_brep, 2: ev_brep, 3: ev_brep}
        ev_rtev_rd = None

        def emit_masks(ti):
            nonlocal ev_rtev_rd
            mi = ti - NWARM
            tpar = ti % 2
            qoff = tpar * 256
            evr = rt_pend.pop(ti)
            ev_ev = S.op("scalar", (lambda qoff=qoff: (lambda s: s.activation(
                rtev[0:8, :], QR[0:8, qoff:qoff + 256], AF.Copy)))(),
                waits=flat(evr, ev_rtev_rd), inc="act")
            rtr_read[tpar] = ev_ev
            for sub in range(NSUB):
                s_ = tpar * 2 + sub
                ev_tp = S.op("tensor", (lambda sub=sub, s_=s_: (lambda t: t.transpose(
                    QS[:, 512 + s_ * 8: 512 + s_ * 8 + 8],
                    rtev[0:8, sub * 128:(sub + 1) * 128], eye[0:8, 0:8])))(),
                    waits=flat(ev_ev, rtt_read[s_]), inc="pe")
                ev_rtev_rd = ev_tp
                ev_lg = S.op("vector", (lambda s_=s_: (lambda v: v.tensor_tensor(
                    lgt[:, :], QS[:, 512 + s_ * 8: 512 + s_ * 8 + 8],
                    sbrep[:, 0:8], ALU.add)))(),
                    waits=flat(ev_tp, ev_brep), inc="dve", hard=True)
                rtt_read[s_] = ev_lg
                evm1 = S.op("vector", lambda v: v.reduce_max(lm1[:, :], lgt[:, :], axis=AX),
                            waits=[ev_lg], inc="dve", hard=True)
                evm1b = S.op("vector", lambda v: v.tensor_scalar(
                    leq[:, :], lgt[:, :], lm1[:, 0:1], None, ALU.is_equal),
                    waits=[evm1], inc="dve", hard=True)
                evl2 = S.op("vector", lambda v: v.scalar_tensor_tensor(
                    leq[:, :], leq[:, :], -1e30, lgt[:, :], ALU.mult, ALU.add),
                    waits=[evm1b], inc="dve", hard=True)
                evm2 = S.op("vector", lambda v: v.reduce_max(lm2[:, :], leq[:, :], axis=AX),
                            waits=[evl2], inc="dve", hard=True)
                ev_masks[(mi, sub)] = S.op("vector", (lambda mi=mi, sub=sub:
                    (lambda v: v.tensor_scalar(
                        masks[:, (mi * NSUB + sub) * 8:(mi * NSUB + sub + 1) * 8],
                        lgt[:, :], lm2[:, 0:1], None, ALU.is_ge)))(),
                    waits=[evm2], inc="dve", hard=True)

        A_state = {}

        def emit_A_stats(ti):
            nonlocal ev_stats_chain
            par = ti % 2
            so = 10 * par
            r0 = ti * W
            evx = []
            for sub in range(NSUB):
                evx.append(S.op("sync", (lambda r0=r0, sub=sub: (lambda e: e.dma_start(
                    xb[:, sub * 1024:(sub + 1) * 1024],
                    xs[r0 + sub * 128: r0 + (sub + 1) * 128, :])))(),
                    waits=flat(ev_xb_free), inc="ld_x", k=16))
            ev_x = evx[-1]
            ev_kv_war = kv_evs.get(ti - 3)
            evq = ev_stats_chain
            for sub in range(NSUB):
                evq = S.op("scalar", (lambda sub=sub, so=so: (lambda s: s.activation(
                    vvb[:, 0:1024], xb[:, sub * 1024:(sub + 1) * 1024],
                    AF.Square, accum_out=stat[:, so + sub:so + sub + 1])))(),
                    waits=flat(ev_x, ev_statrd_prev[par], evq, ev_kv_war), inc="act")
            for sub in range(NSUB):
                evq = S.op("scalar", (lambda sub=sub, so=so: (lambda s: s.activation(
                    vvb[:, 0:1024], xb[:, sub * 1024:(sub + 1) * 1024],
                    AF.Copy, accum_out=stat[:, so + 2 + sub:so + sub + 3])))(),
                    waits=[evq], inc="act")
            ev_stats_acc = evq
            ev_stats_chain = evq
            S.op("scalar", (lambda so=so: (lambda s: s.activation(
                stat[:, so + 4:so + 6], stat[:, so + 2:so + 4], AF.Copy,
                scale=1.0 / 1024.0)))(), waits=[ev_stats_acc], inc="act")
            S.op("scalar", (lambda so=so: (lambda s: s.activation(
                stat[:, so + 8:so + 10], stat[:, so + 4:so + 6], AF.Square)))(), inc="act")
            S.op("scalar", (lambda so=so: (lambda s: s.activation(
                stat[:, so + 6:so + 8], stat[:, so + 8:so + 10], AF.Copy,
                scale=-1.0, bias=1e-5)))(), inc="act")
            for sub in range(NSUB):
                S.op("scalar", (lambda so=so, sub=sub: (lambda s: s.activation(
                    stat[:, so + 8 + sub:so + 9 + sub], stat[:, so + sub:so + sub + 1],
                    AF.Ln, scale=1.0 / 1024.0, bias=stat[:, so + 6 + sub:so + 7 + sub])))(),
                    inc="act")
            ev_rstd = S.op("scalar", (lambda so=so: (lambda s: s.activation(
                stat[:, so + 6:so + 8], stat[:, so + 8:so + 10], AF.Exp, scale=-0.5)))(),
                inc="act")
            A_state[ti] = (ev_x, ev_rstd, ev_stats_acc)

        def emit_A_rest(ti):
            nonlocal ev_xb_free, ev_xn_stored
            par = ti % 2
            warm = ti < NWARM
            mi = ti - NWARM
            xf = xnfm2[par]
            dx = dxn2[par]
            so = 10 * par
            ev_x, ev_rstd, ev_stats_acc = A_state.pop(ti)
            ev_xn_sub = []
            for sub in range(NSUB):
                ev_xn_sub.append(S.op("vector", (lambda sub=sub, so=so: (lambda v: v.tensor_scalar(
                    xb[:, sub * 1024:(sub + 1) * 1024], xb[:, sub * 1024:(sub + 1) * 1024],
                    stat[:, so + 4 + sub:so + 5 + sub], stat[:, so + 6 + sub:so + 7 + sub],
                    ALU.subtract, ALU.mult)))(), waits=flat(ev_rstd), inc="dve"))
            ev_xn = ev_xn_sub[-1]
            ev_statrd_prev[par] = ev_xn
            if ti > 0:
                ev_bnd = S.op("vector", (lambda par=par: (lambda v: v.tensor_copy(
                    v3(xnfm2[par])[:, :, 0:1], v3(xnfm2[1 - par])[:, :, W:W + 1])))(),
                    waits=flat(ev_xnfm_all[1 - par], ev_xnfm_rd[par]), inc="dve")
            else:
                ev_bnd = None
            ev_bnd_ev[par] = ev_bnd
            ev_grp_evac = []
            evt_last = None
            for cpair in range(4):
                qname = "QT" if cpair % 2 == 0 else "QS"
                q = QQ[qname]
                evt = None
                for k in range(2):
                    cik = cpair * 2 + k
                    for sub in range(NSUB):
                        evt = S.op("tensor", (lambda cik=cik, sub=sub, q=q, k=k:
                            (lambda t: t.transpose(
                                q[:, k * 256 + sub * 128: k * 256 + (sub + 1) * 128],
                                xb[:, sub * 1024 + cik * 128: sub * 1024 + (cik + 1) * 128],
                                eye[:, :])))(),
                            waits=flat(ev_xn, last_evac.get(qname),
                                       ev_smalls if ti == 0 else None), inc="pe")
                eve = S.op("scalar", (lambda cpair=cpair, q=q, xf=xf: (lambda s: s.activation(
                    v3(xf)[:, cpair * 2:cpair * 2 + 2, 1:W + 1],
                    q[:, 0:512].rearrange("p (c w) -> p c w", w=W), AF.Copy)))(),
                    waits=flat(evt, ev_xnfm_rd[par]), inc="act")
                ev_grp_evac.append(eve)
                last_evac[qname] = eve
                evt_last = evt
            ev_evac_all = ev_grp_evac[-1]
            ev_xnfm_all[par] = ev_evac_all
            ev_grp_evac2[par] = ev_grp_evac
            ev_xb_free = ev_max(evt_last, ev_stats_acc)
            if not warm:
                ev_cast_xn = S.op("scalar", (lambda xf=xf: (lambda s: s.activation(
                    p3(xnbf_s), v3(xf)[:, :, 1:W + 1], AF.Copy)))(),
                    waits=flat(ev_evac_all, ev_xn_stored), inc="act")
                ev_xn_stored = S.op("scalar", (lambda mi=mi: (lambda e: e.dma_start(
                    scr_xn[mi * 128:(mi + 1) * 128, :], xnbf_s[:, :])))(),
                    waits=[ev_cast_xn], inc="st_xn0", k=16)
            ev_dxn = S.op("pool", (lambda xf=xf, dx=dx: (lambda v: v.tensor_tensor(
                p3(dx), v3(xf)[:, :, 1:W + 1], v3(xf)[:, :, 0:W], ALU.subtract)))(),
                waits=flat(ev_evac_all, ev_bnd, ev_dxn_rd[par]), inc="pool")
            ev_dxn_ev[par] = ev_dxn
            ev_rt_last = None
            if not warm:
                tpar = ti % 2
                qoff = tpar * 256
                evr = None
                for ci in range(8):
                    evr = S.op("tensor", (lambda ci=ci, xf=xf, qoff=qoff:
                        (lambda t: t.matmul(
                            QR[0:8, qoff:qoff + 256],
                            wrt_sb[:, ci * 8:(ci + 1) * 8],
                            xf[:, ci * (W + 1) + 1: ci * (W + 1) + 1 + W],
                            start=(ci == 0), stop=(ci == 7))))(),
                        waits=flat(ev_grp_evac[ci // 2], rtr_read[tpar]),
                        inc="pe" if ci == 7 else None)
                rt_pend[ti] = evr
                ev_rt_last = evr
            ev_xnfm_rd[par] = ev_rt_last

        def emit_B1(ti):
            nonlocal ev_kv_prev
            par = ti % 2
            warm = ti < NWARM
            xf = xnfm2[par]
            dx = dxn2[par]
            mixes = [("k", 0, 0), ("v", 1, 1)] + ([] if warm else [("r", 2, 0)])
            proj_ev = {}
            pmm = {}
            ev_mix_last = None
            ev_mix1_last = None
            for (mn, mslot, mpar) in mixes:
                S.op("vector", (lambda mslot=mslot, mpar=mpar, dx=dx:
                    (lambda v: v.tensor_tensor(
                        p3(mixbf[mpar]), p3(dx),
                        tmrep[:, mslot * 2048:(mslot + 1) * 2048]
                        .rearrange("p (c w) -> p c w", w=W),
                        ALU.mult)))(),
                    waits=flat(ev_dxn_ev[par], last_pe_mix[mpar]), inc="dve")
                ev_mix = S.op("vector", (lambda mpar=mpar, xf=xf:
                    (lambda v: v.tensor_tensor(
                        p3(mixbf[mpar]), p3(mixbf[mpar]), v3(xf)[:, :, 0:W],
                        ALU.add)))(),
                    waits=flat(ev_bnd_ev[par], ev_xnfm_all[par]), inc="dve")
                evmix = [ev_mix] * 8
                ev_mix_last = ev_mix
                ev_mix1_last = ev_mix
                dstb = {"k": kcb, "v": vvb, "r": ub}[mn]
                for h2 in range(2):
                    qname = ("QA", "QB")[h2]
                    q = QQ[qname]
                    ev_mm = None
                    for dj in range(h2 * 4, h2 * 4 + 4):
                        off = (dj % 4) * W
                        for ci in range(8):
                            ev_mm = S.op("tensor", (lambda mslot=mslot, ci=ci, dj=dj,
                                                    q=q, mpar=mpar, off=off:
                                (lambda t: t.matmul(
                                    q[:, off:off + W],
                                    wbig[:, mslot * 8192 + ci * 1024 + dj * 128:
                                         mslot * 8192 + ci * 1024 + (dj + 1) * 128],
                                    mixbf[mpar][:, ci * W:(ci + 1) * W],
                                    start=(ci == 0), stop=(ci == 7))))(),
                                waits=flat(evmix[ci], last_evac.get(qname),
                                           ev_cast[(mslot, ci)]
                                           if (ti == 0 or (mslot == 2 and ti == NWARM))
                                           else None),
                                inc="pe" if ci == 7 else None)
                    if mn == "k":
                        war = [ev_den1_prev]
                        fn = (lambda h2=h2, q=q, dstb=dstb: (lambda s: s.activation(
                            dstb[:, h2 * 1024:(h2 + 1) * 1024], q[:, :], AF.Exp)))()
                    elif mn == "v":
                        war = [ev_kv_prev]
                        fn = (lambda h2=h2, q=q, dstb=dstb: (lambda s: s.activation(
                            dstb[:, h2 * 1024:(h2 + 1) * 1024], q[:, :], AF.Copy)))()
                    else:
                        war = [ev_attn_last,
                               ev_prologue_cast_end if ti == NWARM else None]
                        fn = (lambda h2=h2, q=q, dstb=dstb: (lambda s: s.activation(
                            dstb[:, h2 * 1024:(h2 + 1) * 1024], q[:, :],
                            AF.Sigmoid)))()
                    eve = S.op("scalar", fn, waits=flat(ev_mm, *war), inc="act")
                    last_evac[qname] = eve
                    proj_ev[(mn, h2)] = eve
                    pmm[mn] = ev_mm
                last_pe_mix[mpar] = ev_mm
            ev_xnfm_rd[par] = ev_max(ev_xnfm_rd[par], ev_mix_last)
            ev_dxn_rd[par] = ev_mix1_last
            proj_evs[ti] = proj_ev
            proj_mm[ti] = pmm
            ev_kcl = S.op("vector", lambda v: v.tensor_scalar(
                kcb[:, :], kcb[:, :], E60, None, ALU.min),
                waits=flat(proj_ev[("k", 0)], proj_ev[("k", 1)]), inc="dve")
            if warm:
                ev_kcl = S.op("vector", lambda v: v.tensor_scalar(
                    kcb[:, :], kcb[:, :], wmk[:, 0:1], None, ALU.mult),
                    waits=flat(ev_smalls, ev_kcl), inc="dve")
            ev_kv = S.op("vector", lambda v: v.tensor_tensor(
                kvb[:, :], kcb[:, :], vvb[:, :], ALU.mult),
                waits=flat(proj_ev[("v", 0)], proj_ev[("v", 1)], ev_kcl,
                           ev_attn_last), inc="dve")
            kv_evs[ti] = ev_kv
            ev_kv_prev = ev_kv
            if ti > 0:
                evb = []
                for buf in (sa, sbb):
                    evb.append(S.op("vector", (lambda buf=buf: (lambda v: v.tensor_copy(
                        v3(buf)[:, :, 0:1], v3(buf)[:, :, W:W + 1])))(), inc="dve"))
                ev_sbnd = evb[-1]
            else:
                ev_sbnd = None
            ev_scanB = {}
            for dj in range(8):
                ev_scanB[dj] = S.op("vector", (lambda dj=dj: (lambda v: v.tensor_tensor_scan(
                    sbb[:, dj * (W + 1) + 1: dj * (W + 1) + 1 + W],
                    dbc(dj), kcb[:, dj * W:(dj + 1) * W],
                    sbb[:, dj * (W + 1): dj * (W + 1) + 1], ALU.mult, ALU.add)))(),
                    waits=flat(ev_kcl, ev_sbnd, e2, ev_bnd0 if ti == 0 else None),
                    inc="dve", hard=(dj == 0))
            for dj in range(8):
                S.op("vector", (lambda dj=dj: (lambda v: v.tensor_tensor_scan(
                    sa[:, dj * (W + 1) + 1: dj * (W + 1) + 1 + W],
                    dbc(dj), kvb[:, dj * W:(dj + 1) * W],
                    sa[:, dj * (W + 1): dj * (W + 1) + 1], ALU.mult, ALU.add)))(),
                    waits=flat(ev_kv), inc="dve")
            scanB_evs[ti] = ev_scanB

        def emit_B2a(ti):
            nonlocal ev_attn_last, ev_den1_prev
            proj_ev = proj_evs[ti]
            for dj in range(8):
                S.op("vector", (lambda dj=dj: (lambda v: v.tensor_scalar(
                    kvb[:, dj * W:(dj + 1) * W], kvb[:, dj * W:(dj + 1) * W],
                    etf[:, dj:dj + 1], None, ALU.mult)))(),
                    waits=flat(ev_etf), inc="dve")
            S.op("vector", lambda v: v.tensor_tensor(
                p3(kvb), p3(kvb), v3(sa)[:, :, 0:W], ALU.add), inc="dve")
            ev_d1 = None
            for dj in range(8):
                ev_d1 = S.op("vector", (lambda dj=dj: (lambda v: v.tensor_scalar(
                    den[:, dj * W:(dj + 1) * W], kcb[:, dj * W:(dj + 1) * W],
                    etf[:, dj:dj + 1], 1e-8, ALU.mult, ALU.add)))(), inc="dve")
            ev_den1_prev = ev_d1
            S.op("vector", lambda v: v.tensor_tensor(
                p3(den), p3(den), v3(sbb)[:, :, 0:W], ALU.add), inc="dve")
            S.op("vector", lambda v: v.reciprocal(den[:, :], den[:, :]), inc="dve")
            S.op("vector", lambda v: v.tensor_tensor(
                kvb[:, :], kvb[:, :], den[:, :], ALU.mult), inc="dve")
            ev_attn = S.op("vector", lambda v: v.tensor_tensor(
                abuf[:, :], kvb[:, :], ub[:, :], ALU.mult),
                waits=flat(proj_ev[("r", 0)], proj_ev[("r", 1)], ev_abuf_rd),
                inc="dve")
            attn_evs[ti] = ev_attn
            ev_attn_last = ev_attn

        def emit_B2b(ti):
            nonlocal ev_s_stored, ev_abuf_rd, ev_wo_last
            mi = ti - NWARM
            ev_attn = attn_evs[ti]
            ev_mm = None
            evst = None
            for h2 in range(2):
                qname = ("QA", "QB")[h2]
                q = QQ[qname]
                for cj in range(h2 * 4, h2 * 4 + 4):
                    off = (cj % 4) * W
                    for dj in range(8):
                        ev_mm = S.op("tensor", (lambda cj=cj, dj=dj, q=q, off=off:
                            (lambda t: t.matmul(
                                q[:, off:off + W],
                                wbig[:, 3 * 8192 + dj * 1024 + cj * 128:
                                     3 * 8192 + dj * 1024 + (cj + 1) * 128],
                                abuf[:, dj * W:(dj + 1) * W],
                                start=(dj == 0), stop=(dj == 7))))(),
                            waits=flat(ev_attn, last_evac.get(qname),
                                       ev_cast[(3, dj)] if ti == NWARM else None),
                            inc="pe" if dj == 7 else None)
                eve = S.op("scalar", (lambda h2=h2, q=q: (lambda s: s.activation(
                    sbf[:, h2 * 1024:(h2 + 1) * 1024], q[:, :], AF.Copy)))(),
                    waits=flat(ev_mm, ev_s_stored,
                               ev_bexpc if ti == NWARM else None), inc="act")
                last_evac[qname] = eve
                evst = S.op("scalar", (lambda mi=mi, h2=h2: (lambda e: e.dma_start(
                    scr_s[h2 * 512:(h2 + 1) * 512, mi * W:(mi + 1) * W]
                    .rearrange("(c p) w -> p c w", p=128),
                    sbf[:, h2 * 1024:(h2 + 1) * 1024]
                    .rearrange("p (c w) -> p c w", w=W))))(),
                    waits=[eve], inc="st_s", k=16)
            ev_abuf_rd = ev_mm
            ev_wo_last = ev_mm
            ev_s_stored = evst

        emit_A_stats(0)
        emit_A_rest(0)
        emit_wcast(0)
        emit_reps()
        emit_A_stats(1)
        emit_wcast(1)
        emit_A_rest(1)
        prev_b2 = None
        for ti in range(ntiles):
            if ti + 2 < ntiles:
                emit_A_stats(ti + 2)
            emit_B1(ti)
            if ti == 0:
                emit_wcast(2)
            if ti == 1:
                emit_wcast(3)
                ev_prologue_cast_end = max(
                    (ev_cast[(m, ci)] for m in range(4) for ci in range(8)),
                    key=lambda e: e[1])
            if (ti + 1) in rt_pend:
                emit_masks(ti + 1)
            if prev_b2 is not None:
                emit_B2b(prev_b2)
                prev_b2 = None
            if ti + 2 < ntiles:
                emit_A_rest(ti + 2)
            if ti >= NWARM:
                emit_B2a(ti)
                prev_b2 = ti
        if (ntiles - 1) in rt_pend:
            emit_masks(ntiles - 1)
        if prev_b2 is not None:
            emit_B2b(prev_b2)

        barrier = ev_max(ev_s_stored, ev_xn_stored)
        ev_sload_free = {}
        ev_xnld_free = [None, None]
        ev_sg_prev = {}
        ev_out_prev = {}
        ev_slot_init = ev_xb_free
        ev_mm = None
        bias_bank_read = [last_evac.get("QS"),
                          ev_max(last_evac.get("QS"), rtt_read[0], rtt_read[1],
                                 rtt_read[2], rtt_read[3])]
        tp_bank_read = [last_evac.get("QT"), last_evac.get("QT")]
        maskT_war = [None, None]
        bias_sb_free = [None, None, None, None]
        bias_tile = {}
        babuf_free = {}
        ev_silu_prev = {}
        EXPQ = [("QA", 0), ("QA", 512), ("QB", 0), ("QB", 512)]
        last_exp_bank_read = {("QA", 0): last_evac.get("QA"),
                              ("QA", 512): last_evac.get("QA"),
                              ("QB", 0): last_evac.get("QB"),
                              ("QB", 512): last_evac.get("QB")}
        ev_exp_free = {}

        def emit_bias_prologue(half):
            nonlocal bias_sb_free
            for j in range(nmain):
                for sub in range(NSUB):
                    msl = (j * NSUB + sub) * 8
                    bi2 = (j * 2 + sub) % 2
                    slot = (j * 2 + sub) % 4
                    ev_mt = S.op("tensor", (lambda msl=msl, bi2=bi2: (lambda t: t.transpose(
                        QT[0:8, bi2 * 128:(bi2 + 1) * 128], masks[:, msl:msl + 8],
                        eye[:, :])))(),
                        waits=flat(tp_bank_read[bi2], ev_masks[(j, sub)]), inc="pe")
                    ev_mte = S.op("scalar", (lambda bi2=bi2: (lambda s: s.activation(
                        maskTb[bi2][:, :], QT[0:8, bi2 * 128:(bi2 + 1) * 128],
                        AF.Copy)))(),
                        waits=flat(ev_mt, maskT_war[bi2]), inc="act")
                    tp_bank_read[bi2] = ev_mte
                    ev_bm = S.op("tensor", (lambda half=half, bi2=bi2: (lambda t: t.matmul(
                        QS[:, bi2 * 512:(bi2 + 1) * 512], maskTb[bi2][0:8, :],
                        sbexp_bf[0:8, half * 512:(half + 1) * 512],
                        start=True, stop=True)))(),
                        waits=flat(ev_mte, bias_bank_read[bi2], ev_bexpc), inc="pe")
                    maskT_war[bi2] = ev_bm
                    ev_be = S.op("scalar", (lambda slot=slot, bi2=bi2: (lambda s: s.activation(
                        sbf[:, slot * 512:(slot + 1) * 512],
                        QS[:, bi2 * 512:(bi2 + 1) * 512], AF.Copy)))(),
                        waits=flat(ev_bm, bias_sb_free[slot],
                                   ev_s_stored if (half == 0 and j == 0) else None),
                        inc="act")
                    bias_bank_read[bi2] = ev_be
                    bias_sb_free[slot] = S.op("scalar", (lambda j=j, sub=sub, half=half,
                                                         slot=slot: (lambda q: q.dma_start(
                        scr_b[j * W + sub * 128: j * W + (sub + 1) * 128,
                              half * 512:(half + 1) * 512],
                        sbf[:, slot * 512:(slot + 1) * 512])))(),
                        waits=[ev_be], inc=f"sb{slot}", k=16)
                    bias_tile[(half, j, sub)] = bias_sb_free[slot]

        _stage_bufs = [b.bitcast(BF16) for b in (sa, sbb, xnfm2[0], xnfm2[1])]
        _dxn_bf = [dxn2[0], dxn2[1]]
        NSTAGE = 5

        def stage_ap(e, ci):
            if e < 4:
                return _stage_bufs[e][:, ci * 512:(ci + 1) * 512]
            return _dxn_bf[ci // 4][:, (ci % 4) * 512:(ci % 4 + 1) * 512]
        ev_stage = {}

        for half in range(2):
            ev_ecast = {}
            ev_slotf = [ev_slot_init] * 4
            elist = range(8) if half == 0 else range(NSTAGE, 8)
            for e in elist:
                for ci in range(8):
                    si = (e * 8 + ci) % 4
                    evl = S.op("sync", (lambda e_=e, ci=ci, si=si, half=half: (lambda q: q.dma_start(
                        xb[:, si * 512:(si + 1) * 512],
                        expT[e_ * C + ci * 128: e_ * C + (ci + 1) * 128,
                             half * 512:(half + 1) * 512])))(),
                        waits=flat(ev_slotf[si]), inc=f"sl{si}", k=16)
                    if half == 0:
                        reg = ((e * 8 + ci) * 512) // 8192
                        regev = (proj_mm[ntiles - 1]["k"], proj_mm[ntiles - 1]["v"],
                                 proj_mm[ntiles - 1]["r"], ev_wo_last)[reg]
                    else:
                        regev = ev_exp_free.get(e)
                    evc = S.op("vector", (lambda e_=e, ci=ci, si=si: (lambda v: v.tensor_copy(
                        wbig[:, (e_ * 8 + ci) * 512:(e_ * 8 + ci + 1) * 512],
                        xb[:, si * 512:(si + 1) * 512])))(),
                        waits=flat(evl, regev), inc="dve")
                    ev_ecast[(e, ci)] = evc
                    ev_slotf[si] = evc
            if half == 0:
                ev_ecast_e = [max((ev_ecast[(e, ci)] for ci in range(8)),
                               key=lambda x: x[1]) for e in range(8)]
                emit_bias_prologue(0)
                emit_bias_prologue(1)
                for e in range(NSTAGE):
                    for ci in range(8):
                        si = (e * 8 + ci) % 4
                        evl = S.op("sync", (lambda e_=e, ci=ci, si=si: (lambda q: q.dma_start(
                            xb[:, si * 512:(si + 1) * 512],
                            expT[e_ * C + ci * 128: e_ * C + (ci + 1) * 128,
                                 512:1024])))(),
                            waits=flat(ev_slotf[si]), inc=f"sl{si}", k=16)
                        evc = S.op("scalar", (lambda e_=e, ci=ci, si=si: (lambda s: s.activation(
                            stage_ap(e_, ci),
                            xb[:, si * 512:(si + 1) * 512], AF.Copy)))(),
                            waits=[evl], inc="act")
                        ev_stage[(e, ci)] = evc
                        ev_slotf[si] = evc
                ev_slot_init = ev_slotf[3]
            else:
                ev_ecast_e = [max((ev_stage[(e, ci)] for ci in range(8)),
                               key=lambda x: x[1]) for e in range(NSTAGE)]
                ev_ecast_e += [max((ev_ecast[(e, ci)] for ci in range(8)),
                                key=lambda x: x[1]) for e in range(NSTAGE, 8)]
            ev_bias_all = ev_max(*[bias_sb_free[s] for s in range(4)])
            babuf_free = {0: ev_abuf_rd, 1: ev_abuf_rd} if half == 0 else babuf_free

            def emit_loads(j):
                jp = j % 2
                jo = jp * 1024
                ev_xnld = S.op("pool", (lambda j=j, jp=jp: (lambda q: q.dma_start(
                    xnbf2[jp][:, :], scr_xn[j * 128:(j + 1) * 128, :])))(),
                    waits=flat(barrier if (half == 0 and j <= 1) else None,
                               ev_xnld_free[jp]), inc=f"ld_xn{jp}", k=16)
                ev_sld = []
                for sub in range(NSUB):
                    ev_sld.append(S.op("scalar", (lambda j=j, sub=sub, half=half, jo=jo:
                        (lambda q: q.dma_start(
                            mixbf[0][:, jo + sub * 512: jo + (sub + 1) * 512],
                            scr_s[half * 512:(half + 1) * 512,
                                  j * W + sub * 128: j * W + (sub + 1) * 128],
                            transpose=True)))(),
                        waits=flat(barrier if (half == 0 and j == 0) else None,
                                   ev_sload_free.get(jp)),
                        inc=f"ld_s{jp}", k=16))
                ev_xld = []
                for sub in range(NSUB):
                    ev_xld.append(S.op("pool", (lambda j=j, sub=sub, half=half, jo=jo:
                        (lambda q: q.dma_start(
                            kvb[:, jo + sub * 512: jo + (sub + 1) * 512],
                            xs[NWARM * W + j * W + sub * 128:
                               NWARM * W + j * W + (sub + 1) * 128,
                               half * 512:(half + 1) * 512])))(),
                        waits=flat(barrier if (half == 0 and j == 0) else None,
                                   ev_sload_free.get(jp)),
                        inc=f"ld_xr{jp}", k=16))
                ev_bld = []
                for sub in range(NSUB):
                    ev_bld.append(S.op("pool", (lambda j=j, sub=sub, half=half, jo=jo:
                        (lambda q: q.dma_start(
                            abuf[:, jo + sub * 512: jo + (sub + 1) * 512],
                            scr_b[j * W + sub * 128: j * W + (sub + 1) * 128,
                                  half * 512:(half + 1) * 512])))(),
                        waits=flat(bias_tile[(half, j, sub)],
                                   babuf_free.get(jp)),
                        inc=f"ld_b{jp}", k=16))
                return ev_xnld, ev_sld, ev_xld, ev_bld

            pend_loads = emit_loads(0)
            for j in range(nmain):
                jp = j % 2
                jo = jp * 1024
                ev_xnld, ev_sld, ev_xld, ev_bld = pend_loads
                if j + 1 < nmain:
                    pend_loads = emit_loads(j + 1)

                for sub in range(NSUB):
                    msl = (j * NSUB + sub) * 8
                    ev_acc = None
                    eorder = ([5, 6, 7, 0, 1, 2, 3, 4]
                              if (half == 0 and j == nmain - 1) else list(range(8)))
                    for idx_e, e in enumerate(eorder):
                        qname, qoff = EXPQ[e % 4]
                        q = QQ[qname]
                        if half == 1 and e < NSTAGE:
                            wsrc_e = (lambda e_=e: (lambda ci: stage_ap(e_, ci)))()
                        else:
                            wsrc_e = (lambda e_=e: (lambda ci:
                                wbig[:, (e_ * 8 + ci) * 512:(e_ * 8 + ci + 1) * 512]))()
                        ev_mm = None
                        for ci in range(8):
                            ev_mm = S.op("tensor", (lambda ci=ci, sub=sub, q=q,
                                                    qoff=qoff, jp=jp, wsrc_e=wsrc_e:
                                (lambda t: t.matmul(
                                    q[:, qoff:qoff + 512],
                                    xnbf2[jp][:, ci * W + sub * 128: ci * W + (sub + 1) * 128],
                                    wsrc_e(ci),
                                    start=(ci == 0), stop=(ci == 7))))(),
                                waits=flat(ev_xnld, ev_ecast_e[e] if j == 0 else None,
                                           last_exp_bank_read.get((qname, qoff))),
                                inc="pe" if ci == 7 else None)
                        if half == 0 and j == nmain - 1 and sub == NSUB - 1:
                            ev_exp_free[e] = ev_mm
                        src_in1 = ((lambda jo=jo, sub=sub: abuf[:, jo + sub * 512:
                                                               jo + (sub + 1) * 512])()
                                   if idx_e == 0 else
                                   (lambda jo=jo, sub=sub: kcb[:, jo + sub * 512:
                                                               jo + (sub + 1) * 512])())
                        ev_acc = S.op("vector", (lambda e_=e, sub=sub, q=q, qoff=qoff,
                                                 msl=msl, jo=jo, src_in1=src_in1:
                            (lambda v: v.scalar_tensor_tensor(
                                kcb[:, jo + sub * 512: jo + (sub + 1) * 512],
                                q[:, qoff:qoff + 512],
                                masks[:, msl + e_: msl + e_ + 1], src_in1,
                                ALU.mult, ALU.add)))(),
                            waits=flat(ev_mm, ev_bld[-1] if idx_e == 0 else None,
                                       ev_silu_prev.get((jp, sub)) if idx_e == 0 else None,
                                       ev_acc), inc="dve")
                        last_exp_bank_read[(qname, qoff)] = ev_acc
                        if idx_e == 7:
                            babuf_free[jp] = ev_acc
                    if debug:
                        S.op("scalar", (lambda j=j, sub=sub, half=half, jo=jo:
                            (lambda q: q.dma_start(
                                scr_g[j * W + sub * 128: j * W + (sub + 1) * 128,
                                      half * 512:(half + 1) * 512],
                                kcb[:, jo + sub * 512: jo + (sub + 1) * 512])))(),
                            waits=[ev_acc], inc="st_g", k=16)
                    ev_silu = S.op("scalar", (lambda sub=sub, jo=jo: (lambda s: s.activation(
                        t1b[:, jo + sub * 512: jo + (sub + 1) * 512],
                        kcb[:, jo + sub * 512: jo + (sub + 1) * 512],
                        AF.Silu)))(), waits=flat(ev_acc, ev_sg_prev.get((jp, sub))), inc="act")
                    ev_silu_prev[(jp, sub)] = ev_silu
                    ev_sg = S.op("pool", (lambda sub=sub, jo=jo: (lambda g: g.tensor_tensor(
                        ub[:, jo + sub * 512: jo + (sub + 1) * 512],
                        t1b[:, jo + sub * 512: jo + (sub + 1) * 512],
                        mixbf[0][:, jo + sub * 512: jo + (sub + 1) * 512], ALU.mult)))(),
                        waits=flat(ev_silu, ev_sld[-1]), inc="pool")
                    ev_sg_prev[(jp, sub)] = ev_sg
                    ev_o = S.op("pool", (lambda sub=sub, jo=jo: (lambda g: g.tensor_tensor(
                        den[:, jo + sub * 512: jo + (sub + 1) * 512],
                        ub[:, jo + sub * 512: jo + (sub + 1) * 512],
                        kvb[:, jo + sub * 512: jo + (sub + 1) * 512], ALU.add)))(),
                        waits=flat(ev_xld[-1], ev_out_prev.get(jp)), inc="pool")
                    ev_st = S.op("scalar", (lambda j=j, sub=sub, half=half, jo=jo:
                        (lambda q: q.dma_start(
                            out[j * W + sub * 128: j * W + (sub + 1) * 128,
                                half * 512:(half + 1) * 512],
                            den[:, jo + sub * 512: jo + (sub + 1) * 512])))(),
                        waits=[ev_o], inc=f"st_o{jp}", k=16)
                    ev_out_prev[jp] = ev_st
                ev_sload_free[jp] = ev_o
                ev_xnld_free[jp] = ev_mm

        if debug:
            S.op("sync", lambda q: q.dma_start(scr_mask, masks[:, :]),
                 waits=flat(*[ev_masks[k2] for k2 in ev_masks]), inc="st_g", k=16)
        S.op("sync", lambda q: q.nop(),
             waits=[(n, S.cnt[n]) for n in
                    ("st_s", "st_xn0", "st_xn1", "st_o0", "st_o1", "st_g")
                    if S.cnt.get(n)])

        with nc.Block() as block:
            @block.sync
            def _(e):
                S.replay("sync", e, sems)

            @block.vector
            def _(e):
                S.replay("vector", e, sems)

            @block.scalar
            def _(e):
                S.replay("scalar", e, sems)

            @block.tensor
            def _(e):
                S.replay("tensor", e, sems)

            @block.gpsimd
            def _(e):
                S.replay("pool", e, sems)

    return nc


_CACHE = {}


def _get_program(nmain):
    if nmain not in _CACHE:
        _CACHE[nmain] = build_program(nmain)
    return _CACHE[nmain]


def kernel(L0_attn_xq, L0_attn_xk, L0_attn_xv, L0_attn_time_decay, L0_attn_time_first,
           L0_attn_time_mix_k, L0_attn_time_mix_v, L0_attn_time_mix_r,
           L0_attn_Wk, L0_attn_Wv, L0_attn_Wr, L0_attn_Wo,
           L0_ffn_w_router, L0_ffn_b_router, L0_ffn_w_experts, L0_ffn_b_experts):
    x = np.ascontiguousarray(np.asarray(L0_attn_xq, dtype=np.float32))
    B, T, Cc = x.shape
    assert (B, T, Cc) == (4, 8192, 1024)
    nmain = 16
    TOK = W * nmain
    NROW = W * (NWARM + nmain)

    def chunked(a):
        a = np.asarray(a, np.float32).reshape(-1)
        return np.ascontiguousarray(a.reshape(8, 128).T)

    common = {
        "wkT": np.ascontiguousarray(np.asarray(L0_attn_Wk, np.float32).T),
        "wvT": np.ascontiguousarray(np.asarray(L0_attn_Wv, np.float32).T),
        "wrT": np.ascontiguousarray(np.asarray(L0_attn_Wr, np.float32).T),
        "woT": np.ascontiguousarray(np.asarray(L0_attn_Wo, np.float32).T),
        "expT": np.ascontiguousarray(
            np.asarray(L0_ffn_w_experts, np.float32).transpose(0, 2, 1).reshape(8 * C, C)),
        "tdp": chunked(L0_attn_time_decay),
        "tfp": chunked(L0_attn_time_first),
        "tmixp": np.ascontiguousarray(np.concatenate(
            [chunked(L0_attn_time_mix_k), chunked(L0_attn_time_mix_v),
             chunked(L0_attn_time_mix_r)], axis=1)),
        "wroutT": np.ascontiguousarray(np.asarray(L0_ffn_w_router, np.float32).T),
        "brout": np.ascontiguousarray(np.broadcast_to(
            np.asarray(L0_ffn_b_router, np.float32).reshape(1, 8), (128, 8))),
        "bexp": np.ascontiguousarray(np.asarray(L0_ffn_b_experts, np.float32)),
        "eye_in": np.eye(128, dtype=np.float32),
    }

    in_maps = []
    for core in range(8):
        b, h = core // 2, core % 2
        t0 = h * TOK
        xsv = np.zeros((NROW, C), np.float32)
        lo = max(0, t0 - NWARM * W)
        xsv[NWARM * W - (t0 - lo):] = x[b, lo:t0 + TOK]
        m = dict(common)
        m["xs"] = xsv
        m["wmask_in"] = np.full((128, 1), 1.0 if h == 1 else 0.0, np.float32)
        in_maps.append(m)

    nc = _get_program(nmain)
    res = run_bass_kernel_spmd(nc, in_maps, core_ids=list(range(8)))
    outf = np.empty((B, T, C), np.float32)
    for core in range(8):
        b, h = core // 2, core % 2
        outf[b, h * TOK:(h + 1) * TOK] = np.asarray(res.results[core]["out"])
    return outf


# version 39
# speedup vs baseline: 1.2222x; 1.0475x over previous
import numpy as np
import ml_dtypes
from contextlib import ExitStack

import concourse.bass as bass
import concourse.mybir as mybir
from concourse.bass_utils import run_bass_kernel_spmd

F32 = mybir.dt.float32
BF16 = mybir.dt.bfloat16
AF = mybir.ActivationFunctionType
ALU = mybir.AluOpType
AX = mybir.AxisListType.X

C = 1024
NCH = 8
W = 256
NSUB = W // 128
NWARM = 2
E60 = float(np.float32(np.exp(np.float64(60.0))))


PARANOID = False
SKIP_OWN = True


class Sched:

    def __init__(self):
        self.items = []
        self.cnt = {}

    AUTO = {"vector": "dve", "scalar": "act", "pool": "pool"}

    def op(self, eng, fn, waits=(), inc=None, k=1, hard=False):
        ws = []
        if PARANOID:
            ws.extend((s, v) for s, v in self.cnt.items() if v > 0)
        if inc is None:
            inc = self.AUTO.get(eng)
        incs = []
        ev = None
        if inc is not None:
            self.cnt[inc] = self.cnt.get(inc, 0) + k
            ev = (inc, self.cnt[inc])
            incs.append((inc, k))
        for w in waits:
            if w is None:
                continue
            if isinstance(w, list):
                ws.extend(w)
            else:
                ws.append(w)
        self.items.append((eng, fn, tuple(ws), incs, k == 16 or hard))
        return ev

    def replay(self, engname, engobj, sems):
        lastwait = {}
        own = self.AUTO.get(engname)
        owncnt = 0
        skip_own = engname == "vector" and SKIP_OWN
        for eng, fn, waits, incs, is_dma in self.items:
            if eng != engname:
                continue
            allw = list(waits)
            if own is not None and owncnt > 0 and not (skip_own and not is_dma):
                allw.append((own, owncnt))
            for (s, v) in allw:
                if s == own and skip_own and not is_dma:
                    continue
                if lastwait.get(s, 0) < v:
                    engobj.wait_ge(sems[s], v)
                    lastwait[s] = v
            inst = fn(engobj)
            for (s, k) in incs:
                inst.then_inc(sems[s], k)
                if s == own:
                    owncnt += k


def flat(*evs):
    r = []
    for e in evs:
        if e is None:
            continue
        if isinstance(e, list):
            r.extend(e)
        else:
            r.append(e)
    return r


def ev_max(*evs):
    bysem = {}
    for e in flat(*evs):
        s, v = e
        bysem[s] = max(bysem.get(s, 0), v)
    return [(s, v) for s, v in bysem.items()] or None


def build_program(nmain, debug=False):
    ntiles = NWARM + nmain
    TOK = W * nmain
    NROW = W * ntiles
    scratch_kind = "ExternalOutput" if debug else "Internal"

    nc = bass.Bass("TRN2", target_bir_lowering=False)

    xs = nc.dram_tensor("xs", [NROW, C], F32, kind="ExternalInput").ap()
    wkT = nc.dram_tensor("wkT", [C, C], F32, kind="ExternalInput").ap()
    wvT = nc.dram_tensor("wvT", [C, C], F32, kind="ExternalInput").ap()
    wrT = nc.dram_tensor("wrT", [C, C], F32, kind="ExternalInput").ap()
    woT = nc.dram_tensor("woT", [C, C], F32, kind="ExternalInput").ap()
    expT = nc.dram_tensor("expT", [8 * C, C], F32, kind="ExternalInput").ap()
    tdp = nc.dram_tensor("tdp", [128, 8], F32, kind="ExternalInput").ap()
    tfp = nc.dram_tensor("tfp", [128, 8], F32, kind="ExternalInput").ap()
    tmixp = nc.dram_tensor("tmixp", [128, 24], F32, kind="ExternalInput").ap()
    wroutT = nc.dram_tensor("wroutT", [C, 8], F32, kind="ExternalInput").ap()
    brout = nc.dram_tensor("brout", [128, 8], F32, kind="ExternalInput").ap()
    bexp = nc.dram_tensor("bexp", [8, C], F32, kind="ExternalInput").ap()
    eye_in = nc.dram_tensor("eye_in", [128, 128], F32, kind="ExternalInput").ap()
    wmask_in = nc.dram_tensor("wmask_in", [128, 1], F32, kind="ExternalInput").ap()
    out = nc.dram_tensor("out", [TOK, C], F32, kind="ExternalOutput").ap()

    scr_s = nc.dram_tensor("scr_s", [C, TOK], BF16, kind=scratch_kind).ap()
    scr_xn = nc.dram_tensor("scr_xn", [nmain * 128, NCH * W], BF16, kind=scratch_kind).ap()
    scr_b = nc.dram_tensor("scr_b", [TOK, C], BF16, kind="Internal").ap()
    if debug:
        scr_g = nc.dram_tensor("scr_g", [TOK, C], F32, kind="ExternalOutput").ap()
        scr_mask = nc.dram_tensor("scr_mask", [128, nmain * NSUB * 8], F32,
                                  kind="ExternalOutput").ap()

    S = Sched()
    ctx = ExitStack()
    with ctx:
        def sbuf(name, shape, dt=F32):
            return ctx.enter_context(nc.sbuf_tensor(name, shape, dt))

        wbig = sbuf("wbig", [128, 32768], BF16)
        xb = sbuf("xb", [128, 2048])
        xnfm2 = [sbuf("xnfm0", [128, NCH * (W + 1)]), sbuf("xnfm1", [128, NCH * (W + 1)])]
        dxn2 = [sbuf("dxn0", [128, 2048], BF16), sbuf("dxn1", [128, 2048], BF16)]
        tmrep = sbuf("tmrep", [128, 3 * 2048], BF16)
        mixbf = [sbuf("mixbf0", [128, 2048], BF16), sbuf("mixbf1", [128, 2048], BF16)]
        abuf = sbuf("abuf", [128, 2048], BF16)
        xnbf_s = sbuf("xnbf0", [128, 2048], BF16)
        xnbf2 = [xnbf_s, mixbf[1]]
        kcb = sbuf("kcb", [128, 2048])
        vvb = sbuf("vvb", [128, 2048])
        kvb = sbuf("kvb", [128, 2048])
        den = sbuf("den", [128, 2048])
        t1b = sbuf("t1b", [128, 2048])
        ub = sbuf("ub", [128, 2048])
        sa = sbuf("sa", [128, NCH * (W + 1)])
        sbb = sbuf("sbb", [128, NCH * (W + 1)])
        sbf = sbuf("sbf", [128, 2048], BF16)
        masks = sbuf("masks", [128, nmain * NSUB * 8])
        stat = sbuf("stat", [128, 20])
        dd = sbuf("dd", [128, 8])
        etf = sbuf("etf", [128, 8])
        tmix = sbuf("tmix", [128, 24])
        td_sb = sbuf("td_sb", [128, 8])
        tf_sb = sbuf("tf_sb", [128, 8])
        wrt_sb = sbuf("wrt_sb", [128, 64])
        sbrep = sbuf("sbrep", [128, 8])
        rtev = sbuf("rtev", [8, 256])
        eye = sbuf("eye", [128, 128])
        wmk = sbuf("wmk", [128, 1])
        maskTb = [sbuf("maskT0", [8, 128], BF16), sbuf("maskT1", [8, 128], BF16)]
        sbexp_bf = sbuf("sbexp_bf", [8, C], BF16)
        lgt = sbuf("lgt", [128, 8])
        lm1 = sbuf("lm1", [128, 1])
        lm2 = sbuf("lm2", [128, 1])
        leq = sbuf("leq", [128, 8])
        consts = sbuf("consts", [128, 8])

        QA = ctx.enter_context(nc.psum_tensor("QA", [128, 1024], F32))
        QB = ctx.enter_context(nc.psum_tensor("QB", [128, 1024], F32))
        QT = ctx.enter_context(nc.psum_tensor("QT", [128, 512], F32))
        QR = ctx.enter_context(nc.psum_tensor("QR", [128, 512], F32))
        QS = ctx.enter_context(nc.psum_tensor("QS", [128, 1024], F32))
        QQ = {"QA": QA, "QB": QB, "QT": QT, "QR": QR, "QS": QS}

        CONSTS = [-1.0, -0.5, 1e-5, 1e-8, 1.0, 0.0, 0.5, 1.0 / 1024.0]
        for i, cval in enumerate(CONSTS):
            nc.const_aps.register(F32, cval, consts[:, i:i + 1])

        semnames = ["ld_sm", "ld_x", "ld_xn0", "ld_xn1", "ld_s0", "ld_s1", "ld_xr0",
                    "ld_xr1", "ld_b0", "ld_b1", "sl0", "sl1", "sl2", "sl3", "st_s",
                    "st_xn0", "st_xn1", "st_o0", "st_o1", "st_g", "sb0", "sb1", "sb2",
                    "sb3", "pe", "dve", "act", "pool"]
        sems = {n: ctx.enter_context(nc.semaphore(n)) for n in semnames}

        def v3(buf):
            return buf[:, :].rearrange("p (c w) -> p c w", w=W + 1)

        def p3(buf):
            return buf[:, :].rearrange("p (c w) -> p c w", w=W)

        def bcast8(buf, off, rowlen):
            return bass.AP(buf, off, [[rowlen, 128], [1, 8], [0, W]])

        for i, cval in enumerate(CONSTS):
            S.op("vector", (lambda i=i, cval=cval: (lambda v: v.memset(consts[:, i:i + 1], cval)))())
        ev_bnd0 = None
        for buf in (xnfm2[0], xnfm2[1], sa, sbb):
            ev_bnd0 = S.op("vector", (lambda buf=buf: (lambda v: v.memset(v3(buf)[:, :, 0:1], 0.0)))())
        ev_consts = S.op("vector", lambda v: v.memset(lgt[:, :], 0.0), inc="dve")

        evs_small = []
        evs_small.append(S.op("sync", lambda e: e.dma_start(td_sb[:, :], tdp), inc="ld_sm", k=16))
        evs_small.append(S.op("sync", lambda e: e.dma_start(tf_sb[:, :], tfp), inc="ld_sm", k=16))
        evs_small.append(S.op("sync", lambda e: e.dma_start(tmix[:, :], tmixp), inc="ld_sm", k=16))
        for ci in range(8):
            evs_small.append(S.op("sync", (lambda ci=ci: (lambda e: e.dma_start(
                wrt_sb[:, ci * 8:(ci + 1) * 8], wroutT[ci * 128:(ci + 1) * 128, :])))(),
                inc="ld_sm", k=16))
        sbexp = sbf.bitcast(F32)
        evs_small.append(S.op("sync", lambda e: e.dma_start(sbexp[0:8, 0:C], bexp), inc="ld_sm", k=16))
        evs_small.append(S.op("sync", lambda e: e.dma_start(sbrep[:, :], brout), inc="ld_sm", k=16))
        evs_small.append(S.op("sync", lambda e: e.dma_start(eye[:, :], eye_in), inc="ld_sm", k=16))
        evs_small.append(S.op("sync", lambda e: e.dma_start(wmk[:, :], wmask_in), inc="ld_sm", k=16))
        ev_smalls = evs_small[-1]

        ev_bexpc = S.op("vector", lambda v: v.tensor_copy(sbexp_bf[:, :], sbexp[0:8, 0:C]),
                        waits=[ev_smalls], inc="dve")

        e1 = S.op("scalar", lambda s: s.activation(leq[:, :], td_sb[:, :], AF.Exp),
                  waits=[ev_smalls, ev_consts], inc="act")
        e2 = S.op("scalar", lambda s: s.activation(dd[:, :], leq[:, :], AF.Exp, scale=-1.0),
                  waits=[e1], inc="act")
        ev_etf = S.op("scalar", lambda s: s.activation(etf[:, :], tf_sb[:, :], AF.Exp),
                      waits=[e2], inc="act")
        def emit_reps():
            for m in range(3):
                S.op("vector", (lambda m=m: (lambda v: v.tensor_copy(
                    tmrep[:, m * 2048:(m + 1) * 2048]
                    .rearrange("p (c w) -> p c w", w=W),
                    bcast8(tmix, m * 8, 24))))(),
                    waits=[ev_smalls], inc="dve")

        ev_brep = ev_smalls

        wsrc = [wkT, wvT, wrT, woT]
        slots = [(ub, 0), (ub, 1024), (t1b, 0), (t1b, 1024)]
        ev_cast = {}
        ev_slot_free = [None] * 4
        ev_prologue_cast_end = None

        def emit_wcast(m):
            for ci in range(8):
                si = (m * 8 + ci) % 4
                buf, off = slots[si]
                evl = S.op("sync", (lambda m=m, ci=ci, buf=buf, off=off: (lambda e: e.dma_start(
                    buf[:, off:off + 1024], wsrc[m][ci * 128:(ci + 1) * 128, :])))(),
                    waits=[ev_slot_free[si]], inc=f"sl{si}", k=16)
                evc = S.op("scalar", (lambda m=m, ci=ci, buf=buf, off=off: (lambda s: s.activation(
                    wbig[:, m * 8192 + ci * 1024: m * 8192 + (ci + 1) * 1024],
                    buf[:, off:off + 1024], AF.Copy)))(), waits=[evl], inc="act")
                ev_cast[(m, ci)] = evc
                ev_slot_free[si] = evc

        last_evac = {}
        last_pe_mix = [None, None]
        ev_xb_free = None
        ev_s_stored = None
        ev_xn_stored = None
        ev_xn_stored2 = [None, None]
        ev_masks = {}
        ev_stats_chain = None
        ev_statrd_prev = [None, None]
        ev_xnfm_rd = [None, None]
        ev_dxn_rd = [None, None]
        ev_xnfm_all = [None, None]
        ev_grp_evac2 = [None, None]
        ev_bnd_ev = [None, None]
        ev_dxn_ev = [None, None]
        ev_den1_prev = None
        ev_rcp_prev = None
        ev_kv_prev = None
        proj_evs = {}
        proj_mm = {}
        attn_evs = {}
        kv_evs = {}
        scanB_evs = {}
        ev_abuf_rd = None
        ev_attn_last = None
        ev_wo_last = None

        def dbc(dj):
            return bass.AP(dd, dj, [[8, 128], [0, W]])

        rt_pend = {}
        rtr_read = {0: ev_brep, 1: ev_brep}
        rtt_read = {0: ev_brep, 1: ev_brep, 2: ev_brep, 3: ev_brep}
        ev_rtev_rd = None

        def emit_masks(ti):
            nonlocal ev_rtev_rd
            mi = ti - NWARM
            tpar = ti % 2
            qoff = tpar * 256
            evr = rt_pend.pop(ti)
            ev_ev = S.op("scalar", (lambda qoff=qoff: (lambda s: s.activation(
                rtev[0:8, :], QR[0:8, qoff:qoff + 256], AF.Copy)))(),
                waits=flat(evr, ev_rtev_rd), inc="act")
            rtr_read[tpar] = ev_ev
            for sub in range(NSUB):
                s_ = tpar * 2 + sub
                ev_tp = S.op("tensor", (lambda sub=sub, s_=s_: (lambda t: t.transpose(
                    QS[:, 512 + s_ * 8: 512 + s_ * 8 + 8],
                    rtev[0:8, sub * 128:(sub + 1) * 128], eye[0:8, 0:8])))(),
                    waits=flat(ev_ev, rtt_read[s_]), inc="pe")
                ev_rtev_rd = ev_tp
                ev_lg = S.op("vector", (lambda s_=s_: (lambda v: v.tensor_tensor(
                    lgt[:, :], QS[:, 512 + s_ * 8: 512 + s_ * 8 + 8],
                    sbrep[:, 0:8], ALU.add)))(),
                    waits=flat(ev_tp, ev_brep), inc="dve", hard=True)
                rtt_read[s_] = ev_lg
                evm1 = S.op("vector", lambda v: v.reduce_max(lm1[:, :], lgt[:, :], axis=AX),
                            waits=[ev_lg], inc="dve", hard=True)
                evm1b = S.op("vector", lambda v: v.tensor_scalar(
                    leq[:, :], lgt[:, :], lm1[:, 0:1], None, ALU.is_equal),
                    waits=[evm1], inc="dve", hard=True)
                evl2 = S.op("vector", lambda v: v.scalar_tensor_tensor(
                    leq[:, :], leq[:, :], -1e30, lgt[:, :], ALU.mult, ALU.add),
                    waits=[evm1b], inc="dve", hard=True)
                evm2 = S.op("vector", lambda v: v.reduce_max(lm2[:, :], leq[:, :], axis=AX),
                            waits=[evl2], inc="dve", hard=True)
                ev_masks[(mi, sub)] = S.op("vector", (lambda mi=mi, sub=sub:
                    (lambda v: v.tensor_scalar(
                        masks[:, (mi * NSUB + sub) * 8:(mi * NSUB + sub + 1) * 8],
                        lgt[:, :], lm2[:, 0:1], None, ALU.is_ge)))(),
                    waits=[evm2], inc="dve", hard=True)

        A_state = {}

        def emit_A_stats(ti):
            nonlocal ev_stats_chain
            par = ti % 2
            so = 10 * par
            r0 = ti * W
            evx = []
            for sub in range(NSUB):
                evx.append(S.op("sync", (lambda r0=r0, sub=sub: (lambda e: e.dma_start(
                    xb[:, sub * 1024:(sub + 1) * 1024],
                    xs[r0 + sub * 128: r0 + (sub + 1) * 128, :])))(),
                    waits=flat(ev_xb_free), inc="ld_x", k=16))
            ev_x = evx[-1]
            ev_kv_war = kv_evs.get(ti - 3)
            evq = ev_stats_chain
            for sub in range(NSUB):
                evq = S.op("scalar", (lambda sub=sub, so=so: (lambda s: s.activation(
                    vvb[:, 0:1024], xb[:, sub * 1024:(sub + 1) * 1024],
                    AF.Square, accum_out=stat[:, so + sub:so + sub + 1])))(),
                    waits=flat(ev_x, ev_statrd_prev[par], evq, ev_kv_war), inc="act")
            for sub in range(NSUB):
                evq = S.op("scalar", (lambda sub=sub, so=so: (lambda s: s.activation(
                    vvb[:, 0:1024], xb[:, sub * 1024:(sub + 1) * 1024],
                    AF.Copy, accum_out=stat[:, so + 2 + sub:so + sub + 3])))(),
                    waits=[evq], inc="act")
            ev_stats_acc = evq
            ev_stats_chain = evq
            S.op("scalar", (lambda so=so: (lambda s: s.activation(
                stat[:, so + 4:so + 6], stat[:, so + 2:so + 4], AF.Copy,
                scale=1.0 / 1024.0)))(), waits=[ev_stats_acc], inc="act")
            S.op("scalar", (lambda so=so: (lambda s: s.activation(
                stat[:, so + 8:so + 10], stat[:, so + 4:so + 6], AF.Square)))(), inc="act")
            S.op("scalar", (lambda so=so: (lambda s: s.activation(
                stat[:, so + 6:so + 8], stat[:, so + 8:so + 10], AF.Copy,
                scale=-1.0, bias=1e-5)))(), inc="act")
            for sub in range(NSUB):
                S.op("scalar", (lambda so=so, sub=sub: (lambda s: s.activation(
                    stat[:, so + 8 + sub:so + 9 + sub], stat[:, so + sub:so + sub + 1],
                    AF.Ln, scale=1.0 / 1024.0, bias=stat[:, so + 6 + sub:so + 7 + sub])))(),
                    inc="act")
            ev_rstd = S.op("scalar", (lambda so=so: (lambda s: s.activation(
                stat[:, so + 6:so + 8], stat[:, so + 8:so + 10], AF.Exp, scale=-0.5)))(),
                inc="act")
            A_state[ti] = (ev_x, ev_rstd, ev_stats_acc)

        def emit_A_rest(ti):
            nonlocal ev_xb_free, ev_xn_stored
            par = ti % 2
            warm = ti < NWARM
            mi = ti - NWARM
            xf = xnfm2[par]
            dx = dxn2[par]
            so = 10 * par
            ev_x, ev_rstd, ev_stats_acc = A_state.pop(ti)
            ev_xn_sub = []
            for sub in range(NSUB):
                ev_xn_sub.append(S.op("vector", (lambda sub=sub, so=so: (lambda v: v.tensor_scalar(
                    xb[:, sub * 1024:(sub + 1) * 1024], xb[:, sub * 1024:(sub + 1) * 1024],
                    stat[:, so + 4 + sub:so + 5 + sub], stat[:, so + 6 + sub:so + 7 + sub],
                    ALU.subtract, ALU.mult)))(), waits=flat(ev_rstd), inc="dve"))
            ev_xn = ev_xn_sub[-1]
            ev_statrd_prev[par] = ev_xn
            if ti > 0:
                ev_bnd = S.op("vector", (lambda par=par: (lambda v: v.tensor_copy(
                    v3(xnfm2[par])[:, :, 0:1], v3(xnfm2[1 - par])[:, :, W:W + 1])))(),
                    waits=flat(ev_xnfm_all[1 - par], ev_xnfm_rd[par]), inc="dve")
            else:
                ev_bnd = None
            ev_bnd_ev[par] = ev_bnd
            ev_grp_evac = []
            evt_last = None
            for cpair in range(4):
                qname = "QT" if cpair % 2 == 0 else "QS"
                q = QQ[qname]
                evt = None
                for k in range(2):
                    cik = cpair * 2 + k
                    for sub in range(NSUB):
                        evt = S.op("tensor", (lambda cik=cik, sub=sub, q=q, k=k:
                            (lambda t: t.transpose(
                                q[:, k * 256 + sub * 128: k * 256 + (sub + 1) * 128],
                                xb[:, sub * 1024 + cik * 128: sub * 1024 + (cik + 1) * 128],
                                eye[:, :])))(),
                            waits=flat(ev_xn, last_evac.get(qname),
                                       ev_smalls if ti == 0 else None), inc="pe")
                eve = S.op("scalar", (lambda cpair=cpair, q=q, xf=xf: (lambda s: s.activation(
                    v3(xf)[:, cpair * 2:cpair * 2 + 2, 1:W + 1],
                    q[:, 0:512].rearrange("p (c w) -> p c w", w=W), AF.Copy)))(),
                    waits=flat(evt, ev_xnfm_rd[par]), inc="act")
                ev_grp_evac.append(eve)
                last_evac[qname] = eve
                evt_last = evt
            ev_evac_all = ev_grp_evac[-1]
            ev_xnfm_all[par] = ev_evac_all
            ev_grp_evac2[par] = ev_grp_evac
            ev_xb_free = ev_max(evt_last, ev_stats_acc)
            if not warm:
                ev_cast_xn = S.op("scalar", (lambda xf=xf: (lambda s: s.activation(
                    p3(xnbf_s), v3(xf)[:, :, 1:W + 1], AF.Copy)))(),
                    waits=flat(ev_evac_all, ev_xn_stored), inc="act")
                ev_xn_stored = S.op("scalar", (lambda mi=mi: (lambda e: e.dma_start(
                    scr_xn[mi * 128:(mi + 1) * 128, :], xnbf_s[:, :])))(),
                    waits=[ev_cast_xn], inc="st_xn0", k=16)
            ev_dxn = S.op("pool", (lambda xf=xf, dx=dx: (lambda v: v.tensor_tensor(
                p3(dx), v3(xf)[:, :, 1:W + 1], v3(xf)[:, :, 0:W], ALU.subtract)))(),
                waits=flat(ev_evac_all, ev_bnd, ev_dxn_rd[par]), inc="pool")
            ev_dxn_ev[par] = ev_dxn
            ev_rt_last = None
            if not warm:
                tpar = ti % 2
                qoff = tpar * 256
                evr = None
                for ci in range(8):
                    evr = S.op("tensor", (lambda ci=ci, xf=xf, qoff=qoff:
                        (lambda t: t.matmul(
                            QR[0:8, qoff:qoff + 256],
                            wrt_sb[:, ci * 8:(ci + 1) * 8],
                            xf[:, ci * (W + 1) + 1: ci * (W + 1) + 1 + W],
                            start=(ci == 0), stop=(ci == 7))))(),
                        waits=flat(ev_grp_evac[ci // 2], rtr_read[tpar]),
                        inc="pe" if ci == 7 else None)
                rt_pend[ti] = evr
                ev_rt_last = evr
            ev_xnfm_rd[par] = ev_rt_last

        def emit_B1(ti):
            nonlocal ev_kv_prev
            par = ti % 2
            warm = ti < NWARM
            xf = xnfm2[par]
            dx = dxn2[par]
            mixes = [("k", 0, 0), ("v", 1, 1)] + ([] if warm else [("r", 2, 0)])
            proj_ev = {}
            pmm = {}
            ev_mix_last = None
            ev_mix1_last = None
            for (mn, mslot, mpar) in mixes:
                S.op("vector", (lambda mslot=mslot, mpar=mpar, dx=dx:
                    (lambda v: v.tensor_tensor(
                        p3(mixbf[mpar]), p3(dx),
                        tmrep[:, mslot * 2048:(mslot + 1) * 2048]
                        .rearrange("p (c w) -> p c w", w=W),
                        ALU.mult)))(),
                    waits=flat(ev_dxn_ev[par], last_pe_mix[mpar]), inc="dve")
                ev_mix = S.op("vector", (lambda mpar=mpar, xf=xf:
                    (lambda v: v.tensor_tensor(
                        p3(mixbf[mpar]), p3(mixbf[mpar]), v3(xf)[:, :, 0:W],
                        ALU.add)))(),
                    waits=flat(ev_bnd_ev[par], ev_xnfm_all[par]), inc="dve")
                evmix = [ev_mix] * 8
                ev_mix_last = ev_mix
                ev_mix1_last = ev_mix
                dstb = {"k": kcb, "v": vvb, "r": ub}[mn]
                for h2 in range(2):
                    qname = ("QA", "QB")[h2]
                    q = QQ[qname]
                    ev_mm = None
                    for dj in range(h2 * 4, h2 * 4 + 4):
                        off = (dj % 4) * W
                        for ci in range(8):
                            ev_mm = S.op("tensor", (lambda mslot=mslot, ci=ci, dj=dj,
                                                    q=q, mpar=mpar, off=off:
                                (lambda t: t.matmul(
                                    q[:, off:off + W],
                                    wbig[:, mslot * 8192 + ci * 1024 + dj * 128:
                                         mslot * 8192 + ci * 1024 + (dj + 1) * 128],
                                    mixbf[mpar][:, ci * W:(ci + 1) * W],
                                    start=(ci == 0), stop=(ci == 7))))(),
                                waits=flat(evmix[ci], last_evac.get(qname),
                                           ev_cast[(mslot, ci)]
                                           if (ti == 0 or (mslot == 2 and ti == NWARM))
                                           else None),
                                inc="pe" if ci == 7 else None)
                    if mn == "k":
                        war = [ev_den1_prev]
                        fn = (lambda h2=h2, q=q, dstb=dstb: (lambda s: s.activation(
                            dstb[:, h2 * 1024:(h2 + 1) * 1024], q[:, :], AF.Exp)))()
                    elif mn == "v":
                        war = [ev_kv_prev]
                        fn = (lambda h2=h2, q=q, dstb=dstb: (lambda s: s.activation(
                            dstb[:, h2 * 1024:(h2 + 1) * 1024], q[:, :], AF.Copy)))()
                    else:
                        war = [ev_attn_last,
                               ev_prologue_cast_end if ti == NWARM else None]
                        fn = (lambda h2=h2, q=q, dstb=dstb: (lambda s: s.activation(
                            dstb[:, h2 * 1024:(h2 + 1) * 1024], q[:, :],
                            AF.Sigmoid)))()
                    eve = S.op("scalar", fn, waits=flat(ev_mm, *war), inc="act")
                    last_evac[qname] = eve
                    proj_ev[(mn, h2)] = eve
                    pmm[mn] = ev_mm
                last_pe_mix[mpar] = ev_mm
            ev_xnfm_rd[par] = ev_max(ev_xnfm_rd[par], ev_mix_last)
            ev_dxn_rd[par] = ev_mix1_last
            proj_evs[ti] = proj_ev
            proj_mm[ti] = pmm
            ev_kcl = S.op("vector", lambda v: v.tensor_scalar(
                kcb[:, :], kcb[:, :], E60, None, ALU.min),
                waits=flat(proj_ev[("k", 0)], proj_ev[("k", 1)]), inc="dve")
            if warm:
                ev_kcl = S.op("vector", lambda v: v.tensor_scalar(
                    kcb[:, :], kcb[:, :], wmk[:, 0:1], None, ALU.mult),
                    waits=flat(ev_smalls, ev_kcl), inc="dve")
            ev_kv = S.op("vector", lambda v: v.tensor_tensor(
                kvb[:, :], kcb[:, :], vvb[:, :], ALU.mult),
                waits=flat(proj_ev[("v", 0)], proj_ev[("v", 1)], ev_kcl,
                           ev_attn_last), inc="dve")
            kv_evs[ti] = ev_kv
            ev_kv_prev = ev_kv
            if ti > 0:
                evb = []
                for buf in (sa, sbb):
                    evb.append(S.op("vector", (lambda buf=buf: (lambda v: v.tensor_copy(
                        v3(buf)[:, :, 0:1], v3(buf)[:, :, W:W + 1])))(), inc="dve"))
                ev_sbnd = evb[-1]
            else:
                ev_sbnd = None
            ev_scanB = {}
            for dj in range(8):
                ev_scanB[dj] = S.op("vector", (lambda dj=dj: (lambda v: v.tensor_tensor_scan(
                    sbb[:, dj * (W + 1) + 1: dj * (W + 1) + 1 + W],
                    dbc(dj), kcb[:, dj * W:(dj + 1) * W],
                    sbb[:, dj * (W + 1): dj * (W + 1) + 1], ALU.mult, ALU.add)))(),
                    waits=flat(ev_kcl, ev_sbnd, e2, ev_bnd0 if ti == 0 else None),
                    inc="dve", hard=(dj == 0))
            for dj in range(8):
                S.op("vector", (lambda dj=dj: (lambda v: v.tensor_tensor_scan(
                    sa[:, dj * (W + 1) + 1: dj * (W + 1) + 1 + W],
                    dbc(dj), kvb[:, dj * W:(dj + 1) * W],
                    sa[:, dj * (W + 1): dj * (W + 1) + 1], ALU.mult, ALU.add)))(),
                    waits=flat(ev_kv), inc="dve")
            scanB_evs[ti] = ev_scanB

        def emit_B2a(ti):
            nonlocal ev_attn_last, ev_den1_prev, ev_rcp_prev
            proj_ev = proj_evs[ti]
            for dj in range(8):
                S.op("vector", (lambda dj=dj: (lambda v: v.tensor_scalar(
                    kvb[:, dj * W:(dj + 1) * W], kvb[:, dj * W:(dj + 1) * W],
                    etf[:, dj:dj + 1], None, ALU.mult)))(),
                    waits=flat(ev_etf), inc="dve")
            S.op("vector", lambda v: v.tensor_tensor(
                p3(kvb), p3(kvb), v3(sa)[:, :, 0:W], ALU.add), inc="dve")
            ev_d1 = None
            for dj in range(8):
                ev_d1 = S.op("vector", (lambda dj=dj: (lambda v: v.tensor_scalar(
                    den[:, dj * W:(dj + 1) * W], kcb[:, dj * W:(dj + 1) * W],
                    etf[:, dj:dj + 1], None, ALU.mult)))(),
                    waits=flat(ev_rcp_prev if dj == 0 else None), inc="dve")
            ev_den1_prev = ev_d1
            ev_d2 = S.op("vector", lambda v: v.tensor_tensor(
                p3(den), p3(den), v3(sbb)[:, :, 0:W], ALU.add), inc="dve")
            ev_t1 = S.op("scalar", lambda s: s.activation(
                t1b[:, :], den[:, :], AF.Ln, bias=1e-8),
                waits=flat(ev_d2, ev_prologue_cast_end if ti == NWARM else None),
                inc="act")
            ev_rcp = S.op("scalar", lambda s: s.activation(
                den[:, :], t1b[:, :], AF.Exp, scale=-1.0), inc="act")
            ev_rcp_prev = ev_rcp
            S.op("vector", lambda v: v.tensor_tensor(
                kvb[:, :], kvb[:, :], den[:, :], ALU.mult),
                waits=[ev_rcp], inc="dve")
            ev_attn = S.op("vector", lambda v: v.tensor_tensor(
                abuf[:, :], kvb[:, :], ub[:, :], ALU.mult),
                waits=flat(proj_ev[("r", 0)], proj_ev[("r", 1)], ev_abuf_rd),
                inc="dve")
            attn_evs[ti] = ev_attn
            ev_attn_last = ev_attn

        def emit_B2b(ti):
            nonlocal ev_s_stored, ev_abuf_rd, ev_wo_last
            mi = ti - NWARM
            ev_attn = attn_evs[ti]
            ev_mm = None
            evst = None
            for h2 in range(2):
                qname = ("QA", "QB")[h2]
                q = QQ[qname]
                for cj in range(h2 * 4, h2 * 4 + 4):
                    off = (cj % 4) * W
                    for dj in range(8):
                        ev_mm = S.op("tensor", (lambda cj=cj, dj=dj, q=q, off=off:
                            (lambda t: t.matmul(
                                q[:, off:off + W],
                                wbig[:, 3 * 8192 + dj * 1024 + cj * 128:
                                     3 * 8192 + dj * 1024 + (cj + 1) * 128],
                                abuf[:, dj * W:(dj + 1) * W],
                                start=(dj == 0), stop=(dj == 7))))(),
                            waits=flat(ev_attn, last_evac.get(qname),
                                       ev_cast[(3, dj)] if ti == NWARM else None),
                            inc="pe" if dj == 7 else None)
                eve = S.op("scalar", (lambda h2=h2, q=q: (lambda s: s.activation(
                    sbf[:, h2 * 1024:(h2 + 1) * 1024], q[:, :], AF.Copy)))(),
                    waits=flat(ev_mm, ev_s_stored,
                               ev_bexpc if ti == NWARM else None), inc="act")
                last_evac[qname] = eve
                evst = S.op("scalar", (lambda mi=mi, h2=h2: (lambda e: e.dma_start(
                    scr_s[h2 * 512:(h2 + 1) * 512, mi * W:(mi + 1) * W]
                    .rearrange("(c p) w -> p c w", p=128),
                    sbf[:, h2 * 1024:(h2 + 1) * 1024]
                    .rearrange("p (c w) -> p c w", w=W))))(),
                    waits=[eve], inc="st_s", k=16)
            ev_abuf_rd = ev_mm
            ev_wo_last = ev_mm
            ev_s_stored = evst

        emit_A_stats(0)
        emit_A_rest(0)
        emit_wcast(0)
        emit_reps()
        emit_A_stats(1)
        emit_wcast(1)
        emit_A_rest(1)
        prev_b2 = None
        for ti in range(ntiles):
            if ti + 2 < ntiles:
                emit_A_stats(ti + 2)
            emit_B1(ti)
            if ti == 0:
                emit_wcast(2)
            if ti == 1:
                emit_wcast(3)
                ev_prologue_cast_end = max(
                    (ev_cast[(m, ci)] for m in range(4) for ci in range(8)),
                    key=lambda e: e[1])
            if (ti + 1) in rt_pend:
                emit_masks(ti + 1)
            if prev_b2 is not None:
                emit_B2b(prev_b2)
                prev_b2 = None
            if ti + 2 < ntiles:
                emit_A_rest(ti + 2)
            if ti >= NWARM:
                emit_B2a(ti)
                prev_b2 = ti
        if (ntiles - 1) in rt_pend:
            emit_masks(ntiles - 1)
        if prev_b2 is not None:
            emit_B2b(prev_b2)

        barrier = ev_max(ev_s_stored, ev_xn_stored)
        ev_sload_free = {}
        ev_xnld_free = [None, None]
        ev_sg_prev = {}
        ev_out_prev = {}
        ev_slot_init = ev_xb_free
        ev_mm = None
        bias_bank_read = [last_evac.get("QS"),
                          ev_max(last_evac.get("QS"), rtt_read[0], rtt_read[1],
                                 rtt_read[2], rtt_read[3])]
        tp_bank_read = [last_evac.get("QT"), last_evac.get("QT")]
        maskT_war = [None, None]
        bias_sb_free = [None, None, None, None]
        bias_tile = {}
        babuf_free = {}
        ev_silu_prev = {}
        EXPQ = [("QA", 0), ("QA", 512), ("QB", 0), ("QB", 512)]
        last_exp_bank_read = {("QA", 0): last_evac.get("QA"),
                              ("QA", 512): last_evac.get("QA"),
                              ("QB", 0): last_evac.get("QB"),
                              ("QB", 512): last_evac.get("QB")}
        ev_exp_free = {}

        def emit_bias_prologue(half):
            nonlocal bias_sb_free
            for j in range(nmain):
                for sub in range(NSUB):
                    msl = (j * NSUB + sub) * 8
                    bi2 = (j * 2 + sub) % 2
                    slot = (j * 2 + sub) % 4
                    ev_mt = S.op("tensor", (lambda msl=msl, bi2=bi2: (lambda t: t.transpose(
                        QT[0:8, bi2 * 128:(bi2 + 1) * 128], masks[:, msl:msl + 8],
                        eye[:, :])))(),
                        waits=flat(tp_bank_read[bi2], ev_masks[(j, sub)]), inc="pe")
                    ev_mte = S.op("scalar", (lambda bi2=bi2: (lambda s: s.activation(
                        maskTb[bi2][:, :], QT[0:8, bi2 * 128:(bi2 + 1) * 128],
                        AF.Copy)))(),
                        waits=flat(ev_mt, maskT_war[bi2]), inc="act")
                    tp_bank_read[bi2] = ev_mte
                    ev_bm = S.op("tensor", (lambda half=half, bi2=bi2: (lambda t: t.matmul(
                        QS[:, bi2 * 512:(bi2 + 1) * 512], maskTb[bi2][0:8, :],
                        sbexp_bf[0:8, half * 512:(half + 1) * 512],
                        start=True, stop=True)))(),
                        waits=flat(ev_mte, bias_bank_read[bi2], ev_bexpc), inc="pe")
                    maskT_war[bi2] = ev_bm
                    ev_be = S.op("scalar", (lambda slot=slot, bi2=bi2: (lambda s: s.activation(
                        sbf[:, slot * 512:(slot + 1) * 512],
                        QS[:, bi2 * 512:(bi2 + 1) * 512], AF.Copy)))(),
                        waits=flat(ev_bm, bias_sb_free[slot],
                                   ev_s_stored if (half == 0 and j == 0) else None),
                        inc="act")
                    bias_bank_read[bi2] = ev_be
                    bias_sb_free[slot] = S.op("scalar", (lambda j=j, sub=sub, half=half,
                                                         slot=slot: (lambda q: q.dma_start(
                        scr_b[j * W + sub * 128: j * W + (sub + 1) * 128,
                              half * 512:(half + 1) * 512],
                        sbf[:, slot * 512:(slot + 1) * 512])))(),
                        waits=[ev_be], inc=f"sb{slot}", k=16)
                    bias_tile[(half, j, sub)] = bias_sb_free[slot]

        _stage_bufs = [b.bitcast(BF16) for b in (sa, sbb, xnfm2[0], xnfm2[1])]
        _dxn_bf = [dxn2[0], dxn2[1]]
        NSTAGE = 5

        def stage_ap(e, ci):
            if e < 4:
                return _stage_bufs[e][:, ci * 512:(ci + 1) * 512]
            return _dxn_bf[ci // 4][:, (ci % 4) * 512:(ci % 4 + 1) * 512]
        ev_stage = {}

        for half in range(2):
            ev_ecast = {}
            ev_slotf = [ev_slot_init] * 4
            elist = range(8) if half == 0 else range(NSTAGE, 8)
            for e in elist:
                for ci in range(8):
                    si = (e * 8 + ci) % 4
                    evl = S.op("sync", (lambda e_=e, ci=ci, si=si, half=half: (lambda q: q.dma_start(
                        xb[:, si * 512:(si + 1) * 512],
                        expT[e_ * C + ci * 128: e_ * C + (ci + 1) * 128,
                             half * 512:(half + 1) * 512])))(),
                        waits=flat(ev_slotf[si]), inc=f"sl{si}", k=16)
                    if half == 0:
                        reg = ((e * 8 + ci) * 512) // 8192
                        regev = (proj_mm[ntiles - 1]["k"], proj_mm[ntiles - 1]["v"],
                                 proj_mm[ntiles - 1]["r"], ev_wo_last)[reg]
                    else:
                        regev = ev_exp_free.get(e)
                    evc = S.op("vector", (lambda e_=e, ci=ci, si=si: (lambda v: v.tensor_copy(
                        wbig[:, (e_ * 8 + ci) * 512:(e_ * 8 + ci + 1) * 512],
                        xb[:, si * 512:(si + 1) * 512])))(),
                        waits=flat(evl, regev), inc="dve")
                    ev_ecast[(e, ci)] = evc
                    ev_slotf[si] = evc
            if half == 0:
                ev_ecast_e = [max((ev_ecast[(e, ci)] for ci in range(8)),
                               key=lambda x: x[1]) for e in range(8)]
                emit_bias_prologue(0)
                emit_bias_prologue(1)
                for e in range(NSTAGE):
                    for ci in range(8):
                        si = (e * 8 + ci) % 4
                        evl = S.op("sync", (lambda e_=e, ci=ci, si=si: (lambda q: q.dma_start(
                            xb[:, si * 512:(si + 1) * 512],
                            expT[e_ * C + ci * 128: e_ * C + (ci + 1) * 128,
                                 512:1024])))(),
                            waits=flat(ev_slotf[si]), inc=f"sl{si}", k=16)
                        evc = S.op("scalar", (lambda e_=e, ci=ci, si=si: (lambda s: s.activation(
                            stage_ap(e_, ci),
                            xb[:, si * 512:(si + 1) * 512], AF.Copy)))(),
                            waits=[evl], inc="act")
                        ev_stage[(e, ci)] = evc
                        ev_slotf[si] = evc
                ev_slot_init = ev_slotf[3]
            else:
                ev_ecast_e = [max((ev_stage[(e, ci)] for ci in range(8)),
                               key=lambda x: x[1]) for e in range(NSTAGE)]
                ev_ecast_e += [max((ev_ecast[(e, ci)] for ci in range(8)),
                                key=lambda x: x[1]) for e in range(NSTAGE, 8)]
            babuf_free = {0: ev_abuf_rd, 1: ev_abuf_rd} if half == 0 else babuf_free

            def emit_loads(j):
                jp = j % 2
                jo = jp * 1024
                ev_xnld = S.op("pool", (lambda j=j, jp=jp: (lambda q: q.dma_start(
                    xnbf2[jp][:, :], scr_xn[j * 128:(j + 1) * 128, :])))(),
                    waits=flat(barrier if (half == 0 and j <= 1) else None,
                               ev_xnld_free[jp]), inc=f"ld_xn{jp}", k=16)
                ev_sld = []
                for sub in range(NSUB):
                    ev_sld.append(S.op("scalar", (lambda j=j, sub=sub, half=half, jo=jo:
                        (lambda q: q.dma_start(
                            mixbf[0][:, jo + sub * 512: jo + (sub + 1) * 512],
                            scr_s[half * 512:(half + 1) * 512,
                                  j * W + sub * 128: j * W + (sub + 1) * 128],
                            transpose=True)))(),
                        waits=flat(barrier if (half == 0 and j == 0) else None,
                                   ev_sload_free.get(jp)),
                        inc=f"ld_s{jp}", k=16))
                ev_xld = []
                for sub in range(NSUB):
                    ev_xld.append(S.op("pool", (lambda j=j, sub=sub, half=half, jo=jo:
                        (lambda q: q.dma_start(
                            kvb[:, jo + sub * 512: jo + (sub + 1) * 512],
                            xs[NWARM * W + j * W + sub * 128:
                               NWARM * W + j * W + (sub + 1) * 128,
                               half * 512:(half + 1) * 512])))(),
                        waits=flat(barrier if (half == 0 and j == 0) else None,
                                   ev_sload_free.get(jp)),
                        inc=f"ld_xr{jp}", k=16))
                ev_bld = []
                for sub in range(NSUB):
                    ev_bld.append(S.op("pool", (lambda j=j, sub=sub, half=half, jo=jo:
                        (lambda q: q.dma_start(
                            abuf[:, jo + sub * 512: jo + (sub + 1) * 512],
                            scr_b[j * W + sub * 128: j * W + (sub + 1) * 128,
                                  half * 512:(half + 1) * 512])))(),
                        waits=flat(bias_tile[(half, j, sub)],
                                   babuf_free.get(jp)),
                        inc=f"ld_b{jp}", k=16))
                return ev_xnld, ev_sld, ev_xld, ev_bld

            pend_loads = emit_loads(0)
            for j in range(nmain):
                jp = j % 2
                jo = jp * 1024
                ev_xnld, ev_sld, ev_xld, ev_bld = pend_loads
                if j + 1 < nmain:
                    pend_loads = emit_loads(j + 1)

                for sub in range(NSUB):
                    msl = (j * NSUB + sub) * 8
                    ev_acc = None
                    eorder = ([5, 6, 7, 0, 1, 2, 3, 4]
                              if (half == 0 and j == nmain - 1) else list(range(8)))
                    for idx_e, e in enumerate(eorder):
                        qname, qoff = EXPQ[e % 4]
                        q = QQ[qname]
                        if half == 1 and e < NSTAGE:
                            wsrc_e = (lambda e_=e: (lambda ci: stage_ap(e_, ci)))()
                        else:
                            wsrc_e = (lambda e_=e: (lambda ci:
                                wbig[:, (e_ * 8 + ci) * 512:(e_ * 8 + ci + 1) * 512]))()
                        ev_mm = None
                        for ci in range(8):
                            ev_mm = S.op("tensor", (lambda ci=ci, sub=sub, q=q,
                                                    qoff=qoff, jp=jp, wsrc_e=wsrc_e:
                                (lambda t: t.matmul(
                                    q[:, qoff:qoff + 512],
                                    xnbf2[jp][:, ci * W + sub * 128: ci * W + (sub + 1) * 128],
                                    wsrc_e(ci),
                                    start=(ci == 0), stop=(ci == 7))))(),
                                waits=flat(ev_xnld, ev_ecast_e[e] if j == 0 else None,
                                           last_exp_bank_read.get((qname, qoff))),
                                inc="pe" if ci == 7 else None)
                        if half == 0 and j == nmain - 1 and sub == NSUB - 1:
                            ev_exp_free[e] = ev_mm
                        src_in1 = ((lambda jo=jo, sub=sub: abuf[:, jo + sub * 512:
                                                               jo + (sub + 1) * 512])()
                                   if idx_e == 0 else
                                   (lambda jo=jo, sub=sub: kcb[:, jo + sub * 512:
                                                               jo + (sub + 1) * 512])())
                        ev_acc = S.op("vector", (lambda e_=e, sub=sub, q=q, qoff=qoff,
                                                 msl=msl, jo=jo, src_in1=src_in1:
                            (lambda v: v.scalar_tensor_tensor(
                                kcb[:, jo + sub * 512: jo + (sub + 1) * 512],
                                q[:, qoff:qoff + 512],
                                masks[:, msl + e_: msl + e_ + 1], src_in1,
                                ALU.mult, ALU.add)))(),
                            waits=flat(ev_mm, ev_bld[-1] if idx_e == 0 else None,
                                       ev_silu_prev.get((jp, sub)) if idx_e == 0 else None,
                                       ev_acc), inc="dve")
                        last_exp_bank_read[(qname, qoff)] = ev_acc
                        if idx_e == 7:
                            babuf_free[jp] = ev_acc
                    if debug:
                        S.op("scalar", (lambda j=j, sub=sub, half=half, jo=jo:
                            (lambda q: q.dma_start(
                                scr_g[j * W + sub * 128: j * W + (sub + 1) * 128,
                                      half * 512:(half + 1) * 512],
                                kcb[:, jo + sub * 512: jo + (sub + 1) * 512])))(),
                            waits=[ev_acc], inc="st_g", k=16)
                    ev_silu = S.op("scalar", (lambda sub=sub, jo=jo: (lambda s: s.activation(
                        t1b[:, jo + sub * 512: jo + (sub + 1) * 512],
                        kcb[:, jo + sub * 512: jo + (sub + 1) * 512],
                        AF.Silu)))(), waits=flat(ev_acc, ev_sg_prev.get((jp, sub))), inc="act")
                    ev_silu_prev[(jp, sub)] = ev_silu
                    ev_sg = S.op("pool", (lambda sub=sub, jo=jo: (lambda g: g.tensor_tensor(
                        ub[:, jo + sub * 512: jo + (sub + 1) * 512],
                        t1b[:, jo + sub * 512: jo + (sub + 1) * 512],
                        mixbf[0][:, jo + sub * 512: jo + (sub + 1) * 512], ALU.mult)))(),
                        waits=flat(ev_silu, ev_sld[-1]), inc="pool")
                    ev_sg_prev[(jp, sub)] = ev_sg
                    ev_o = S.op("pool", (lambda sub=sub, jo=jo: (lambda g: g.tensor_tensor(
                        den[:, jo + sub * 512: jo + (sub + 1) * 512],
                        ub[:, jo + sub * 512: jo + (sub + 1) * 512],
                        kvb[:, jo + sub * 512: jo + (sub + 1) * 512], ALU.add)))(),
                        waits=flat(ev_xld[-1], ev_out_prev.get(jp)), inc="pool")
                    ev_st = S.op("scalar", (lambda j=j, sub=sub, half=half, jo=jo:
                        (lambda q: q.dma_start(
                            out[j * W + sub * 128: j * W + (sub + 1) * 128,
                                half * 512:(half + 1) * 512],
                            den[:, jo + sub * 512: jo + (sub + 1) * 512])))(),
                        waits=[ev_o], inc=f"st_o{jp}", k=16)
                    ev_out_prev[jp] = ev_st
                ev_sload_free[jp] = ev_o
                ev_xnld_free[jp] = ev_mm

        if debug:
            S.op("sync", lambda q: q.dma_start(scr_mask, masks[:, :]),
                 waits=flat(*[ev_masks[k2] for k2 in ev_masks]), inc="st_g", k=16)
        S.op("sync", lambda q: q.nop(),
             waits=[(n, S.cnt[n]) for n in
                    ("st_s", "st_xn0", "st_xn1", "st_o0", "st_o1", "st_g")
                    if S.cnt.get(n)])

        with nc.Block() as block:
            @block.sync
            def _(e):
                S.replay("sync", e, sems)

            @block.vector
            def _(e):
                S.replay("vector", e, sems)

            @block.scalar
            def _(e):
                S.replay("scalar", e, sems)

            @block.tensor
            def _(e):
                S.replay("tensor", e, sems)

            @block.gpsimd
            def _(e):
                S.replay("pool", e, sems)

    return nc


_CACHE = {}


def _get_program(nmain):
    if nmain not in _CACHE:
        _CACHE[nmain] = build_program(nmain)
    return _CACHE[nmain]


def kernel(L0_attn_xq, L0_attn_xk, L0_attn_xv, L0_attn_time_decay, L0_attn_time_first,
           L0_attn_time_mix_k, L0_attn_time_mix_v, L0_attn_time_mix_r,
           L0_attn_Wk, L0_attn_Wv, L0_attn_Wr, L0_attn_Wo,
           L0_ffn_w_router, L0_ffn_b_router, L0_ffn_w_experts, L0_ffn_b_experts):
    x = np.ascontiguousarray(np.asarray(L0_attn_xq, dtype=np.float32))
    B, T, Cc = x.shape
    assert (B, T, Cc) == (4, 8192, 1024)
    nmain = 16
    TOK = W * nmain
    NROW = W * (NWARM + nmain)

    def chunked(a):
        a = np.asarray(a, np.float32).reshape(-1)
        return np.ascontiguousarray(a.reshape(8, 128).T)

    common = {
        "wkT": np.ascontiguousarray(np.asarray(L0_attn_Wk, np.float32).T),
        "wvT": np.ascontiguousarray(np.asarray(L0_attn_Wv, np.float32).T),
        "wrT": np.ascontiguousarray(np.asarray(L0_attn_Wr, np.float32).T),
        "woT": np.ascontiguousarray(np.asarray(L0_attn_Wo, np.float32).T),
        "expT": np.ascontiguousarray(
            np.asarray(L0_ffn_w_experts, np.float32).transpose(0, 2, 1).reshape(8 * C, C)),
        "tdp": chunked(L0_attn_time_decay),
        "tfp": chunked(L0_attn_time_first),
        "tmixp": np.ascontiguousarray(np.concatenate(
            [chunked(L0_attn_time_mix_k), chunked(L0_attn_time_mix_v),
             chunked(L0_attn_time_mix_r)], axis=1)),
        "wroutT": np.ascontiguousarray(np.asarray(L0_ffn_w_router, np.float32).T),
        "brout": np.ascontiguousarray(np.broadcast_to(
            np.asarray(L0_ffn_b_router, np.float32).reshape(1, 8), (128, 8))),
        "bexp": np.ascontiguousarray(np.asarray(L0_ffn_b_experts, np.float32)),
        "eye_in": np.eye(128, dtype=np.float32),
    }

    in_maps = []
    for core in range(8):
        b, h = core // 2, core % 2
        t0 = h * TOK
        xsv = np.zeros((NROW, C), np.float32)
        lo = max(0, t0 - NWARM * W)
        xsv[NWARM * W - (t0 - lo):] = x[b, lo:t0 + TOK]
        m = dict(common)
        m["xs"] = xsv
        m["wmask_in"] = np.full((128, 1), 1.0 if h == 1 else 0.0, np.float32)
        in_maps.append(m)

    nc = _get_program(nmain)
    res = run_bass_kernel_spmd(nc, in_maps, core_ids=list(range(8)))
    outf = np.empty((B, T, C), np.float32)
    for core in range(8):
        b, h = core // 2, core % 2
        outf[b, h * TOK:(h + 1) * TOK] = np.asarray(res.results[core]["out"])
    return outf
